# version 29
# baseline (speedup 1.0000x reference)
import numpy as np
import ml_dtypes
import concourse.bass as bass
import concourse.mybir as mybir
from concourse.bass_utils import run_bass_kernel_spmd

F32 = mybir.dt.float32
BF16 = mybir.dt.bfloat16
I32 = mybir.dt.int32
U8 = mybir.dt.uint8
AF = mybir.ActivationFunctionType
ALU = mybir.AluOpType
AX = mybir.AxisListType

T = 2048
C = 1024
NH = 16
HD = 64
FF = 4096
NE = 8
LR = 4
NT = 16
CCn = 8
NEG = -1.0e30
BIGI = 1000.0

ENGS = ("sync", "tensor", "vector", "scalar", "gpsimd")
DEBUG = False
PREFETCH_QKV = True


class Sched:

    def __init__(self, nc):
        self.nc = nc
        self.sems = {}
        self.cnt = {}
        self.seen = {}
        self.ops = {e: [] for e in ENGS}

    def _sem(self, name):
        if name not in self.sems:
            self.sems[name] = self.nc.alloc_semaphore(name="sem_" + name)
            self.cnt[name] = 0
        return self.sems[name]

    def op(self, eng, build, deps=(), mark=None, inc=1):
        waits = []
        for d in deps:
            if d is None:
                continue
            s, v = d
            if self.seen.get((eng, s), 0) < v:
                waits.append((s, v))
                self.seen[(eng, s)] = v
        tok = None
        if mark is not None:
            self._sem(mark)
            self.cnt[mark] += inc
            tok = (mark, self.cnt[mark])
        self.ops[eng].append((waits, build, mark, inc))
        return tok

    def barrier_wait(self, eng, deps):
        best = {}
        for d in deps:
            if d is None:
                continue
            s, v = d
            best[s] = max(best.get(s, 0), v)
        self.op(eng, None, deps=list(best.items()), mark=None)

    def replay(self, eng_name, handle):
        for waits, build, mark, inc in self.ops[eng_name]:
            for s, v in waits:
                handle.wait_ge(self.sems[s], v)
            if build is not None:
                ins = build(handle)
                if mark is not None:
                    ins.then_inc(self.sems[mark], inc)


def build_program():
    nc = bass.Bass("TRN2", target_bir_lowering=False, debug=False)

    d_x = nc.dram_tensor("x_nat", [T, C], F32, kind="ExternalInput").ap()
    d_watT = nc.dram_tensor("watT", [C, 3 * C], F32, kind="ExternalInput").ap()
    d_wprT = nc.dram_tensor("wprT", [C, C], F32, kind="ExternalInput").ap()
    d_gwT = nc.dram_tensor("gwT", [C, NE], F32, kind="ExternalInput").ap()
    d_gb = nc.dram_tensor("gb", [128, NE], F32, kind="ExternalInput").ap()
    d_wiT = nc.dram_tensor("wiT", [C, FF], F32, kind="ExternalInput").ap()
    d_wib = nc.dram_tensor("wib", [128, 32], F32, kind="ExternalInput").ap()
    d_woT = nc.dram_tensor("woT", [FF, C], F32, kind="ExternalInput").ap()
    d_wob = nc.dram_tensor("wob", [128, C], F32, kind="ExternalInput").ap()
    d_AT = nc.dram_tensor("AT", [C, 32], F32, kind="ExternalInput").ap()
    d_BT = nc.dram_tensor("BT", [32, FF], F32, kind="ExternalInput").ap()
    d_ident = nc.dram_tensor("ident", [128, 128], BF16, kind="ExternalInput").ap()
    d_identf = nc.dram_tensor("identf", [128, 128], F32, kind="ExternalInput").ap()
    d_tril = nc.dram_tensor("tril", [128, 128], BF16, kind="ExternalInput").ap()
    d_iotapb = nc.dram_tensor("iotapb", [128, NE], F32, kind="ExternalInput").ap()
    d_out = nc.dram_tensor("out", [T, C], F32, kind="ExternalOutput").ap()
    if DEBUG:
        d_dbg_xT = nc.dram_tensor("dbg_xT", [128, 16384], BF16, kind="ExternalOutput").ap()
        d_dbg_va = nc.dram_tensor("dbg_va", [128, 16640], BF16, kind="ExternalOutput").ap()
        d_dbg_qk = nc.dram_tensor("dbg_qk", [128, 4096], BF16, kind="ExternalOutput").ap()
        d_dbg_qk7 = nc.dram_tensor("dbg_qk7", [128, 4096], BF16, kind="ExternalOutput").ap()
        d_dbg_yT = nc.dram_tensor("dbg_yT", [128, 16384], BF16, kind="ExternalOutput").ap()
        d_dbg_res = nc.dram_tensor("dbg_res", [128, 16384], F32, kind="ExternalOutput").ap()

    AR = 212160
    arena = nc.alloc_sbuf_tensor("arena", [128, AR], U8).ap()

    def vw(off, nelem, dt):
        nb = {F32: 4, BF16: 2, I32: 4}[dt] * nelem
        return arena[:, off:off + nb].bitcast(dt)

    res = vw(0, 16384, F32)
    slotA = vw(65536, 16384, BF16)
    ident = vw(98304, 128, BF16)
    tril = vw(98560, 128, BF16)
    ones128 = vw(99072, 128, F32)
    iotapb = vw(99584, NE, F32)
    gbuf = vw(99616, NE, F32)
    wib = vw(99648, 32, F32)
    gwT = vw(99776, 64, BF16)
    ATs = vw(99904, 256, BF16)
    ypairs = vw(100416, 512, BF16)
    colv = vw(101440, 64, F32)
    rm_bf = vw(101696, 32, BF16)
    gscr = vw(101760, 24, F32)
    keep8 = vw(101856, NE, F32)
    scr1024 = vw(101888, 1024, BF16)
    wob = vw(104448, 1024, F32)
    V_aug = vw(108544, 16640, BF16)
    qkb = vw(141824, 8192, BF16)
    ptb = vw(158208, 2048, BF16)
    pairW = vw(162304, 4096, BF16)
    wstg = vw(170496, 1024, F32)
    qscr = vw(174592, 1024, F32)
    yT = vw(178688, 16384, BF16)
    wdb0 = vw(108544, 16384, BF16)
    wdb1 = vw(141312, 16384, BF16)
    rmT = vw(174592, 2048, BF16)[0:32, :]
    it_sb = vw(178688, 768, BF16)
    BTs = vw(180224, 4096, BF16)[0:32, :]
    identf = vw(211456, 128, F32)
    rm_bf2 = vw(211968, 32, BF16)
    gscr2 = vw(212032, 24, F32)
    keep8b = vw(212128, NE, F32)
    LNC = vw(104192, 64, F32)

    def col(k):
        return colv[:, k:k + 1]

    ACC_A, TCOL, ACC_P, SCL_A, RCP_A, ALPH, SCL_P, RCP_P, S2C = range(9)
    EPSC, NHLF = 9, 10
    LSUM, LM, LSQ, LT1, LVAR, LRSTD, LMR = 11, 12, 13, 14, 15, 16, 17
    GM, GIDX = 18, 19
    GM2, GIDX2 = 41, 42
    RC0 = 20
    THA = 28
    THP = 34

    p_a = nc.alloc_psum_tensor("p_a", [128, 1024], F32).ap()
    p_b = nc.alloc_psum_tensor("p_b", [128, 1024], F32).ap()
    p_c = nc.alloc_psum_tensor("p_c", [128, 1024], F32).ap()
    p_d = nc.alloc_psum_tensor("p_d", [128, 1024], F32).ap()
    sbanks = [p_a, p_b]
    ybanks = [p_c, p_d]

    S = Sched(nc)

    S.op("sync", lambda e: e.dma_start(ident, d_ident), mark="qC", inc=16)
    S.op("sync", lambda e: e.dma_start(identf, d_identf), mark="qC", inc=16)
    S.op("sync", lambda e: e.dma_start(tril, d_tril), mark="qC", inc=16)
    S.op("sync", lambda e: e.dma_start(iotapb, d_iotapb), mark="qC", inc=16)
    S.op("sync", lambda e: e.dma_start(gbuf, d_gb), mark="qC", inc=16)
    S.op("sync", lambda e: e.dma_start(wib, d_wib), mark="qC", inc=16)
    t_const_all = S.op("sync", lambda e: e.dma_start(wob, d_wob), mark="qC", inc=16)
    t_ident = t_tril = t_iot = t_gb = t_wib = t_wob = t_idf = t_const_all
    S.op("gpsimd", lambda e: e.dma_start(
        gwT.rearrange("p (c n) -> p c n", c=CCn),
        d_gwT.rearrange("(c p) n -> p c n", p=128)), mark="qG2", inc=16)
    t_ATs = S.op("gpsimd", lambda e: e.dma_start(
        ATs.rearrange("p (c n) -> p c n", c=CCn),
        d_AT.rearrange("(c p) n -> p c n", p=128)), mark="qG2", inc=16)
    t_gwT = t_ATs

    t_dbg_xT = None
    if DEBUG:
        t_dbg_xT = S.op("sync", lambda e: e.dma_start(d_dbg_xT, slotA),
                        deps=[t_xT], mark="qD1", inc=16)

    t_ones = S.op("vector", lambda e: e.memset(ones128, 1.0), mark="V")
    t_eps = S.op("vector", lambda e: e.memset(col(EPSC), 1e-5), mark="V")
    t_nh = S.op("vector", lambda e: e.memset(col(NHLF), -0.5), mark="V")
    t_acca = S.op("vector", lambda e: e.memset(col(ACC_A), 0.0), mark="V")
    t_accp = S.op("vector", lambda e: e.memset(col(ACC_P), 0.0), mark="V")
    watT_r = d_watT.rearrange("(c p) o -> p c o", p=128)
    wprT_r = d_wprT.rearrange("(c p) o -> p c o", p=128)

    vstg = vw(108544, 6144, F32)
    p1_tok = [None, None]
    last_ta = [None]

    def pass1(src_r, width, lnc_base, tok):
        for cc in range(CCn):
            h = cc % 2
            stg = vstg[:, h * 3072: h * 3072 + width]
            tdma = S.op("sync", lambda e, stg=stg, s=src_r[:, cc, :]: e.dma_start(stg, s),
                        deps=[p1_tok[h]], mark="qP%d" % h, inc=16)
            ta = S.op("scalar", lambda e, stg=stg, cc=cc, lnc_base=lnc_base: e.activation(
                stg, stg, AF.Abs, accum_out=LNC[:, lnc_base + cc: lnc_base + cc + 1]),
                deps=[tdma], mark="S")
            p1_tok[h] = ta
            tok[0] = ta

    x_chunk_tok = []
    for k in range(4):
        t = S.op("sync", lambda e, k=k: e.dma_start(
            res.rearrange("p (t c) -> p t c", t=NT)[:, 4 * k:4 * (k + 1), :],
            d_x.rearrange("(t p) c -> p t c", p=128)[:, 4 * k:4 * (k + 1), :]),
            mark="qX%d" % k, inc=16)
        x_chunk_tok.append(t)
    t_xnat = x_chunk_tok[3]

    t_sum_a = [t_acca]
    t_sum_p = [t_accp]
    pass1(watT_r, 3 * C, 0, t_sum_a)
    pass1(wprT_r, C, 8, t_sum_p)
    ta_fin = t_sum_a[0]
    tp_fin = t_sum_p[0]
    wstg_free = [None, None, None]
    t_vone = S.op("vector", lambda e: e.memset(
        V_aug.rearrange("p (t h n) -> p (t h) n", t=NT, h=NH)[:, :, 64:65], 1.0),
        deps=[p1_tok[0], p1_tok[1]], mark="V")

    t_mm_a = S.op("tensor", lambda e: e.matmul(p_d[:, 0:8], ones128, LNC[:, 0:8],
                                               start=True, stop=True),
                  deps=[t_ones, ta_fin], mark="T")
    t_mm_p = S.op("tensor", lambda e: e.matmul(p_d[:, 512:520], ones128, LNC[:, 8:16],
                                               start=True, stop=True),
                  deps=[t_ones, tp_fin], mark="T")
    NA = float(3 * C * C)
    NP = float(C * C)
    t_ra = S.op("vector", lambda e: e.reduce_sum(
        col(ACC_A), p_d[:, 0:8], AX.X), deps=[t_mm_a, t_acca], mark="V")
    t_rp = S.op("vector", lambda e: e.reduce_sum(
        col(ACC_P), p_d[:, 512:520], AX.X), deps=[t_mm_p, t_accp], mark="V")
    t_scla = S.op("vector", lambda e: e.tensor_scalar(
        col(SCL_A), col(ACC_A), 1.0 / NA, 1e-8, ALU.mult, ALU.max),
        deps=[t_ra], mark="V")
    t_rcpa = S.op("vector", lambda e: e.reciprocal(col(RCP_A), col(SCL_A)),
                  deps=[t_scla], mark="V")
    t_al1 = S.op("vector", lambda e: e.tensor_mul(col(ALPH), col(SCL_A), col(SCL_A)),
                 deps=[t_scla], mark="V")
    t_alph = S.op("vector", lambda e: e.tensor_scalar_mul(col(ALPH), col(ALPH), 0.125),
                  deps=[t_al1], mark="V")
    t_sclp = S.op("vector", lambda e: e.tensor_scalar(
        col(SCL_P), col(ACC_P), 1.0 / NP, 1e-8, ALU.mult, ALU.max),
        deps=[t_rp], mark="V")
    t_rcpp = S.op("vector", lambda e: e.reciprocal(col(RCP_P), col(SCL_P)),
                  deps=[t_sclp], mark="V")
    t_s2 = S.op("vector", lambda e: e.tensor_mul(col(S2C), col(SCL_A), col(SCL_P)),
                deps=[t_sclp, t_scla], mark="V")
    t_pd_free = S.op("vector", lambda e: e.engine_nop(), deps=[t_scla, t_sclp], mark="V")

    t_tha = t_thp = None
    for i, f in enumerate((0.5, -0.5, 1.5, -1.5)):
        t_tha = S.op("vector", lambda e, i=i, f=f: e.tensor_scalar_mul(
            col(THA + i), col(SCL_A), f), deps=[t_scla], mark="V")
        t_thp = S.op("vector", lambda e, i=i, f=f: e.tensor_scalar_mul(
            col(THP + i), col(SCL_P), f), deps=[t_sclp], mark="V")
    t_ths = S.op("vector", lambda e: e.engine_nop(), deps=[t_tha, t_thp], mark="V")

    q_chain = [None]
    wstg_bufs = [wstg[:, 0:512], wstg[:, 512:1024], qscr[:, 512:1024]]
    wstg_rr = [0]
    qv_steps = []

    def make_quant_steps(src_ap, dst_ap, th, dst_war=()):
        cell = {}
        qa = qscr[:, 0:512]

        def s_dma():
            wh = wstg_rr[0] % 3
            wstg_rr[0] += 1
            stg = wstg_bufs[wh]
            sview = stg if src_ap.ndim == 2 else stg.rearrange(
                "p (c o) -> p c o", c=src_ap.shape[1])
            cell["stg"] = stg
            cell["wh"] = wh
            cell["td"] = S.op("sync", lambda e, sview=sview: e.dma_start(sview, src_ap),
                              deps=[wstg_free[wh]], mark="qW%d" % wh, inc=16)

        def s1():
            cell["t"] = S.op("vector", lambda e: e.tensor_scalar(
                qa, cell["stg"], col(th + 0), 2.0, ALU.is_ge, ALU.subtract),
                deps=[cell["td"], t_ths, q_chain[0]], mark="V")

        def s2():
            cell["t"] = S.op("vector", lambda e: e.scalar_tensor_tensor(
                dst_ap, cell["stg"], col(th + 1), qa, ALU.is_gt, ALU.add),
                deps=[cell["t"]] + list(dst_war), mark="V")

        def s3():
            cell["t"] = S.op("vector", lambda e: e.scalar_tensor_tensor(
                qa, cell["stg"], col(th + 2), dst_ap, ALU.is_ge, ALU.add),
                deps=[cell["t"]], mark="V")

        def s4():
            t4 = S.op("vector", lambda e: e.scalar_tensor_tensor(
                dst_ap, cell["stg"], col(th + 3), qa, ALU.is_gt, ALU.add),
                deps=[cell["t"]], mark="V")
            wstg_free[cell["wh"]] = t4
            q_chain[0] = t4
            cell["t4"] = t4
        return cell, [s_dma, s1, s2, s3, s4]

    def quant(src_ap, dst_ap, th, dst_war=()):
        assert not qv_steps
        cell, steps = make_quant_steps(src_ap, dst_ap, th, dst_war)
        for s in steps:
            s()
        return cell["t4"]

    def push_quant_chunks(specs, sink):
        built = []
        for (src, dst, th, war, key) in specs:
            cell, steps = make_quant_steps(src, dst, th, war)

            def fin(cell=cell, key=key):
                sink[key] = cell["t4"]
            built.append(steps + [fin])
        out = [built[0][0]]
        if len(built) > 1:
            out.append(built[1][0])
        for c in range(len(built)):
            out.extend(built[c][1:])
            if c + 2 < len(built):
                out.append(built[c + 2][0])
        qv_steps.extend(out)

    def pump_quant(n):
        k = 0
        while qv_steps and k < n:
            qv_steps.pop(0)()
            k += 1

    Vv = V_aug.rearrange("p (t h n) -> p t h n", t=NT, h=NH)
    vbank_free = [t_pd_free, t_pd_free, t_pd_free, t_pd_free]
    vbank_ap = [p_a[:, 0:512], p_a[:, 512:1024], p_b[:, 0:512], p_b[:, 512:1024]]
    xTv = slotA.rearrange("p (c t) -> p c t", c=CCn)

    xTile_tok = [None] * NT
    xps = [p_d, p_c]
    xps_free = [[t_ra, t_rp], []]
    for tt in range(NT):
        pd = xps[tt % 2]
        deps0 = [x_chunk_tok[tt // 4], t_idf] + xps_free[tt % 2]
        tr_last = None
        for cc in range(CCn):
            tr_last = S.op("tensor", lambda e, cc=cc, tt=tt, pd=pd: e.matmul(
                pd[:, cc * 128:(cc + 1) * 128],
                res[:, tt * 1024 + cc * 128: tt * 1024 + (cc + 1) * 128], identf,
                start=True, stop=True),
                deps=deps0 if cc == 0 else [], mark="T")
        t_xvu = S.op("scalar", lambda e, tt=tt, pd=pd: e.copy(
            slotA.rearrange("p (c w) -> p c w", c=8)[:, 0:4, tt * 128:(tt + 1) * 128],
            pd.rearrange("p (c w) -> p c w", c=8)[:, 0:4, :]),
            deps=[tr_last], mark="S")
        t_xvl = S.op("scalar", lambda e, tt=tt, pd=pd: e.copy(
            slotA.rearrange("p (c w) -> p c w", c=8)[:, 4:8, tt * 128:(tt + 1) * 128],
            pd.rearrange("p (c w) -> p c w", c=8)[:, 4:8, :]),
            deps=[tr_last], mark="S")
        xTile_tok[tt] = t_xvl
        xps_free[tt % 2] = [t_xvu, t_xvl]
    t_xT = xTile_tok[NT - 1]

    qk_free = [None, None]
    pairW_free = [None, None]
    pc_free = [xps_free[1][0], xps_free[1][1]]
    sb_free = [None, None]
    last_pv_read = {}
    yb_free = [t_pd_free, t_pd_free]
    yT_tok = {}
    pair_attn_done = [None] * 8
    pair_qkv_done = [None] * 8
    qkv_evac_tok = {}
    pair_q_toks = {}
    v_dead = [None]

    def emit_pair_quant(p):
        pb = p % 2
        specs = []
        for half, obase in ((0, p * 128), (1, C + p * 128)):
            for ci in range(2):
                specs.append((watT_r[:, ci * 4:(ci + 1) * 4, obase:obase + 128],
                              pairW[:, pb * 2048 + half * 1024 + ci * 512:
                                    pb * 2048 + half * 1024 + (ci + 1) * 512],
                              THA, [pairW_free[pb]], 2 * half + ci))
        sink = {}
        pair_q_toks[p] = sink
        if p < 2:
            for (sp, dp, th, war, key) in specs:
                sink[key] = quant(sp, dp, th, dst_war=war)
        else:
            push_quant_chunks(specs, sink)

    def emit_pair_qkv(p):
        pb = p % 2
        early = p < 2
        q_toks = [pair_q_toks[p][k] for k in range(4)]
        for half in range(2):
            for st in range(4):
                bk = (half * 4 + st) % 2
                bank = (p_c[:, bk * 512: bk * 512 + 512] if early
                        else sbanks[bk][:, 0:512])
                deps = ([pc_free[bk] if early else sb_free[bk]]
                        + [xTile_tok[4 * st + i] for i in range(4)])
                mmtok = None
                for cc in range(CCn):
                    mmtok = S.op("tensor", lambda e, cc=cc, st=st, half=half, pb=pb, bank=bank: e.matmul(
                        bank,
                        pairW[:, pb * 2048 + half * 1024 + cc * 128:
                              pb * 2048 + half * 1024 + (cc + 1) * 128],
                        xTv[:, cc, st * 512:(st + 1) * 512],
                        start=(cc == 0), stop=(cc == CCn - 1)),
                        deps=(deps if cc == 0 else []) + [q_toks[2 * half + cc // 4]],
                        mark="T" if cc == CCn - 1 else None)
                tev = S.op("vector", lambda e, pb=pb, half=half, st=st, bank=bank: e.tensor_copy(
                    qkb[:, pb * 4096 + half * 2048 + st * 512:
                        pb * 4096 + half * 2048 + (st + 1) * 512],
                    bank),
                    deps=[mmtok, qk_free[pb]], mark="V")
                qkv_evac_tok[(p, half, st)] = tev
                if early:
                    pc_free[bk] = tev
                else:
                    sb_free[bk] = tev
        pair_qkv_done[p] = tev
        pairW_free[pb] = tev

    v_mm_last = None
    for og in range(2):
        vW = yT[:, 8192 + og * 4096: 8192 + (og + 1) * 4096]
        vq_toks = []
        for cc in range(CCn):
            tc = quant(watT_r[:, cc, 2048 + og * 512: 2048 + (og + 1) * 512],
                       vW[:, cc * 512:(cc + 1) * 512], THA)
            vq_toks.append(tc)
        for tt in range(NT):
            bk = (og * NT + tt) % 4
            mmtok = None
            for cc in range(CCn):
                mmtok = S.op("tensor", lambda e, cc=cc, bk=bk, tt=tt, vW=vW: e.matmul(
                    vbank_ap[bk], xTv[:, cc, tt * 128:(tt + 1) * 128],
                    vW[:, cc * 512:(cc + 1) * 512],
                    start=(cc == 0), stop=(cc == CCn - 1)),
                    deps=([vbank_free[bk], xTile_tok[tt]] if cc == 0 else []) + [vq_toks[cc]],
                    mark="T" if cc == CCn - 1 else None)
            tev = S.op("scalar", lambda e, bk=bk, tt=tt, og=og: e.copy(
                Vv[:, tt, og * 8:(og + 1) * 8, 0:64],
                vbank_ap[bk].rearrange("p (h d) -> p h d", h=8)),
                deps=[mmtok, t_vone], mark="S")
            vbank_free[bk] = tev
            v_mm_last = mmtok
    v_dead[0] = v_mm_last
    sb_free[:] = [vbank_free[1], vbank_free[3]]

    emit_pair_quant(0)
    emit_pair_quant(1)
    emit_pair_qkv(0)
    emit_pair_qkv(1)
    yb_free[0] = yb_free[1] = pair_qkv_done[1]

    t_dbg_va = None
    if DEBUG:
        t_dbg_va = S.op("sync", lambda e: e.dma_start(d_dbg_va, V_aug),
                        deps=vbank_free + [v_mm_last, t_vone], mark="qD2", inc=16)

    pend_tr = [None]
    pend_r = [None]
    pend_pv = []

    def drain_tr():
        if pend_tr[0] is not None:
            pend_tr[0]()
            pend_tr[0] = None

    def drain_r():
        if pend_r[0] is not None:
            pend_tr[0] = pend_r[0]()
            pend_r[0] = None

    def flush_pv():
        if not pend_pv:
            return
        kb, ptk, t_exp, jmin, qq, yb, tlpv, p = pend_pv.pop(0)
        t_pv = None
        for h01 in range(2):
            for j in range(jmin, 4):
                qt = 4 * qq + j
                st = (kb == 0 and j == 0)
                sp = (kb == 4 * qq + 3 and j == 3)
                last_for_tile = (kb == qt)
                last_of_flush = (h01 == 1 and j == 3)
                t_pv = S.op("tensor", lambda e, h01=h01, j=j, kb=kb, ptk=ptk, yb=yb, p=p, st=st, sp=sp: e.matmul(
                    yb[:, h01 * 512 + j * 128: h01 * 512 + j * 128 + 65],
                    ptb[:, ptk * 1024 + h01 * 512 + j * 128:
                        ptk * 1024 + h01 * 512 + (j + 1) * 128],
                    Vv[:, kb, 2 * p + h01, :],
                    start=st, stop=sp),
                    deps=[t_exp, t_vone, t_pd_free] if (kb == 0 and j == jmin) else [t_exp],
                    mark="T" if (last_for_tile or last_of_flush) else None)
                if last_for_tile:
                    tlpv[(h01, j)] = t_pv
        last_pv_read[ptk] = t_pv

    def emit_pair_attention(p):
        pb = p % 2
        Qh = qkb[:, pb * 4096: pb * 4096 + 2048]
        Kh = qkb[:, pb * 4096 + 2048: pb * 4096 + 4096]
        qdeps = [qkv_evac_tok[(p, 0, st)] for st in range(4)] + \
                [qkv_evac_tok[(p, 1, st)] for st in range(4)]
        last_yT_evac = []
        t_exp = None

        for qq in range(4):
            if p == 6 and qq == 3:
                prep_kprojT()
            ybk = qq % 2
            yb = ybanks[ybk]
            tile_last_pv = {}

            for kb in range(4 * qq + 4):
                sbk = kb % 2
                j0 = kb - 4 * qq
                off = max(0, j0 * 128)
                diag = j0 >= 0
                sdeps = qdeps + [sb_free[sbk], yb_free[ybk] if kb == 0 else None]
                S.op("tensor", lambda e, sbk=sbk, off=off, kb=kb, qq=qq, diag=diag: e.matmul(
                    sbanks[sbk][:, off:512],
                    Kh[0:64, kb * 128:(kb + 1) * 128],
                    Qh[0:64, qq * 512 + off:(qq + 1) * 512],
                    start=True, stop=not diag, tile_position=(0, 0)),
                    deps=sdeps, mark=None)
                t_sb = S.op("tensor", lambda e, sbk=sbk, off=off, kb=kb, qq=qq, diag=diag: e.matmul(
                    sbanks[sbk][:, 512 + off:1024],
                    Kh[64:128, kb * 128:(kb + 1) * 128],
                    Qh[64:128, qq * 512 + off:(qq + 1) * 512],
                    start=True, stop=not diag, tile_position=(64, 0)),
                    deps=[], mark=None if diag else "T")
                if diag:
                    S.op("tensor", lambda e, sbk=sbk, off=off: e.matmul(
                        sbanks[sbk][:, off:off + 128], ident, tril,
                        start=False, stop=True),
                        deps=[t_ident, t_tril], mark=None)
                    t_sb = S.op("tensor", lambda e, sbk=sbk, off=off: e.matmul(
                        sbanks[sbk][:, 512 + off:512 + off + 128], ident, tril,
                        start=False, stop=True),
                        deps=[], mark="T")
                if kb == 2:
                    drain_r()
                if kb == 3:
                    drain_tr()
                if len(pend_pv) >= 2:
                    flush_pv()
                ptk = kb % 2
                t_exp = S.op("scalar", lambda e, sbk=sbk, off=off, ptk=ptk: e.activation(
                    ptb.rearrange("p (b h w) -> p (b h) w", b=2, h=2)[:, 2 * ptk:2 * ptk + 2, off:512],
                    sbanks[sbk].rearrange("p (h w) -> p h w", h=2)[:, :, off:512],
                    AF.Exp, scale=col(ALPH)),
                    deps=[t_sb, t_alph, last_pv_read.get(ptk)], mark="S")
                sb_free[sbk] = t_exp
                pend_pv.append((kb, ptk, t_exp, max(0, j0), qq, yb, tile_last_pv, p))
                pump_quant(2)

            def stage_r(yb=yb, qq=qq, ybk=ybk, tile_last_pv=tile_last_pv):
                ev_toks = []
                rc_t = {}
                for h01 in range(2):
                    rc_t[h01] = S.op("vector", lambda e, yb=yb, h01=h01: e.reciprocal(
                        colv[:, RC0 + h01 * 4: RC0 + h01 * 4 + 4].unsqueeze(2),
                        yb[:, h01 * 512: h01 * 512 + 512].rearrange(
                            "p (j w) -> p j w", j=4)[:, :, 64:65]),
                        deps=[tile_last_pv[(h01, 3)]], mark="V")
                for j in range(4):
                    t_na = S.op("vector", lambda e, yb=yb, j=j: e.tensor_scalar(
                        ypairs[:, j * 128: j * 128 + 64],
                        yb[:, j * 128: j * 128 + 64],
                        col(RC0 + j), None, ALU.mult),
                        deps=[rc_t[0]], mark="V")
                    t_nb = S.op("vector", lambda e, yb=yb, j=j: e.tensor_scalar(
                        ypairs[:, j * 128 + 64: j * 128 + 128],
                        yb[:, 512 + j * 128: 512 + j * 128 + 64],
                        col(RC0 + 4 + j), None, ALU.mult),
                        deps=[rc_t[1]], mark="V")
                    ev_toks.append(t_na)
                    ev_toks.append(t_nb)

                def do_tr(yb=yb, qq=qq, ev_toks=ev_toks, ybk=ybk):
                    tr_last = None
                    for j in range(4):
                        tr_last = S.op("tensor", lambda e, yb=yb, j=j: e.matmul(
                            yb[:, j * 128:(j + 1) * 128],
                            ypairs[:, j * 128:(j + 1) * 128], ident,
                            start=True, stop=True),
                            deps=ev_toks + [t_ident], mark="T")
                    t_ye = S.op("vector", lambda e, yb=yb, p=p, qq=qq: e.tensor_copy(
                        yT[:, p * 2048 + qq * 512: p * 2048 + (qq + 1) * 512],
                        yb[:, 0:512]),
                        deps=[tr_last, v_dead[0]], mark="V")
                    for j in range(4):
                        yT_tok[(p, 4 * qq + j)] = t_ye
                    last_yT_evac[:] = [t_ye]
                    yb_free[ybk] = S.op("vector", lambda e: e.engine_nop(),
                                        deps=[t_ye] + ev_toks, mark="V")
                return do_tr

            if qq == 0 and p + 2 < 8:
                emit_pair_quant(p + 2)
            if qq == 3 and p == 7:
                while qv_steps:
                    qv_steps.pop(0)()
                while pend_pv:
                    flush_pv()
                drain_r()
                drain_tr()
                do_tr_f = stage_r()
                do_tr_f()
            else:
                pend_r[0] = stage_r
        done = S.op("vector", lambda e: e.engine_nop(),
                    deps=list(last_yT_evac) + [t_exp], mark="V")
        pair_attn_done[p] = done
        qk_free[pb] = t_exp

    kprojT = slotA
    kp_toks = {}

    def prep_kprojT():
        xT_dead = pair_qkv_done[7]
        specs = []
        for cc in range(CCn):
            for i in range(2):
                specs.append((wprT_r[:, cc, i * 512:(i + 1) * 512],
                              kprojT[:, cc * 1024 + i * 512: cc * 1024 + (i + 1) * 512],
                              THP, [xT_dead], (cc, i)))
        push_quant_chunks(specs, kp_toks)

    for p in range(8):
        if 2 <= p + 1 < 8:
            emit_pair_qkv(p + 1)
        emit_pair_attention(p)
        if DEBUG and p in (0, 7):
            t_dbg_qk = S.op("sync", lambda e, p=p: e.dma_start(
                d_dbg_qk if p == 0 else d_dbg_qk7, qkb[:, (p % 2) * 4096:(p % 2) * 4096 + 4096]),
                deps=[qkv_evac_tok[(p, h, st)] for h in range(2) for st in range(4)],
                mark="qD%d" % (3 + p // 7), inc=16)
            qk_free[p % 2] = S.op("vector", lambda e: e.engine_nop(),
                                  deps=[t_dbg_qk, qk_free[p % 2]], mark="V")

    t_dbg_yT = None
    if DEBUG:
        t_dbg_yT = S.op("sync", lambda e: e.dma_start(d_dbg_yT, yT),
                        deps=[yT_tok[(cc, tt)] for cc in range(CCn)
                              for tt in range(NT)], mark="qD5", inc=16)


    x2T_tok = {}
    ln1_tok = [None] * NT
    ln_stat_tok = []
    mrb_toks = []
    proj_mm_last = None
    pj_free = [sb_free[0], sb_free[1]]
    ln_col_tok = [None, None]
    scr_tok = [None]
    x2t_ps_free = [t_pd_free]

    for tt in range(NT):
        pbk = tt % 2
        pj = sbanks[pbk]
        ydeps = [yT_tok[(cc, tt)] for cc in range(CCn)]
        mmtok = None
        for ch in range(2):
            for cc in range(CCn):
                first = (ch == 0 and cc == 0)
                mmtok = S.op("tensor", lambda e, ch=ch, cc=cc, tt=tt, pj=pj: e.matmul(
                    pj[:, ch * 512:(ch + 1) * 512],
                    yT[:, cc * 2048 + tt * 128: cc * 2048 + (tt + 1) * 128],
                    kprojT[:, cc * 1024 + ch * 512: cc * 1024 + (ch + 1) * 512],
                    start=(cc == 0), stop=(cc == CCn - 1)),
                    deps=(ydeps + [pj_free[pbk]] if first else []) + [kp_toks[(cc, ch)]],
                    mark="T" if cc == CCn - 1 else None)
        proj_mm_last = mmtok
        t_e0 = S.op("vector", lambda e, pj=pj, tt=tt: e.scalar_tensor_tensor(
            res[:, tt * 1024: tt * 1024 + 512], pj[:, 0:512], col(S2C),
            res[:, tt * 1024: tt * 1024 + 512], ALU.mult, ALU.add,
            accum_out=LNC[:, tt:tt + 1]),
            deps=[mmtok, t_s2, t_xnat], mark="V")
        t_e1 = S.op("vector", lambda e, pj=pj, tt=tt: e.scalar_tensor_tensor(
            res[:, tt * 1024 + 512: tt * 1024 + 1024], pj[:, 512:1024], col(S2C),
            res[:, tt * 1024 + 512: tt * 1024 + 1024], ALU.mult, ALU.add,
            accum_out=LNC[:, 16 + tt:17 + tt]),
            deps=[mmtok, t_s2, t_xnat], mark="V")
        pj_free[pbk] = S.op("vector", lambda e: e.engine_nop(), deps=[t_e0, t_e1], mark="V")

        rt = res[:, tt * 1024:(tt + 1) * 1024]
        t_sq = S.op("scalar", lambda e, rt=rt, tt=tt: e.activation(
            scr1024, rt, AF.Square, accum_out=LNC[:, 32 + tt:33 + tt]),
            deps=[t_e0, t_e1, scr_tok[0]], mark="S")
        scr_tok[0] = t_sq
        ln_stat_tok.append(t_sq)
        ln_stat_tok.append(t_e0)
        ln_stat_tok.append(t_e1)
        if tt in (7, 15):
            lo, hi = (0, 8) if tt == 7 else (8, 16)
            t_hs = S.op("vector", lambda e, lo=lo, hi=hi: e.tensor_add(
                LNC[:, lo:hi], LNC[:, lo:hi], LNC[:, 16 + lo:16 + hi]),
                deps=list(ln_stat_tok), mark="V")
            t_bm = S.op("vector", lambda e, lo=lo, hi=hi: e.tensor_scalar_mul(
                LNC[:, lo:hi], LNC[:, lo:hi], 1.0 / C), deps=[t_hs], mark="V")
            t_bs = S.op("vector", lambda e, lo=lo, hi=hi: e.tensor_scalar_mul(
                LNC[:, 16 + lo:16 + hi], LNC[:, 32 + lo:32 + hi], 1.0 / C),
                deps=[t_hs], mark="V")
            t_m2 = S.op("vector", lambda e, lo=lo, hi=hi: e.tensor_mul(
                LNC[:, 32 + lo:32 + hi], LNC[:, lo:hi], LNC[:, lo:hi]),
                deps=[t_bm, t_bs], mark="V")
            t_var = S.op("vector", lambda e, lo=lo, hi=hi: e.tensor_sub(
                LNC[:, 16 + lo:16 + hi], LNC[:, 16 + lo:16 + hi],
                LNC[:, 32 + lo:32 + hi]), deps=[t_bs, t_m2], mark="V")
            t_ln = S.op("scalar", lambda e, lo=lo, hi=hi: e.activation(
                LNC[:, 32 + lo:32 + hi], LNC[:, 16 + lo:16 + hi], AF.Ln,
                bias=col(EPSC)), deps=[t_var, t_eps], mark="S")
            t_rs = S.op("scalar", lambda e, lo=lo, hi=hi: e.activation(
                LNC[:, 32 + lo:32 + hi], LNC[:, 32 + lo:32 + hi], AF.Exp,
                scale=col(NHLF)), deps=[t_ln, t_nh], mark="S")
            t_mrb = S.op("vector", lambda e, lo=lo, hi=hi: e.tensor_mul(
                LNC[:, 48 + lo:48 + hi], LNC[:, lo:hi], LNC[:, 32 + lo:32 + hi]),
                deps=[t_rs, t_bm], mark="V")
            mrb_toks.append(t_mrb)
            ln_stat_tok.clear()
        if tt >= 8:
            t2 = tt - 8
            ln1_tok[t2] = S.op("vector", lambda e, t2=t2: e.tensor_scalar(
                res[:, t2 * 1024:(t2 + 1) * 1024],
                res[:, t2 * 1024:(t2 + 1) * 1024],
                LNC[:, 32 + t2:33 + t2], LNC[:, 48 + t2:49 + t2],
                ALU.mult, ALU.subtract),
                deps=[mrb_toks[0]], mark="V")


    proj_mm_final = proj_mm_last

    def x2T_ap(cc, tt128, wid):
        if cc < 4:
            return slotA[:, 8192 + cc * 2048 + tt128: 8192 + cc * 2048 + tt128 + wid]
        return slotA[:, (cc - 4) * 2048 + tt128: (cc - 4) * 2048 + tt128 + wid]

    x2t_ps = [p_d, p_c]
    x2t_free = [x2t_ps_free[0], x2t_ps_free[0]]
    gmm_stop = {}
    gmm_started = set()

    def emit_gmm(tt):
        bk = tt // 8
        toff = bk * 512 + (tt % 8) * 64
        xdeps = [x2T_tok[(0, tt)], x2T_tok[(4, tt)], t_gwT, t_ATs,
                 pj_free[0], pj_free[1]]
        bank_first = bk not in gmm_started
        gmm_started.add(bk)
        for cc in range(CCn):
            S.op("tensor", lambda e, cc=cc, tt=tt, toff=toff: e.matmul(
                p_a[:, toff: toff + 8], x2T_ap(cc, tt * 128, 128),
                gwT.rearrange("p (c n) -> p c n", c=CCn)[:, cc, :],
                start=(bank_first and cc == 0), stop=False),
                deps=xdeps if cc == 0 else [], mark=None)
        bank_last = (tt % 8) == 7
        for cc in range(CCn):
            tok = S.op("tensor", lambda e, cc=cc, tt=tt, toff=toff: e.matmul(
                p_a[:, toff + 8: toff + 40], x2T_ap(cc, tt * 128, 128),
                ATs.rearrange("p (c n) -> p c n", c=CCn)[:, cc, :],
                start=False, stop=(bank_last and cc == CCn - 1)),
                deps=[], mark="T" if (bank_last and cc == CCn - 1) else None)
        if bank_last:
            gmm_stop[bk] = tok

    for tt in range(NT):
        pd = x2t_ps[tt % 2]
        tr_deps = [ln1_tok[tt], t_idf, x2t_free[tt % 2], t_pd_free, proj_mm_final]
        tr_last = None
        for cc in range(CCn):
            tr_last = S.op("tensor", lambda e, cc=cc, tt=tt, pd=pd: e.matmul(
                pd[:, (cc % 8) * 128:((cc % 8) + 1) * 128],
                res[:, tt * 1024 + cc * 128: tt * 1024 + (cc + 1) * 128], identf,
                start=True, stop=True),
                deps=tr_deps, mark="T")
        t_evu = S.op("scalar", lambda e, tt=tt, pd=pd: e.copy(
            slotA.rearrange("p (c w) -> p c w", c=8)[:, 4:8, tt * 128:(tt + 1) * 128],
            pd.rearrange("p (c w) -> p c w", c=8)[:, 0:4, :]),
            deps=[tr_last], mark="S")
        t_evl = S.op("vector", lambda e, tt=tt, pd=pd: e.tensor_copy(
            slotA.rearrange("p (c w) -> p c w", c=8)[:, 0:4, tt * 128:(tt + 1) * 128],
            pd.rearrange("p (c w) -> p c w", c=8)[:, 4:8, :]),
            deps=[tr_last], mark="V")
        for cc in range(CCn):
            x2T_tok[(cc, tt)] = t_evl if cc < 4 else t_evu
        x2t_free[tt % 2] = S.op("vector", lambda e: e.engine_nop(),
                                deps=[t_evu, t_evl], mark="V")
        if tt < 8:
            t2 = tt + 8
            ln1_tok[t2] = S.op("vector", lambda e, t2=t2: e.tensor_scalar(
                res[:, t2 * 1024:(t2 + 1) * 1024],
                res[:, t2 * 1024:(t2 + 1) * 1024],
                LNC[:, 32 + t2:33 + t2], LNC[:, 48 + t2:49 + t2],
                ALU.mult, ALU.subtract),
                deps=[mrb_toks[1]], mark="V")
        if tt > 0:
            emit_gmm(tt - 1)
    emit_gmm(NT - 1)

    t_dbg_res = None
    if DEBUG:
        t_dbg_res = S.op("sync", lambda e: e.dma_start(d_dbg_res, res),
                         deps=list(ln1_tok), mark="qD6", inc=16)
    t_BT = S.op("gpsimd", lambda e: e.dma_start(BTs, d_BT),
                deps=[proj_mm_last, yT_tok[(7, NT - 1)], t_dbg_yT], mark="qBT", inc=16)

    x2b_tok = {}
    for tt in range(2):
        x2b_tok[tt] = S.op("vector", lambda e, tt=tt: e.tensor_add(
            res[:, tt * 1024:(tt + 1) * 1024], res[:, tt * 1024:(tt + 1) * 1024], wob),
            deps=[ln1_tok[tt], scr_tok[0], t_wob,
                  x2T_tok[(7, tt)], x2T_tok[(0, tt)], t_dbg_res], mark="V")

    q_dead = q_chain[0]
    Lps = p_a[:, 0:1024].rearrange("p (t x) -> p t x", t=16)
    Lsc = qscr[:, 0:128].rearrange("p (t e) -> p t e", t=16)
    Msc = qscr[:, 128:256].rearrange("p (t e) -> p t e", t=16)
    Vsc = qscr[:, 256:384].rearrange("p (t e) -> p t e", t=16)
    Ksc = qscr[:, 384:512].rearrange("p (t e) -> p t e", t=16)
    t_lg = S.op("vector", lambda e: e.tensor_tensor(
        Lsc, Lps[:, :, 0:8],
        gbuf.unsqueeze(1).broadcast_to([128, 16, 8]), ALU.add),
        deps=[gmm_stop[0], gmm_stop[1], t_gb, q_dead], mark="V")
    t_gm = S.op("vector", lambda e: e.reduce_max(
        LNC[:, 0:16].unsqueeze(2), Lsc, AX.X), deps=[t_lg] + mrb_toks, mark="V")
    t_im = S.op("vector", lambda e: e.tensor_tensor(
        Msc, Lsc, LNC[:, 0:16].unsqueeze(2).broadcast_to([128, 16, 8]),
        ALU.is_ge), deps=[t_gm], mark="V")
    t_v8 = S.op("vector", lambda e: e.scalar_tensor_tensor(
        Vsc, Msc, -BIGI, iotapb.unsqueeze(1).broadcast_to([128, 16, 8]),
        ALU.mult, ALU.add), deps=[t_im, t_iot], mark="V")
    t_ix = S.op("vector", lambda e: e.tensor_reduce(
        LNC[:, 16:32].unsqueeze(2), Vsc, AX.X, ALU.min), deps=[t_v8], mark="V")
    t_kp = S.op("vector", lambda e: e.tensor_tensor(
        Ksc, Vsc, LNC[:, 16:32].unsqueeze(2).broadcast_to([128, 16, 8]),
        ALU.is_le), deps=[t_ix], mark="V")
    rmall = scr1024[:, 0:512]
    t_me = S.op("vector", lambda e: e.tensor_tensor(
        rmall.rearrange("p (t e r) -> p t e r", t=16, e=8),
        Lps[:, :, 8:40].rearrange("p t (e r) -> p t e r", e=8),
        Ksc.unsqueeze(3).broadcast_to([128, 16, 8, 4]), ALU.mult),
        deps=[t_kp, scr_tok[0]], mark="V")

    rmT_tok = {}
    rm_state = {"re": None}
    ffn_bank_free = {0: t_me, 1: pj_free[1], 2: None}

    def emit_rm_round(r):
        tr_t = None
        for k in range(8):
            tt = r * 8 + k
            tr_t = S.op("tensor", lambda e, tt=tt, k=k: e.matmul(
                p_d[0:32, k * 128:(k + 1) * 128],
                rmall[:, tt * 32:(tt + 1) * 32], ident,
                start=True, stop=True),
                deps=([t_me, t_ident, x2t_free[0], rm_state["re"]] if k == 0 else []),
                mark="T" if k == 7 else None)
        re_tok = S.op("scalar", lambda e, r=r: e.copy(
            rmT[:, r * 1024:(r + 1) * 1024], p_d[0:32, 0:1024]),
            deps=[tr_t], mark="S")
        for k in range(8):
            rmT_tok[r * 8 + k] = re_tok
        rm_state["re"] = re_tok
        ffn_bank_free[2] = re_tok

    emit_rm_round(0)
    emit_rm_round(1)

    for tt in range(2, NT):
        t0 = S.op("vector", lambda e, tt=tt: e.tensor_add(
            res[:, tt * 1024:(tt + 1) * 1024], res[:, tt * 1024:(tt + 1) * 1024], wob),
            deps=[ln1_tok[tt], scr_tok[0], t_wob,
                  x2T_tok[(7, tt)], x2T_tok[(0, tt)], t_dbg_res], mark="V")
        x2b_tok[tt] = t0

    wiT_r = d_wiT.rearrange("(c p) f -> p c f", p=128)
    woT_r = d_woT.rearrange("(f p) c -> p f c", p=128)
    wdb = [wdb0, wdb1]
    wdb_tok = [None, None]
    wdb_free = [[pair_attn_done[7], t_dbg_va],
                [pair_attn_done[7], q_chain[0]]]

    def load_group(g):
        gb_ = g % 2
        t_wi = S.op("gpsimd", lambda e, g=g, gb_=gb_: e.dma_start(
            wdb[gb_][:, 0:8192].rearrange("p (c f) -> p c f", c=CCn),
            wiT_r[:, :, g * 1024:(g + 1) * 1024]),
            deps=wdb_free[gb_], mark="qF%d" % gb_, inc=16)
        t_wo = S.op("gpsimd", lambda e, g=g, gb_=gb_: e.dma_start(
            wdb[gb_][:, 8192:16384].rearrange("p (f c) -> p f c", f=CCn),
            woT_r[:, g * CCn:(g + 1) * CCn, :]),
            deps=[], mark="qF%d" % gb_, inc=16)
        wdb_tok[gb_] = [t_wo]

    load_group(0)
    load_group(1)

    it_ps_free = [x2t_free[1], x2t_free[1]]
    it_sb_tok = [t_dbg_yT, t_dbg_yT, t_dbg_yT]
    fbanks = [p_a, p_b, p_d]
    add_tok = {tt: None for tt in range(NT)}
    ln2_done = []
    out_dma_toks = []

    pend_ln2 = []

    pend_ln2b = []

    def flush_ln2_stats():
        if not pend_ln2:
            return
        tt, par, atok = pend_ln2.pop(0)
        B0 = LSUM if par == 0 else 43
        GMx = GM if par == 0 else GM2
        rt = res[:, tt * 1024:(tt + 1) * 1024]
        t_hs = S.op("vector", lambda e, B0=B0, GMx=GMx: e.tensor_add(
            col(B0), col(B0), col(GMx)), deps=[atok, ln_col_tok[par]], mark="V")
        t_sq = S.op("vector", lambda e, rt=rt, B0=B0: e.scalar_tensor_tensor(
            scr1024, rt, 1.0, rt, ALU.mult, ALU.mult, accum_out=col(B0 + 1)),
            deps=[atok, scr_tok[0]], mark="V")
        scr_tok[0] = t_sq
        t_m = S.op("vector", lambda e, B0=B0: e.tensor_scalar_mul(
            col(B0), col(B0), 1.0 / C), deps=[t_hs], mark="V")
        t_v1 = S.op("vector", lambda e, B0=B0: e.tensor_scalar_mul(
            col(B0 + 1), col(B0 + 1), 1.0 / C), deps=[t_sq], mark="V")
        t_m2 = S.op("vector", lambda e, B0=B0: e.tensor_mul(
            col(B0 + 2), col(B0), col(B0)), deps=[t_m], mark="V")
        t_var = S.op("vector", lambda e, B0=B0: e.tensor_sub(
            col(B0 + 2), col(B0 + 1), col(B0 + 2)), deps=[t_v1, t_m2], mark="V")
        t_ln = S.op("scalar", lambda e, B0=B0: e.activation(
            col(B0 + 3), col(B0 + 2), AF.Ln, bias=col(EPSC)),
            deps=[t_var], mark="S")
        t_rs = S.op("scalar", lambda e, B0=B0: e.activation(
            col(B0 + 3), col(B0 + 3), AF.Exp, scale=col(NHLF)),
            deps=[t_ln], mark="S")
        t_mr = S.op("vector", lambda e, B0=B0: e.tensor_mul(
            col(B0 + 4), col(B0), col(B0 + 3)), deps=[t_rs, t_m], mark="V")
        pend_ln2b.append((rt, B0, par, t_mr, t_sq, tt))

    def flush_ln2_store():
        while pend_ln2b:
            rt, B0, par, t_mr, t_sq, tt = pend_ln2b.pop(0)
            t_nrm = S.op("vector", lambda e, rt=rt, B0=B0: e.tensor_scalar(
                rt, rt, col(B0 + 3), col(B0 + 4), ALU.mult, ALU.subtract),
                deps=[t_mr, t_sq], mark="V")
            ln_col_tok[par] = t_nrm
            t_st = S.op("sync", lambda e, rt=rt, tt=tt: e.dma_start(
                d_out.rearrange("(t p) c -> p t c", p=128)[:, tt, :], rt),
                deps=[t_nrm], mark="qO", inc=16)
            out_dma_toks.append(t_st)

    for g in range(4):
        gb_ = g % 2
        gdeps = wdb_tok[gb_]
        for tb in range(8):
            wo_last = {}
            pend_wo = []

            def flush_wo(min_keep, tb=tb, gb_=gb_, wo_last=wo_last):
                while len(pend_wo) > min_keep:
                    fc, ib3, t_rl = pend_wo.pop(0)
                    slots = [(2 * tb + tt2) % 3 for tt2 in range(2)]
                    wdeps = [t_rl] + ([ffn_bank_free[slots[0]], ffn_bank_free[slots[1]]]
                                      if fc == 0 else [])
                    for tt2 in range(2):
                        for ch in range(2):
                            wo_last[(tt2, ch)] = S.op(
                                "tensor",
                                lambda e, tt2=tt2, ch=ch, fc=fc, ib3=ib3, gb_=gb_, sl=slots[tt2]: e.matmul(
                                    fbanks[sl][:, ch * 512:(ch + 1) * 512],
                                    it_sb[:, ib3 * 256 + tt2 * 128: ib3 * 256 + (tt2 + 1) * 128],
                                    wdb[gb_][:, 8192 + fc * 1024 + ch * 512:
                                             8192 + fc * 1024 + (ch + 1) * 512],
                                    start=(fc == 0), stop=(fc == CCn - 1)),
                                deps=wdeps, mark="T" if fc == CCn - 1 else None)
                    it_sb_tok[ib3] = wo_last[(1, 1)]

            for fc in range(CCn):
                gf = g * 8 + fc
                ib = fc % 2
                ib3 = fc % 3
                ips = p_c[:, ib * 512: ib * 512 + 256]
                first_deps = gdeps + [it_ps_free[ib],
                                      x2b_tok[2 * tb], x2b_tok[2 * tb + 1]]
                mmtok = None
                for cc in range(CCn):
                    mmtok = S.op("tensor", lambda e, cc=cc, fc=fc, tb=tb, gb_=gb_, ips=ips: e.matmul(
                        ips, wdb[gb_][:, cc * 1024 + fc * 128: cc * 1024 + (fc + 1) * 128],
                        x2T_ap(cc, tb * 256, 256),
                        start=(cc == 0), stop=False),
                        deps=first_deps if cc == 0 else [], mark=None)
                mmtok = S.op("tensor", lambda e, gf=gf, tb=tb, ips=ips: e.matmul(
                    ips, BTs[:, gf * 128:(gf + 1) * 128],
                    rmT[:, tb * 256:(tb + 1) * 256],
                    start=False, stop=True),
                    deps=[t_BT, rmT_tok[2 * tb], rmT_tok[2 * tb + 1]], mark="T")
                flush_wo(1)
                if fc in (3, 5):
                    flush_ln2_stats()
                if fc == 7:
                    flush_ln2_store()
                t_rl = S.op("scalar", lambda e, ib3=ib3, gf=gf, ips=ips: e.activation(
                    it_sb[:, ib3 * 256:(ib3 + 1) * 256], ips, AF.Relu,
                    bias=wib[:, gf:gf + 1]),
                    deps=[mmtok, t_wib, it_sb_tok[ib3]], mark="S")
                it_ps_free[ib] = t_rl
                pend_wo.append((fc, ib3, t_rl))
            flush_wo(0)
            for tt2 in range(2):
                tt = 2 * tb + tt2
                sl = (2 * tb + tt2) % 3
                B0 = LSUM if tt2 == 0 else 43
                GMx = GM if tt2 == 0 else GM2
                t_a0 = S.op("vector", lambda e, tt=tt, sl=sl, B0=B0: e.scalar_tensor_tensor(
                    res[:, tt * 1024: tt * 1024 + 512],
                    res[:, tt * 1024: tt * 1024 + 512], 1.0, fbanks[sl][:, 0:512],
                    ALU.mult, ALU.add, accum_out=col(B0)),
                    deps=[wo_last[(tt2, 0)], x2b_tok[tt], add_tok[tt]], mark="V")
                t_a1 = S.op("vector", lambda e, tt=tt, sl=sl, GMx=GMx: e.scalar_tensor_tensor(
                    res[:, tt * 1024 + 512: tt * 1024 + 1024],
                    res[:, tt * 1024 + 512: tt * 1024 + 1024], 1.0, fbanks[sl][:, 512:1024],
                    ALU.mult, ALU.add, accum_out=col(GMx)),
                    deps=[wo_last[(tt2, 1)], add_tok[tt]], mark="V")
                add_tok[tt] = S.op("vector", lambda e: e.engine_nop(),
                                   deps=[t_a0, t_a1], mark="V")
                ffn_bank_free[sl] = add_tok[tt]
            if g == 3:
                pend_ln2.append((2 * tb, 0, add_tok[2 * tb]))
                pend_ln2.append((2 * tb + 1, 1, add_tok[2 * tb + 1]))
        wdb_free[gb_] = [wo_last[(1, 1)]]
        if g + 2 < 4:
            load_group(g + 2)
    while pend_ln2 or pend_ln2b:
        flush_ln2_stats()
        flush_ln2_stats()
        flush_ln2_store()

    S.barrier_wait("sync", out_dma_toks)

    with nc.Block() as block:
        @block.sync
        def _(eng):
            S.replay("sync", eng)

        @block.tensor
        def _(eng):
            S.replay("tensor", eng)

        @block.vector
        def _(eng):
            S.replay("vector", eng)

        @block.scalar
        def _(eng):
            S.replay("scalar", eng)

        @block.gpsimd
        def _(eng):
            S.replay("gpsimd", eng)

    return nc


_NC_CACHE = None


def kernel(L0_attn_x, L0_attn_w_attn, L0_attn_w_proj, L0_ffn_gate_W, L0_ffn_gate_b,
           L0_ffn_wi_W, L0_ffn_wi_b, L0_ffn_wo_W, L0_ffn_wo_b, L0_ffn_lora_A,
           L0_ffn_lora_B):
    global _NC_CACHE
    x = np.asarray(L0_attn_x, np.float32)
    w_attn = np.asarray(L0_attn_w_attn, np.float32)
    w_proj = np.asarray(L0_attn_w_proj, np.float32)
    gate_W = np.asarray(L0_ffn_gate_W, np.float32)
    gate_b = np.asarray(L0_ffn_gate_b, np.float32)
    wi_W = np.asarray(L0_ffn_wi_W, np.float32)
    wi_b = np.asarray(L0_ffn_wi_b, np.float32)
    wo_W = np.asarray(L0_ffn_wo_W, np.float32)
    wo_b = np.asarray(L0_ffn_wo_b, np.float32)
    lora_A = np.asarray(L0_ffn_lora_A, np.float32)
    lora_B = np.asarray(L0_ffn_lora_B, np.float32)

    watT = np.ascontiguousarray(w_attn.T)
    wprT = np.ascontiguousarray(w_proj.T)
    gwT = np.ascontiguousarray(gate_W.T)
    gb = np.ascontiguousarray(np.tile(gate_b[None, :], (128, 1)))
    wiT = np.ascontiguousarray(wi_W.T)
    wib = np.ascontiguousarray(wi_b.reshape(32, 128).T)
    woT = np.ascontiguousarray(wo_W.T)
    wob = np.ascontiguousarray(np.tile(wo_b[None, :], (128, 1)))
    AT = np.ascontiguousarray(lora_A.reshape(32, C).T)
    BT = np.ascontiguousarray(lora_B.transpose(0, 2, 1).reshape(32, FF))
    ident = np.eye(128, dtype=ml_dtypes.bfloat16)
    identf = np.eye(128, dtype=np.float32)
    rr, cc_ = np.indices((128, 128))
    tril = np.where(rr > cc_, NEG, 0.0).astype(ml_dtypes.bfloat16)
    iotapb = np.ascontiguousarray(
        np.tile(np.arange(NE, dtype=np.float32)[None, :] + BIGI, (128, 1)))

    if _NC_CACHE is None:
        _NC_CACHE = build_program()
    nc = _NC_CACHE

    shared = {"watT": watT, "wprT": wprT, "gwT": gwT, "gb": gb, "wiT": wiT,
              "wib": wib, "woT": woT, "wob": wob, "AT": AT, "BT": BT,
              "ident": ident, "identf": identf, "tril": tril, "iotapb": iotapb}
    in_maps = []
    for b in range(8):
        m = dict(shared)
        m["x_nat"] = np.ascontiguousarray(x[b])
        in_maps.append(m)

    resobj = run_bass_kernel_spmd(nc, in_maps, core_ids=list(range(8)))
    return np.stack([r["out"] for r in resobj.results], axis=0)


if __name__ == "__main__":
    import sys
    sys.path.insert(0, "/root/problem")
    import reference as R
    d = R.setup_inputs(0)
    inp = {k: np.asarray(v) for k, v in d.items()}
    got = kernel(**inp)
    ref = np.asarray(R.reference(**d))
    num = float(np.mean((got.astype(np.float64) - ref.astype(np.float64)) ** 2))
    den = float(np.mean(ref.astype(np.float64) ** 2))
    print("ratio:", num / den)


# version 30
# speedup vs baseline: 1.2057x; 1.2057x over previous
import numpy as np
import ml_dtypes
import concourse.bass as bass
import concourse.mybir as mybir
from concourse.bass_utils import run_bass_kernel_spmd

F32 = mybir.dt.float32
BF16 = mybir.dt.bfloat16
I32 = mybir.dt.int32
U8 = mybir.dt.uint8
AF = mybir.ActivationFunctionType
ALU = mybir.AluOpType
AX = mybir.AxisListType

T = 2048
C = 1024
NH = 16
HD = 64
FF = 4096
NE = 8
LR = 4
NT = 16
CCn = 8
NEG = -1.0e30
BIGI = 1000.0

ENGS = ("sync", "tensor", "vector", "scalar", "gpsimd")
DEBUG = False
PREFETCH_QKV = True


class Sched:

    def __init__(self, nc):
        self.nc = nc
        self.sems = {}
        self.cnt = {}
        self.seen = {}
        self.ops = {e: [] for e in ENGS}

    def _sem(self, name):
        if name not in self.sems:
            self.sems[name] = self.nc.alloc_semaphore(name="sem_" + name)
            self.cnt[name] = 0
        return self.sems[name]

    def op(self, eng, build, deps=(), mark=None, inc=1):
        waits = []
        for d in deps:
            if d is None:
                continue
            s, v = d
            if self.seen.get((eng, s), 0) < v:
                waits.append((s, v))
                self.seen[(eng, s)] = v
        tok = None
        if mark is not None:
            self._sem(mark)
            self.cnt[mark] += inc
            tok = (mark, self.cnt[mark])
        self.ops[eng].append((waits, build, mark, inc))
        return tok

    def barrier_wait(self, eng, deps):
        best = {}
        for d in deps:
            if d is None:
                continue
            s, v = d
            best[s] = max(best.get(s, 0), v)
        self.op(eng, None, deps=list(best.items()), mark=None)

    def replay(self, eng_name, handle):
        for waits, build, mark, inc in self.ops[eng_name]:
            for s, v in waits:
                handle.wait_ge(self.sems[s], v)
            if build is not None:
                ins = build(handle)
                if mark is not None:
                    ins.then_inc(self.sems[mark], inc)


def build_program():
    nc = bass.Bass("TRN2", target_bir_lowering=False, debug=False)

    d_x = nc.dram_tensor("x_nat", [T, C], F32, kind="ExternalInput").ap()
    d_watT = nc.dram_tensor("watT", [C, 3 * C], F32, kind="ExternalInput").ap()
    d_wprT = nc.dram_tensor("wprT", [C, C], F32, kind="ExternalInput").ap()
    d_gwT = nc.dram_tensor("gwT", [C, NE], F32, kind="ExternalInput").ap()
    d_gb = nc.dram_tensor("gb", [128, NE], F32, kind="ExternalInput").ap()
    d_wiT = nc.dram_tensor("wiT", [C, FF], F32, kind="ExternalInput").ap()
    d_wib = nc.dram_tensor("wib", [128, 32], F32, kind="ExternalInput").ap()
    d_woT = nc.dram_tensor("woT", [FF, C], F32, kind="ExternalInput").ap()
    d_wob = nc.dram_tensor("wob", [128, C], F32, kind="ExternalInput").ap()
    d_AT = nc.dram_tensor("AT", [C, 32], F32, kind="ExternalInput").ap()
    d_BT = nc.dram_tensor("BT", [32, FF], F32, kind="ExternalInput").ap()
    d_ident = nc.dram_tensor("ident", [128, 128], BF16, kind="ExternalInput").ap()
    d_identf = nc.dram_tensor("identf", [128, 128], F32, kind="ExternalInput").ap()
    d_tril = nc.dram_tensor("tril", [128, 128], BF16, kind="ExternalInput").ap()
    d_iotapb = nc.dram_tensor("iotapb", [128, NE], F32, kind="ExternalInput").ap()
    d_out = nc.dram_tensor("out", [T, C], F32, kind="ExternalOutput").ap()
    if DEBUG:
        d_dbg_xT = nc.dram_tensor("dbg_xT", [128, 16384], BF16, kind="ExternalOutput").ap()
        d_dbg_va = nc.dram_tensor("dbg_va", [128, 16640], BF16, kind="ExternalOutput").ap()
        d_dbg_qk = nc.dram_tensor("dbg_qk", [128, 4096], BF16, kind="ExternalOutput").ap()
        d_dbg_qk7 = nc.dram_tensor("dbg_qk7", [128, 4096], BF16, kind="ExternalOutput").ap()
        d_dbg_yT = nc.dram_tensor("dbg_yT", [128, 16384], BF16, kind="ExternalOutput").ap()
        d_dbg_res = nc.dram_tensor("dbg_res", [128, 16384], F32, kind="ExternalOutput").ap()

    AR = 212160
    arena = nc.alloc_sbuf_tensor("arena", [128, AR], U8).ap()

    def vw(off, nelem, dt):
        nb = {F32: 4, BF16: 2, I32: 4}[dt] * nelem
        return arena[:, off:off + nb].bitcast(dt)

    res = vw(0, 16384, F32)
    slotA = vw(65536, 16384, BF16)
    ident = vw(98304, 128, BF16)
    tril = vw(98560, 128, BF16)
    ones128 = vw(99072, 128, F32)
    iotapb = vw(99584, NE, F32)
    gbuf = vw(99616, NE, F32)
    wib = vw(99648, 32, F32)
    gwT = vw(99776, 64, BF16)
    ATs = vw(99904, 256, BF16)
    ypairs = vw(100416, 512, BF16)
    colv = vw(101440, 64, F32)
    rm_bf = vw(101696, 32, BF16)
    gscr = vw(101760, 24, F32)
    keep8 = vw(101856, NE, F32)
    scr1024 = vw(101888, 1024, BF16)
    wob = vw(104448, 1024, F32)
    V_aug = vw(108544, 16640, BF16)
    qkb = vw(141824, 8192, BF16)
    ptb = vw(158208, 2048, BF16)
    pairW = vw(162304, 4096, BF16)
    wstg = vw(170496, 1024, F32)
    qscr = vw(174592, 1024, F32)
    yT = vw(178688, 16384, BF16)
    wdb0 = vw(108544, 16384, BF16)
    wdb1 = vw(141312, 16384, BF16)
    rmT = vw(174592, 2048, BF16)[0:32, :]
    it_sb = vw(178688, 768, BF16)
    BTs = vw(180224, 4096, BF16)[0:32, :]
    identf = vw(211456, 128, F32)
    rm_bf2 = vw(211968, 32, BF16)
    gscr2 = vw(212032, 24, F32)
    keep8b = vw(212128, NE, F32)
    LNC = vw(104192, 64, F32)

    def col(k):
        return colv[:, k:k + 1]

    ACC_A, TCOL, ACC_P, SCL_A, RCP_A, ALPH, SCL_P, RCP_P, S2C = range(9)
    EPSC, NHLF = 9, 10
    LSUM, LM, LSQ, LT1, LVAR, LRSTD, LMR = 11, 12, 13, 14, 15, 16, 17
    GM, GIDX = 18, 19
    GM2, GIDX2 = 41, 42
    RC0 = 20
    THA = 28
    THP = 34

    p_a = nc.alloc_psum_tensor("p_a", [128, 1024], F32).ap()
    p_b = nc.alloc_psum_tensor("p_b", [128, 1024], F32).ap()
    p_c = nc.alloc_psum_tensor("p_c", [128, 1024], F32).ap()
    p_d = nc.alloc_psum_tensor("p_d", [128, 1024], F32).ap()
    sbanks = [p_a, p_b]
    ybanks = [p_c, p_d]

    S = Sched(nc)

    S.op("sync", lambda e: e.dma_start(ident, d_ident), mark="qC", inc=16)
    S.op("sync", lambda e: e.dma_start(identf, d_identf), mark="qC", inc=16)
    S.op("sync", lambda e: e.dma_start(tril, d_tril), mark="qC", inc=16)
    S.op("sync", lambda e: e.dma_start(iotapb, d_iotapb), mark="qC", inc=16)
    S.op("sync", lambda e: e.dma_start(gbuf, d_gb), mark="qC", inc=16)
    S.op("sync", lambda e: e.dma_start(wib, d_wib), mark="qC", inc=16)
    t_const_all = S.op("sync", lambda e: e.dma_start(wob, d_wob), mark="qC", inc=16)
    t_ident = t_tril = t_iot = t_gb = t_wib = t_wob = t_idf = t_const_all
    S.op("gpsimd", lambda e: e.dma_start(
        gwT.rearrange("p (c n) -> p c n", c=CCn),
        d_gwT.rearrange("(c p) n -> p c n", p=128)), mark="qG2", inc=16)
    t_ATs = S.op("gpsimd", lambda e: e.dma_start(
        ATs.rearrange("p (c n) -> p c n", c=CCn),
        d_AT.rearrange("(c p) n -> p c n", p=128)), mark="qG2", inc=16)
    t_gwT = t_ATs

    t_dbg_xT = None
    if DEBUG:
        t_dbg_xT = S.op("sync", lambda e: e.dma_start(d_dbg_xT, slotA),
                        deps=[t_xT], mark="qD1", inc=16)

    t_ones = S.op("vector", lambda e: e.memset(ones128, 1.0), mark="V")
    t_eps = S.op("vector", lambda e: e.memset(col(EPSC), 1e-5), mark="V")
    t_nh = S.op("vector", lambda e: e.memset(col(NHLF), -0.5), mark="V")
    t_acca = S.op("vector", lambda e: e.memset(col(ACC_A), 0.0), mark="V")
    t_accp = S.op("vector", lambda e: e.memset(col(ACC_P), 0.0), mark="V")
    watT_r = d_watT.rearrange("(c p) o -> p c o", p=128)
    wprT_r = d_wprT.rearrange("(c p) o -> p c o", p=128)

    vstg = vw(108544, 6144, F32)
    p1_tok = [None, None]
    last_ta = [None]

    def pass1(src_r, width, lnc_base, tok):
        for cc in range(CCn):
            h = cc % 2
            stg = vstg[:, h * 3072: h * 3072 + width]
            tdma = S.op("sync", lambda e, stg=stg, s=src_r[:, cc, :]: e.dma_start(stg, s),
                        deps=[p1_tok[h]], mark="qP%d" % h, inc=16)
            ta = S.op("scalar", lambda e, stg=stg, cc=cc, lnc_base=lnc_base: e.activation(
                stg, stg, AF.Abs, accum_out=LNC[:, lnc_base + cc: lnc_base + cc + 1]),
                deps=[tdma], mark="S")
            p1_tok[h] = ta
            tok[0] = ta

    x_chunk_tok = []
    for k in range(4):
        t = S.op("sync", lambda e, k=k: e.dma_start(
            res.rearrange("p (t c) -> p t c", t=NT)[:, 4 * k:4 * (k + 1), :],
            d_x.rearrange("(t p) c -> p t c", p=128)[:, 4 * k:4 * (k + 1), :]),
            mark="qX%d" % k, inc=16)
        x_chunk_tok.append(t)
    t_xnat = x_chunk_tok[3]

    t_sum_a = [t_acca]
    t_sum_p = [t_accp]
    pass1(watT_r, 3 * C, 0, t_sum_a)
    pass1(wprT_r, C, 8, t_sum_p)
    ta_fin = t_sum_a[0]
    tp_fin = t_sum_p[0]
    wstg_free = [None, None, None]
    t_vone = S.op("vector", lambda e: e.memset(
        V_aug.rearrange("p (t h n) -> p (t h) n", t=NT, h=NH)[:, :, 64:65], 1.0),
        deps=[p1_tok[0], p1_tok[1]], mark="V")

    t_mm_a = S.op("tensor", lambda e: e.matmul(p_d[:, 0:8], ones128, LNC[:, 0:8],
                                               start=True, stop=True),
                  deps=[t_ones, ta_fin], mark="T")
    t_mm_p = S.op("tensor", lambda e: e.matmul(p_d[:, 512:520], ones128, LNC[:, 8:16],
                                               start=True, stop=True),
                  deps=[t_ones, tp_fin], mark="T")
    NA = float(3 * C * C)
    NP = float(C * C)
    t_ra = S.op("vector", lambda e: e.reduce_sum(
        col(ACC_A), p_d[:, 0:8], AX.X), deps=[t_mm_a, t_acca], mark="V")
    t_rp = S.op("vector", lambda e: e.reduce_sum(
        col(ACC_P), p_d[:, 512:520], AX.X), deps=[t_mm_p, t_accp], mark="V")
    t_scla = S.op("vector", lambda e: e.tensor_scalar(
        col(SCL_A), col(ACC_A), 1.0 / NA, 1e-8, ALU.mult, ALU.max),
        deps=[t_ra], mark="V")
    t_rcpa = S.op("vector", lambda e: e.reciprocal(col(RCP_A), col(SCL_A)),
                  deps=[t_scla], mark="V")
    t_al1 = S.op("vector", lambda e: e.tensor_mul(col(ALPH), col(SCL_A), col(SCL_A)),
                 deps=[t_scla], mark="V")
    t_alph = S.op("vector", lambda e: e.tensor_scalar_mul(col(ALPH), col(ALPH), 0.125),
                  deps=[t_al1], mark="V")
    t_sclp = S.op("vector", lambda e: e.tensor_scalar(
        col(SCL_P), col(ACC_P), 1.0 / NP, 1e-8, ALU.mult, ALU.max),
        deps=[t_rp], mark="V")
    t_rcpp = S.op("vector", lambda e: e.reciprocal(col(RCP_P), col(SCL_P)),
                  deps=[t_sclp], mark="V")
    t_s2 = S.op("vector", lambda e: e.tensor_mul(col(S2C), col(SCL_A), col(SCL_P)),
                deps=[t_sclp, t_scla], mark="V")
    t_pd_free = S.op("vector", lambda e: e.engine_nop(), deps=[t_scla, t_sclp], mark="V")

    t_tha = t_thp = None
    for i, f in enumerate((0.5, -0.5, 1.5, -1.5)):
        t_tha = S.op("vector", lambda e, i=i, f=f: e.tensor_scalar_mul(
            col(THA + i), col(SCL_A), f), deps=[t_scla], mark="V")
        t_thp = S.op("vector", lambda e, i=i, f=f: e.tensor_scalar_mul(
            col(THP + i), col(SCL_P), f), deps=[t_sclp], mark="V")
    t_ths = S.op("vector", lambda e: e.engine_nop(), deps=[t_tha, t_thp], mark="V")

    q_chain = [None]
    wstg_bufs = [wstg[:, 0:512], wstg[:, 512:1024], qscr[:, 512:1024]]
    wstg_rr = [0]
    qv_steps = []

    def make_quant_steps(src_ap, dst_ap, th, dst_war=()):
        cell = {}
        qa = qscr[:, 0:512]

        def s_dma():
            wh = wstg_rr[0] % 3
            wstg_rr[0] += 1
            stg = wstg_bufs[wh]
            sview = stg if src_ap.ndim == 2 else stg.rearrange(
                "p (c o) -> p c o", c=src_ap.shape[1])
            cell["stg"] = stg
            cell["wh"] = wh
            cell["td"] = S.op("sync", lambda e, sview=sview: e.dma_start(sview, src_ap),
                              deps=[wstg_free[wh]], mark="qW%d" % wh, inc=16)

        def s1():
            cell["t"] = S.op("vector", lambda e: e.tensor_scalar(
                qa, cell["stg"], col(th + 0), 2.0, ALU.is_ge, ALU.subtract),
                deps=[cell["td"], t_ths, q_chain[0]], mark="V")

        def s2():
            cell["t"] = S.op("vector", lambda e: e.scalar_tensor_tensor(
                dst_ap, cell["stg"], col(th + 1), qa, ALU.is_gt, ALU.add),
                deps=[cell["t"]] + list(dst_war), mark="V")

        def s3():
            cell["t"] = S.op("vector", lambda e: e.scalar_tensor_tensor(
                qa, cell["stg"], col(th + 2), dst_ap, ALU.is_ge, ALU.add),
                deps=[cell["t"]], mark="V")

        def s4():
            t4 = S.op("vector", lambda e: e.scalar_tensor_tensor(
                dst_ap, cell["stg"], col(th + 3), qa, ALU.is_gt, ALU.add),
                deps=[cell["t"]], mark="V")
            wstg_free[cell["wh"]] = t4
            q_chain[0] = t4
            cell["t4"] = t4
        return cell, [s_dma, s1, s2, s3, s4]

    def quant(src_ap, dst_ap, th, dst_war=()):
        assert not qv_steps
        cell, steps = make_quant_steps(src_ap, dst_ap, th, dst_war)
        for s in steps:
            s()
        return cell["t4"]

    def push_quant_chunks(specs, sink):
        built = []
        for (src, dst, th, war, key) in specs:
            cell, steps = make_quant_steps(src, dst, th, war)

            def fin(cell=cell, key=key):
                sink[key] = cell["t4"]
            built.append(steps + [fin])
        out = [built[0][0]]
        if len(built) > 1:
            out.append(built[1][0])
        for c in range(len(built)):
            out.extend(built[c][1:])
            if c + 2 < len(built):
                out.append(built[c + 2][0])
        qv_steps.extend(out)

    def pump_quant(n):
        k = 0
        while qv_steps and k < n:
            qv_steps.pop(0)()
            k += 1

    Vv = V_aug.rearrange("p (t h n) -> p t h n", t=NT, h=NH)
    vbank_free = [t_pd_free, t_pd_free, t_pd_free, t_pd_free]
    vbank_ap = [p_a[:, 0:512], p_a[:, 512:1024], p_b[:, 0:512], p_b[:, 512:1024]]
    xTv = slotA.rearrange("p (c t) -> p c t", c=CCn)

    xTile_tok = [None] * NT
    xps = [p_d, p_c]
    xps_free = [[t_ra, t_rp], []]
    for tt in range(NT):
        pd = xps[tt % 2]
        deps0 = [x_chunk_tok[tt // 4], t_idf] + xps_free[tt % 2]
        tr_last = None
        for cc in range(CCn):
            tr_last = S.op("tensor", lambda e, cc=cc, tt=tt, pd=pd: e.matmul(
                pd[:, cc * 128:(cc + 1) * 128],
                res[:, tt * 1024 + cc * 128: tt * 1024 + (cc + 1) * 128], identf,
                start=True, stop=True),
                deps=deps0 if cc == 0 else [], mark="T")
        t_xvu = S.op("scalar", lambda e, tt=tt, pd=pd: e.copy(
            slotA.rearrange("p (c w) -> p c w", c=8)[:, 0:4, tt * 128:(tt + 1) * 128],
            pd.rearrange("p (c w) -> p c w", c=8)[:, 0:4, :]),
            deps=[tr_last], mark="S")
        t_xvl = S.op("scalar", lambda e, tt=tt, pd=pd: e.copy(
            slotA.rearrange("p (c w) -> p c w", c=8)[:, 4:8, tt * 128:(tt + 1) * 128],
            pd.rearrange("p (c w) -> p c w", c=8)[:, 4:8, :]),
            deps=[tr_last], mark="S")
        xTile_tok[tt] = t_xvl
        xps_free[tt % 2] = [t_xvu, t_xvl]
    t_xT = xTile_tok[NT - 1]

    qk_free = [None, None]
    pairW_free = [None, None]
    pc_free = [xps_free[1][0], xps_free[1][1]]
    sb_free = [None, None]
    last_pv_read = {}
    yb_free = [t_pd_free, t_pd_free]
    yT_tok = {}
    pair_attn_done = [None] * 8
    pair_qkv_done = [None] * 8
    qkv_evac_tok = {}
    pair_q_toks = {}
    v_dead = [None]

    def emit_pair_quant(p):
        pb = p % 2
        specs = []
        for half, obase in ((0, p * 128), (1, C + p * 128)):
            for ci in range(2):
                specs.append((watT_r[:, ci * 4:(ci + 1) * 4, obase:obase + 128],
                              pairW[:, pb * 2048 + half * 1024 + ci * 512:
                                    pb * 2048 + half * 1024 + (ci + 1) * 512],
                              THA, [pairW_free[pb]], 2 * half + ci))
        sink = {}
        pair_q_toks[p] = sink
        if p < 2:
            for (sp, dp, th, war, key) in specs:
                sink[key] = quant(sp, dp, th, dst_war=war)
        else:
            push_quant_chunks(specs, sink)

    def emit_pair_qkv(p):
        pb = p % 2
        early = p < 2
        q_toks = [pair_q_toks[p][k] for k in range(4)]
        for half in range(2):
            for st in range(4):
                bk = (half * 4 + st) % 2
                bank = (p_c[:, bk * 512: bk * 512 + 512] if early
                        else sbanks[bk][:, 0:512])
                deps = ([pc_free[bk] if early else sb_free[bk]]
                        + [xTile_tok[4 * st + i] for i in range(4)])
                mmtok = None
                for cc in range(CCn):
                    mmtok = S.op("tensor", lambda e, cc=cc, st=st, half=half, pb=pb, bank=bank: e.matmul(
                        bank,
                        pairW[:, pb * 2048 + half * 1024 + cc * 128:
                              pb * 2048 + half * 1024 + (cc + 1) * 128],
                        xTv[:, cc, st * 512:(st + 1) * 512],
                        start=(cc == 0), stop=(cc == CCn - 1)),
                        deps=(deps if cc == 0 else []) + [q_toks[2 * half + cc // 4]],
                        mark="T" if cc == CCn - 1 else None)
                tev = S.op("vector", lambda e, pb=pb, half=half, st=st, bank=bank: e.tensor_copy(
                    qkb[:, pb * 4096 + half * 2048 + st * 512:
                        pb * 4096 + half * 2048 + (st + 1) * 512],
                    bank),
                    deps=[mmtok, qk_free[pb]], mark="V")
                qkv_evac_tok[(p, half, st)] = tev
                if early:
                    pc_free[bk] = tev
                else:
                    sb_free[bk] = tev
        pair_qkv_done[p] = tev
        pairW_free[pb] = tev

    v_mm_last = None
    for og in range(2):
        vW = yT[:, 8192 + og * 4096: 8192 + (og + 1) * 4096]
        vq_toks = []
        for cc in range(CCn):
            tc = quant(watT_r[:, cc, 2048 + og * 512: 2048 + (og + 1) * 512],
                       vW[:, cc * 512:(cc + 1) * 512], THA)
            vq_toks.append(tc)
        for tt in range(NT):
            bk = (og * NT + tt) % 4
            mmtok = None
            for cc in range(CCn):
                mmtok = S.op("tensor", lambda e, cc=cc, bk=bk, tt=tt, vW=vW: e.matmul(
                    vbank_ap[bk], xTv[:, cc, tt * 128:(tt + 1) * 128],
                    vW[:, cc * 512:(cc + 1) * 512],
                    start=(cc == 0), stop=(cc == CCn - 1)),
                    deps=([vbank_free[bk], xTile_tok[tt]] if cc == 0 else []) + [vq_toks[cc]],
                    mark="T" if cc == CCn - 1 else None)
            tev = S.op("scalar", lambda e, bk=bk, tt=tt, og=og: e.copy(
                Vv[:, tt, og * 8:(og + 1) * 8, 0:64],
                vbank_ap[bk].rearrange("p (h d) -> p h d", h=8)),
                deps=[mmtok, t_vone], mark="S")
            vbank_free[bk] = tev
            v_mm_last = mmtok
    v_dead[0] = v_mm_last
    sb_free[:] = [vbank_free[1], vbank_free[3]]

    emit_pair_quant(0)
    emit_pair_quant(1)
    emit_pair_qkv(0)
    emit_pair_qkv(1)
    yb_free[0] = yb_free[1] = pair_qkv_done[1]

    t_dbg_va = None
    if DEBUG:
        t_dbg_va = S.op("sync", lambda e: e.dma_start(d_dbg_va, V_aug),
                        deps=vbank_free + [v_mm_last, t_vone], mark="qD2", inc=16)

    pend_tr = [None]
    pend_r = [None]
    pend_pv = []

    def drain_tr():
        if pend_tr[0] is not None:
            pend_tr[0]()
            pend_tr[0] = None

    def drain_r():
        if pend_r[0] is not None:
            pend_tr[0] = pend_r[0]()
            pend_r[0] = None

    def flush_pv():
        if not pend_pv:
            return
        kb, ptk, t_exp, jmin, qq, yb, tlpv, p = pend_pv.pop(0)
        t_pv = None
        for h01 in range(2):
            for j in range(jmin, 4):
                qt = 4 * qq + j
                st = (kb == 0 and j == 0)
                sp = (kb == 4 * qq + 3 and j == 3)
                last_for_tile = (kb == qt)
                last_of_flush = (h01 == 1 and j == 3)
                t_pv = S.op("tensor", lambda e, h01=h01, j=j, kb=kb, ptk=ptk, yb=yb, p=p, st=st, sp=sp: e.matmul(
                    yb[:, h01 * 512 + j * 128: h01 * 512 + j * 128 + 65],
                    ptb[:, ptk * 1024 + h01 * 512 + j * 128:
                        ptk * 1024 + h01 * 512 + (j + 1) * 128],
                    Vv[:, kb, 2 * p + h01, :],
                    start=st, stop=sp),
                    deps=[t_exp, t_vone, t_pd_free] if (kb == 0 and j == jmin) else [t_exp],
                    mark="T" if (last_for_tile or last_of_flush) else None)
                if last_for_tile:
                    tlpv[(h01, j)] = t_pv
        last_pv_read[ptk] = t_pv

    def emit_pair_attention(p):
        pb = p % 2
        Qh = qkb[:, pb * 4096: pb * 4096 + 2048]
        Kh = qkb[:, pb * 4096 + 2048: pb * 4096 + 4096]
        qdeps = [qkv_evac_tok[(p, 0, st)] for st in range(4)] + \
                [qkv_evac_tok[(p, 1, st)] for st in range(4)]
        last_yT_evac = []
        t_exp = None

        for qq in range(4):
            if p == 6 and qq == 3:
                prep_kprojT()
            ybk = qq % 2
            yb = ybanks[ybk]
            tile_last_pv = {}

            for kb in range(4 * qq + 4):
                sbk = kb % 2
                j0 = kb - 4 * qq
                off = max(0, j0 * 128)
                diag = j0 >= 0
                sdeps = qdeps + [sb_free[sbk], yb_free[ybk] if kb == 0 else None]
                S.op("tensor", lambda e, sbk=sbk, off=off, kb=kb, qq=qq, diag=diag: e.matmul(
                    sbanks[sbk][:, off:512],
                    Kh[0:64, kb * 128:(kb + 1) * 128],
                    Qh[0:64, qq * 512 + off:(qq + 1) * 512],
                    start=True, stop=not diag, tile_position=(0, 0)),
                    deps=sdeps, mark=None)
                t_sb = S.op("tensor", lambda e, sbk=sbk, off=off, kb=kb, qq=qq, diag=diag: e.matmul(
                    sbanks[sbk][:, 512 + off:1024],
                    Kh[64:128, kb * 128:(kb + 1) * 128],
                    Qh[64:128, qq * 512 + off:(qq + 1) * 512],
                    start=True, stop=not diag, tile_position=(64, 0)),
                    deps=[], mark=None if diag else "T")
                if diag:
                    S.op("tensor", lambda e, sbk=sbk, off=off: e.matmul(
                        sbanks[sbk][:, off:off + 128], ident, tril,
                        start=False, stop=True),
                        deps=[t_ident, t_tril], mark=None)
                    t_sb = S.op("tensor", lambda e, sbk=sbk, off=off: e.matmul(
                        sbanks[sbk][:, 512 + off:512 + off + 128], ident, tril,
                        start=False, stop=True),
                        deps=[], mark="T")
                if kb == 2:
                    drain_r()
                if kb == 3:
                    drain_tr()
                if len(pend_pv) >= 2:
                    flush_pv()
                ptk = kb % 2
                t_exp = S.op("scalar", lambda e, sbk=sbk, off=off, ptk=ptk: e.activation(
                    ptb.rearrange("p (b h w) -> p (b h) w", b=2, h=2)[:, 2 * ptk:2 * ptk + 2, off:512],
                    sbanks[sbk].rearrange("p (h w) -> p h w", h=2)[:, :, off:512],
                    AF.Exp, scale=col(ALPH)),
                    deps=[t_sb, t_alph, last_pv_read.get(ptk)], mark="S")
                sb_free[sbk] = t_exp
                pend_pv.append((kb, ptk, t_exp, max(0, j0), qq, yb, tile_last_pv, p))
                pump_quant(2)

            def stage_r(yb=yb, qq=qq, ybk=ybk, tile_last_pv=tile_last_pv):
                ev_toks = []
                rc_t = {}
                for h01 in range(2):
                    rc_t[h01] = S.op("vector", lambda e, yb=yb, h01=h01: e.reciprocal(
                        colv[:, RC0 + h01 * 4: RC0 + h01 * 4 + 4].unsqueeze(2),
                        yb[:, h01 * 512: h01 * 512 + 512].rearrange(
                            "p (j w) -> p j w", j=4)[:, :, 64:65]),
                        deps=[tile_last_pv[(h01, 3)]], mark="V")
                ypv = ypairs.rearrange("p (j w) -> p j w", j=4)
                t_na = S.op("vector", lambda e, yb=yb, ypv=ypv: e.tensor_tensor(
                    ypv[:, :, 0:64],
                    yb[:, 0:512].rearrange("p (j w) -> p j w", j=4)[:, :, 0:64],
                    colv[:, RC0:RC0 + 4].unsqueeze(2).broadcast_to([128, 4, 64]),
                    ALU.mult),
                    deps=[rc_t[0]], mark="V")
                t_nb = S.op("vector", lambda e, yb=yb, ypv=ypv: e.tensor_tensor(
                    ypv[:, :, 64:128],
                    yb[:, 512:1024].rearrange("p (j w) -> p j w", j=4)[:, :, 0:64],
                    colv[:, RC0 + 4:RC0 + 8].unsqueeze(2).broadcast_to([128, 4, 64]),
                    ALU.mult),
                    deps=[rc_t[1]], mark="V")
                ev_toks.append(t_na)
                ev_toks.append(t_nb)

                def do_tr(yb=yb, qq=qq, ev_toks=ev_toks, ybk=ybk):
                    tr_last = None
                    for j in range(4):
                        tr_last = S.op("tensor", lambda e, yb=yb, j=j: e.matmul(
                            yb[:, j * 128:(j + 1) * 128],
                            ypairs[:, j * 128:(j + 1) * 128], ident,
                            start=True, stop=True),
                            deps=ev_toks + [t_ident], mark="T")
                    t_ye = S.op("vector", lambda e, yb=yb, p=p, qq=qq: e.tensor_copy(
                        yT[:, p * 2048 + qq * 512: p * 2048 + (qq + 1) * 512],
                        yb[:, 0:512]),
                        deps=[tr_last, v_dead[0]], mark="V")
                    for j in range(4):
                        yT_tok[(p, 4 * qq + j)] = t_ye
                    last_yT_evac[:] = [t_ye]
                    yb_free[ybk] = S.op("vector", lambda e: e.engine_nop(),
                                        deps=[t_ye] + ev_toks, mark="V")
                return do_tr

            if qq == 0 and p + 2 < 8:
                emit_pair_quant(p + 2)
            if qq == 3 and p == 7:
                while qv_steps:
                    qv_steps.pop(0)()
                while pend_pv:
                    flush_pv()
                drain_r()
                drain_tr()
                do_tr_f = stage_r()
                do_tr_f()
            else:
                pend_r[0] = stage_r
        done = S.op("vector", lambda e: e.engine_nop(),
                    deps=list(last_yT_evac) + [t_exp], mark="V")
        pair_attn_done[p] = done
        qk_free[pb] = t_exp

    kprojT = slotA
    kp_toks = {}

    def prep_kprojT():
        xT_dead = pair_qkv_done[7]
        specs = []
        for cc in range(CCn):
            for i in range(2):
                specs.append((wprT_r[:, cc, i * 512:(i + 1) * 512],
                              kprojT[:, cc * 1024 + i * 512: cc * 1024 + (i + 1) * 512],
                              THP, [xT_dead], (cc, i)))
        push_quant_chunks(specs, kp_toks)

    for p in range(8):
        if 2 <= p + 1 < 8:
            emit_pair_qkv(p + 1)
        emit_pair_attention(p)
        if DEBUG and p in (0, 7):
            t_dbg_qk = S.op("sync", lambda e, p=p: e.dma_start(
                d_dbg_qk if p == 0 else d_dbg_qk7, qkb[:, (p % 2) * 4096:(p % 2) * 4096 + 4096]),
                deps=[qkv_evac_tok[(p, h, st)] for h in range(2) for st in range(4)],
                mark="qD%d" % (3 + p // 7), inc=16)
            qk_free[p % 2] = S.op("vector", lambda e: e.engine_nop(),
                                  deps=[t_dbg_qk, qk_free[p % 2]], mark="V")

    t_dbg_yT = None
    if DEBUG:
        t_dbg_yT = S.op("sync", lambda e: e.dma_start(d_dbg_yT, yT),
                        deps=[yT_tok[(cc, tt)] for cc in range(CCn)
                              for tt in range(NT)], mark="qD5", inc=16)


    x2T_tok = {}
    ln1_tok = [None] * NT
    ln_stat_tok = []
    mrb_toks = []
    proj_mm_last = None
    pj_free = [sb_free[0], sb_free[1]]
    ln_col_tok = [None, None]
    scr_tok = [None]
    x2t_ps_free = [t_pd_free]

    for tt in range(NT):
        pbk = tt % 2
        pj = sbanks[pbk]
        ydeps = [yT_tok[(cc, tt)] for cc in range(CCn)]
        mmtok = None
        for ch in range(2):
            for cc in range(CCn):
                first = (ch == 0 and cc == 0)
                mmtok = S.op("tensor", lambda e, ch=ch, cc=cc, tt=tt, pj=pj: e.matmul(
                    pj[:, ch * 512:(ch + 1) * 512],
                    yT[:, cc * 2048 + tt * 128: cc * 2048 + (tt + 1) * 128],
                    kprojT[:, cc * 1024 + ch * 512: cc * 1024 + (ch + 1) * 512],
                    start=(cc == 0), stop=(cc == CCn - 1)),
                    deps=(ydeps + [pj_free[pbk]] if first else []) + [kp_toks[(cc, ch)]],
                    mark="T" if cc == CCn - 1 else None)
        proj_mm_last = mmtok
        t_e0 = S.op("vector", lambda e, pj=pj, tt=tt: e.scalar_tensor_tensor(
            res[:, tt * 1024: tt * 1024 + 512], pj[:, 0:512], col(S2C),
            res[:, tt * 1024: tt * 1024 + 512], ALU.mult, ALU.add,
            accum_out=LNC[:, tt:tt + 1]),
            deps=[mmtok, t_s2, t_xnat], mark="V")
        t_e1 = S.op("vector", lambda e, pj=pj, tt=tt: e.scalar_tensor_tensor(
            res[:, tt * 1024 + 512: tt * 1024 + 1024], pj[:, 512:1024], col(S2C),
            res[:, tt * 1024 + 512: tt * 1024 + 1024], ALU.mult, ALU.add,
            accum_out=LNC[:, 16 + tt:17 + tt]),
            deps=[mmtok, t_s2, t_xnat], mark="V")
        pj_free[pbk] = S.op("vector", lambda e: e.engine_nop(), deps=[t_e0, t_e1], mark="V")

        rt = res[:, tt * 1024:(tt + 1) * 1024]
        t_sq = S.op("scalar", lambda e, rt=rt, tt=tt: e.activation(
            scr1024, rt, AF.Square, accum_out=LNC[:, 32 + tt:33 + tt]),
            deps=[t_e0, t_e1, scr_tok[0]], mark="S")
        scr_tok[0] = t_sq
        ln_stat_tok.append(t_sq)
        ln_stat_tok.append(t_e0)
        ln_stat_tok.append(t_e1)
        if tt in (7, 15):
            lo, hi = (0, 8) if tt == 7 else (8, 16)
            t_hs = S.op("vector", lambda e, lo=lo, hi=hi: e.tensor_add(
                LNC[:, lo:hi], LNC[:, lo:hi], LNC[:, 16 + lo:16 + hi]),
                deps=list(ln_stat_tok), mark="V")
            t_bm = S.op("vector", lambda e, lo=lo, hi=hi: e.tensor_scalar_mul(
                LNC[:, lo:hi], LNC[:, lo:hi], 1.0 / C), deps=[t_hs], mark="V")
            t_bs = S.op("vector", lambda e, lo=lo, hi=hi: e.tensor_scalar_mul(
                LNC[:, 16 + lo:16 + hi], LNC[:, 32 + lo:32 + hi], 1.0 / C),
                deps=[t_hs], mark="V")
            t_m2 = S.op("vector", lambda e, lo=lo, hi=hi: e.tensor_mul(
                LNC[:, 32 + lo:32 + hi], LNC[:, lo:hi], LNC[:, lo:hi]),
                deps=[t_bm, t_bs], mark="V")
            t_var = S.op("vector", lambda e, lo=lo, hi=hi: e.tensor_sub(
                LNC[:, 16 + lo:16 + hi], LNC[:, 16 + lo:16 + hi],
                LNC[:, 32 + lo:32 + hi]), deps=[t_bs, t_m2], mark="V")
            t_ln = S.op("scalar", lambda e, lo=lo, hi=hi: e.activation(
                LNC[:, 32 + lo:32 + hi], LNC[:, 16 + lo:16 + hi], AF.Ln,
                bias=col(EPSC)), deps=[t_var, t_eps], mark="S")
            t_rs = S.op("scalar", lambda e, lo=lo, hi=hi: e.activation(
                LNC[:, 32 + lo:32 + hi], LNC[:, 32 + lo:32 + hi], AF.Exp,
                scale=col(NHLF)), deps=[t_ln, t_nh], mark="S")
            t_mrb = S.op("vector", lambda e, lo=lo, hi=hi: e.tensor_mul(
                LNC[:, 48 + lo:48 + hi], LNC[:, lo:hi], LNC[:, 32 + lo:32 + hi]),
                deps=[t_rs, t_bm], mark="V")
            mrb_toks.append(t_mrb)
            ln_stat_tok.clear()
        if tt >= 8:
            t2 = tt - 8
            ln1_tok[t2] = S.op("vector", lambda e, t2=t2: e.tensor_scalar(
                res[:, t2 * 1024:(t2 + 1) * 1024],
                res[:, t2 * 1024:(t2 + 1) * 1024],
                LNC[:, 32 + t2:33 + t2], LNC[:, 48 + t2:49 + t2],
                ALU.mult, ALU.subtract),
                deps=[mrb_toks[0]], mark="V")


    proj_mm_final = proj_mm_last

    def x2T_ap(cc, tt128, wid):
        if cc < 4:
            return slotA[:, 8192 + cc * 2048 + tt128: 8192 + cc * 2048 + tt128 + wid]
        return slotA[:, (cc - 4) * 2048 + tt128: (cc - 4) * 2048 + tt128 + wid]

    x2t_ps = [p_d, p_c]
    x2t_free = [x2t_ps_free[0], x2t_ps_free[0]]
    gmm_stop = {}
    gmm_started = set()

    def emit_gmm(tt):
        bk = tt // 8
        toff = bk * 512 + (tt % 8) * 64
        xdeps = [x2T_tok[(0, tt)], x2T_tok[(4, tt)], t_gwT, t_ATs,
                 pj_free[0], pj_free[1]]
        bank_first = bk not in gmm_started
        gmm_started.add(bk)
        for cc in range(CCn):
            S.op("tensor", lambda e, cc=cc, tt=tt, toff=toff: e.matmul(
                p_a[:, toff: toff + 8], x2T_ap(cc, tt * 128, 128),
                gwT.rearrange("p (c n) -> p c n", c=CCn)[:, cc, :],
                start=(bank_first and cc == 0), stop=False),
                deps=xdeps if cc == 0 else [], mark=None)
        bank_last = (tt % 8) == 7
        for cc in range(CCn):
            tok = S.op("tensor", lambda e, cc=cc, tt=tt, toff=toff: e.matmul(
                p_a[:, toff + 8: toff + 40], x2T_ap(cc, tt * 128, 128),
                ATs.rearrange("p (c n) -> p c n", c=CCn)[:, cc, :],
                start=False, stop=(bank_last and cc == CCn - 1)),
                deps=[], mark="T" if (bank_last and cc == CCn - 1) else None)
        if bank_last:
            gmm_stop[bk] = tok

    for tt in range(NT):
        pd = x2t_ps[tt % 2]
        tr_deps = [ln1_tok[tt], t_idf, x2t_free[tt % 2], t_pd_free, proj_mm_final]
        tr_last = None
        for cc in range(CCn):
            tr_last = S.op("tensor", lambda e, cc=cc, tt=tt, pd=pd: e.matmul(
                pd[:, (cc % 8) * 128:((cc % 8) + 1) * 128],
                res[:, tt * 1024 + cc * 128: tt * 1024 + (cc + 1) * 128], identf,
                start=True, stop=True),
                deps=tr_deps, mark="T")
        t_evu = S.op("scalar", lambda e, tt=tt, pd=pd: e.copy(
            slotA.rearrange("p (c w) -> p c w", c=8)[:, 4:8, tt * 128:(tt + 1) * 128],
            pd.rearrange("p (c w) -> p c w", c=8)[:, 0:4, :]),
            deps=[tr_last], mark="S")
        t_evl = S.op("vector", lambda e, tt=tt, pd=pd: e.tensor_copy(
            slotA.rearrange("p (c w) -> p c w", c=8)[:, 0:4, tt * 128:(tt + 1) * 128],
            pd.rearrange("p (c w) -> p c w", c=8)[:, 4:8, :]),
            deps=[tr_last], mark="V")
        for cc in range(CCn):
            x2T_tok[(cc, tt)] = t_evl if cc < 4 else t_evu
        x2t_free[tt % 2] = S.op("vector", lambda e: e.engine_nop(),
                                deps=[t_evu, t_evl], mark="V")
        if tt < 8:
            t2 = tt + 8
            ln1_tok[t2] = S.op("vector", lambda e, t2=t2: e.tensor_scalar(
                res[:, t2 * 1024:(t2 + 1) * 1024],
                res[:, t2 * 1024:(t2 + 1) * 1024],
                LNC[:, 32 + t2:33 + t2], LNC[:, 48 + t2:49 + t2],
                ALU.mult, ALU.subtract),
                deps=[mrb_toks[1]], mark="V")
        if tt > 0:
            emit_gmm(tt - 1)
    emit_gmm(NT - 1)

    t_dbg_res = None
    if DEBUG:
        t_dbg_res = S.op("sync", lambda e: e.dma_start(d_dbg_res, res),
                         deps=list(ln1_tok), mark="qD6", inc=16)
    t_BT = S.op("gpsimd", lambda e: e.dma_start(BTs, d_BT),
                deps=[proj_mm_last, yT_tok[(7, NT - 1)], t_dbg_yT], mark="qBT", inc=16)

    x2b_tok = {}
    for tt in range(2):
        x2b_tok[tt] = S.op("vector", lambda e, tt=tt: e.tensor_add(
            res[:, tt * 1024:(tt + 1) * 1024], res[:, tt * 1024:(tt + 1) * 1024], wob),
            deps=[ln1_tok[tt], scr_tok[0], t_wob,
                  x2T_tok[(7, tt)], x2T_tok[(0, tt)], t_dbg_res], mark="V")

    q_dead = q_chain[0]
    Lps = p_a[:, 0:1024].rearrange("p (t x) -> p t x", t=16)
    Lsc = qscr[:, 0:128].rearrange("p (t e) -> p t e", t=16)
    Msc = qscr[:, 128:256].rearrange("p (t e) -> p t e", t=16)
    Vsc = qscr[:, 256:384].rearrange("p (t e) -> p t e", t=16)
    Ksc = qscr[:, 384:512].rearrange("p (t e) -> p t e", t=16)
    t_lg = S.op("vector", lambda e: e.tensor_tensor(
        Lsc, Lps[:, :, 0:8],
        gbuf.unsqueeze(1).broadcast_to([128, 16, 8]), ALU.add),
        deps=[gmm_stop[0], gmm_stop[1], t_gb, q_dead], mark="V")
    t_gm = S.op("vector", lambda e: e.reduce_max(
        LNC[:, 0:16].unsqueeze(2), Lsc, AX.X), deps=[t_lg] + mrb_toks, mark="V")
    t_im = S.op("vector", lambda e: e.tensor_tensor(
        Msc, Lsc, LNC[:, 0:16].unsqueeze(2).broadcast_to([128, 16, 8]),
        ALU.is_ge), deps=[t_gm], mark="V")
    t_v8 = S.op("vector", lambda e: e.scalar_tensor_tensor(
        Vsc, Msc, -BIGI, iotapb.unsqueeze(1).broadcast_to([128, 16, 8]),
        ALU.mult, ALU.add), deps=[t_im, t_iot], mark="V")
    t_ix = S.op("vector", lambda e: e.tensor_reduce(
        LNC[:, 16:32].unsqueeze(2), Vsc, AX.X, ALU.min), deps=[t_v8], mark="V")
    t_kp = S.op("vector", lambda e: e.tensor_tensor(
        Ksc, Vsc, LNC[:, 16:32].unsqueeze(2).broadcast_to([128, 16, 8]),
        ALU.is_le), deps=[t_ix], mark="V")
    rmall = scr1024[:, 0:512]
    t_me = S.op("vector", lambda e: e.tensor_tensor(
        rmall.rearrange("p (t e r) -> p t e r", t=16, e=8),
        Lps[:, :, 8:40].rearrange("p t (e r) -> p t e r", e=8),
        Ksc.unsqueeze(3).broadcast_to([128, 16, 8, 4]), ALU.mult),
        deps=[t_kp, scr_tok[0]], mark="V")

    rmT_tok = {}
    rm_state = {"re": None}
    ffn_bank_free = {0: t_me, 1: pj_free[1], 2: None}

    def emit_rm_round(r):
        tr_t = None
        for k in range(8):
            tt = r * 8 + k
            tr_t = S.op("tensor", lambda e, tt=tt, k=k: e.matmul(
                p_d[0:32, k * 128:(k + 1) * 128],
                rmall[:, tt * 32:(tt + 1) * 32], ident,
                start=True, stop=True),
                deps=([t_me, t_ident, x2t_free[0], rm_state["re"]] if k == 0 else []),
                mark="T" if k == 7 else None)
        re_tok = S.op("scalar", lambda e, r=r: e.copy(
            rmT[:, r * 1024:(r + 1) * 1024], p_d[0:32, 0:1024]),
            deps=[tr_t], mark="S")
        for k in range(8):
            rmT_tok[r * 8 + k] = re_tok
        rm_state["re"] = re_tok
        ffn_bank_free[2] = re_tok

    emit_rm_round(0)
    emit_rm_round(1)

    for tt in range(2, NT):
        t0 = S.op("vector", lambda e, tt=tt: e.tensor_add(
            res[:, tt * 1024:(tt + 1) * 1024], res[:, tt * 1024:(tt + 1) * 1024], wob),
            deps=[ln1_tok[tt], scr_tok[0], t_wob,
                  x2T_tok[(7, tt)], x2T_tok[(0, tt)], t_dbg_res], mark="V")
        x2b_tok[tt] = t0

    wiT_r = d_wiT.rearrange("(c p) f -> p c f", p=128)
    woT_r = d_woT.rearrange("(f p) c -> p f c", p=128)
    wdb = [wdb0, wdb1]
    wdb_tok = [None, None]
    wdb_free = [[pair_attn_done[7], t_dbg_va],
                [pair_attn_done[7], q_chain[0]]]

    def load_group(g):
        gb_ = g % 2
        t_wi = S.op("gpsimd", lambda e, g=g, gb_=gb_: e.dma_start(
            wdb[gb_][:, 0:8192].rearrange("p (c f) -> p c f", c=CCn),
            wiT_r[:, :, g * 1024:(g + 1) * 1024]),
            deps=wdb_free[gb_], mark="qF%d" % gb_, inc=16)
        t_wo = S.op("gpsimd", lambda e, g=g, gb_=gb_: e.dma_start(
            wdb[gb_][:, 8192:16384].rearrange("p (f c) -> p f c", f=CCn),
            woT_r[:, g * CCn:(g + 1) * CCn, :]),
            deps=[], mark="qF%d" % gb_, inc=16)
        wdb_tok[gb_] = [t_wo]

    load_group(0)
    load_group(1)

    it_ps_free = [x2t_free[1], x2t_free[1]]
    it_sb_tok = [t_dbg_yT, t_dbg_yT, t_dbg_yT]
    fbanks = [p_a, p_b, p_d]
    add_tok = {tt: None for tt in range(NT)}
    ln2_done = []
    out_dma_toks = []

    pend_ln2 = []

    pend_ln2b = []

    def flush_ln2_stats():
        if not pend_ln2:
            return
        tt, par, atok = pend_ln2.pop(0)
        B0 = LSUM if par == 0 else 43
        GMx = GM if par == 0 else GM2
        rt = res[:, tt * 1024:(tt + 1) * 1024]
        t_hs = S.op("vector", lambda e, B0=B0, GMx=GMx: e.tensor_add(
            col(B0), col(B0), col(GMx)), deps=[atok, ln_col_tok[par]], mark="V")
        t_sq = S.op("vector", lambda e, rt=rt, B0=B0: e.scalar_tensor_tensor(
            scr1024, rt, 1.0, rt, ALU.mult, ALU.mult, accum_out=col(B0 + 1)),
            deps=[atok, scr_tok[0]], mark="V")
        scr_tok[0] = t_sq
        t_m = S.op("vector", lambda e, B0=B0: e.tensor_scalar_mul(
            col(B0), col(B0), 1.0 / C), deps=[t_hs], mark="V")
        t_v1 = S.op("vector", lambda e, B0=B0: e.tensor_scalar_mul(
            col(B0 + 1), col(B0 + 1), 1.0 / C), deps=[t_sq], mark="V")
        t_m2 = S.op("vector", lambda e, B0=B0: e.tensor_mul(
            col(B0 + 2), col(B0), col(B0)), deps=[t_m], mark="V")
        t_var = S.op("vector", lambda e, B0=B0: e.tensor_sub(
            col(B0 + 2), col(B0 + 1), col(B0 + 2)), deps=[t_v1, t_m2], mark="V")
        t_ln = S.op("scalar", lambda e, B0=B0: e.activation(
            col(B0 + 3), col(B0 + 2), AF.Ln, bias=col(EPSC)),
            deps=[t_var], mark="S")
        t_rs = S.op("scalar", lambda e, B0=B0: e.activation(
            col(B0 + 3), col(B0 + 3), AF.Exp, scale=col(NHLF)),
            deps=[t_ln], mark="S")
        t_mr = S.op("vector", lambda e, B0=B0: e.tensor_mul(
            col(B0 + 4), col(B0), col(B0 + 3)), deps=[t_rs, t_m], mark="V")
        pend_ln2b.append((rt, B0, par, t_mr, t_sq, tt))

    def flush_ln2_store():
        while pend_ln2b:
            rt, B0, par, t_mr, t_sq, tt = pend_ln2b.pop(0)
            t_nrm = S.op("vector", lambda e, rt=rt, B0=B0: e.tensor_scalar(
                rt, rt, col(B0 + 3), col(B0 + 4), ALU.mult, ALU.subtract),
                deps=[t_mr, t_sq], mark="V")
            ln_col_tok[par] = t_nrm
            t_st = S.op("sync", lambda e, rt=rt, tt=tt: e.dma_start(
                d_out.rearrange("(t p) c -> p t c", p=128)[:, tt, :], rt),
                deps=[t_nrm], mark="qO", inc=16)
            out_dma_toks.append(t_st)

    for g in range(4):
        gb_ = g % 2
        gdeps = wdb_tok[gb_]
        for tb in range(8):
            wo_last = {}
            pend_wo = []

            def flush_wo(min_keep, tb=tb, gb_=gb_, wo_last=wo_last):
                while len(pend_wo) > min_keep:
                    fc, ib3, t_rl = pend_wo.pop(0)
                    slots = [(2 * tb + tt2) % 3 for tt2 in range(2)]
                    wdeps = [t_rl] + ([ffn_bank_free[slots[0]], ffn_bank_free[slots[1]]]
                                      if fc == 0 else [])
                    for tt2 in range(2):
                        for ch in range(2):
                            wo_last[(tt2, ch)] = S.op(
                                "tensor",
                                lambda e, tt2=tt2, ch=ch, fc=fc, ib3=ib3, gb_=gb_, sl=slots[tt2]: e.matmul(
                                    fbanks[sl][:, ch * 512:(ch + 1) * 512],
                                    it_sb[:, ib3 * 256 + tt2 * 128: ib3 * 256 + (tt2 + 1) * 128],
                                    wdb[gb_][:, 8192 + fc * 1024 + ch * 512:
                                             8192 + fc * 1024 + (ch + 1) * 512],
                                    start=(fc == 0), stop=(fc == CCn - 1)),
                                deps=wdeps, mark="T" if fc == CCn - 1 else None)
                    it_sb_tok[ib3] = wo_last[(1, 1)]

            for fc in range(CCn):
                gf = g * 8 + fc
                ib = fc % 2
                ib3 = fc % 3
                ips = p_c[:, ib * 512: ib * 512 + 256]
                first_deps = gdeps + [it_ps_free[ib],
                                      x2b_tok[2 * tb], x2b_tok[2 * tb + 1]]
                mmtok = None
                for cc in range(CCn):
                    mmtok = S.op("tensor", lambda e, cc=cc, fc=fc, tb=tb, gb_=gb_, ips=ips: e.matmul(
                        ips, wdb[gb_][:, cc * 1024 + fc * 128: cc * 1024 + (fc + 1) * 128],
                        x2T_ap(cc, tb * 256, 256),
                        start=(cc == 0), stop=False),
                        deps=first_deps if cc == 0 else [], mark=None)
                mmtok = S.op("tensor", lambda e, gf=gf, tb=tb, ips=ips: e.matmul(
                    ips, BTs[:, gf * 128:(gf + 1) * 128],
                    rmT[:, tb * 256:(tb + 1) * 256],
                    start=False, stop=True),
                    deps=[t_BT, rmT_tok[2 * tb], rmT_tok[2 * tb + 1]], mark="T")
                flush_wo(1)
                if fc in (3, 5):
                    flush_ln2_stats()
                if fc == 7:
                    flush_ln2_store()
                t_rl = S.op("scalar", lambda e, ib3=ib3, gf=gf, ips=ips: e.activation(
                    it_sb[:, ib3 * 256:(ib3 + 1) * 256], ips, AF.Relu,
                    bias=wib[:, gf:gf + 1]),
                    deps=[mmtok, t_wib, it_sb_tok[ib3]], mark="S")
                it_ps_free[ib] = t_rl
                pend_wo.append((fc, ib3, t_rl))
            flush_wo(0)
            for tt2 in range(2):
                tt = 2 * tb + tt2
                sl = (2 * tb + tt2) % 3
                B0 = LSUM if tt2 == 0 else 43
                GMx = GM if tt2 == 0 else GM2
                t_a0 = S.op("vector", lambda e, tt=tt, sl=sl, B0=B0: e.scalar_tensor_tensor(
                    res[:, tt * 1024: tt * 1024 + 512],
                    res[:, tt * 1024: tt * 1024 + 512], 1.0, fbanks[sl][:, 0:512],
                    ALU.mult, ALU.add, accum_out=col(B0)),
                    deps=[wo_last[(tt2, 0)], x2b_tok[tt], add_tok[tt]], mark="V")
                t_a1 = S.op("vector", lambda e, tt=tt, sl=sl, GMx=GMx: e.scalar_tensor_tensor(
                    res[:, tt * 1024 + 512: tt * 1024 + 1024],
                    res[:, tt * 1024 + 512: tt * 1024 + 1024], 1.0, fbanks[sl][:, 512:1024],
                    ALU.mult, ALU.add, accum_out=col(GMx)),
                    deps=[wo_last[(tt2, 1)], add_tok[tt]], mark="V")
                add_tok[tt] = S.op("vector", lambda e: e.engine_nop(),
                                   deps=[t_a0, t_a1], mark="V")
                ffn_bank_free[sl] = add_tok[tt]
            if g == 3:
                pend_ln2.append((2 * tb, 0, add_tok[2 * tb]))
                pend_ln2.append((2 * tb + 1, 1, add_tok[2 * tb + 1]))
        wdb_free[gb_] = [wo_last[(1, 1)]]
        if g + 2 < 4:
            load_group(g + 2)
    while pend_ln2 or pend_ln2b:
        flush_ln2_stats()
        flush_ln2_stats()
        flush_ln2_store()

    S.barrier_wait("sync", out_dma_toks)

    with nc.Block() as block:
        @block.sync
        def _(eng):
            S.replay("sync", eng)

        @block.tensor
        def _(eng):
            S.replay("tensor", eng)

        @block.vector
        def _(eng):
            S.replay("vector", eng)

        @block.scalar
        def _(eng):
            S.replay("scalar", eng)

        @block.gpsimd
        def _(eng):
            S.replay("gpsimd", eng)

    return nc


_NC_CACHE = None


def kernel(L0_attn_x, L0_attn_w_attn, L0_attn_w_proj, L0_ffn_gate_W, L0_ffn_gate_b,
           L0_ffn_wi_W, L0_ffn_wi_b, L0_ffn_wo_W, L0_ffn_wo_b, L0_ffn_lora_A,
           L0_ffn_lora_B):
    global _NC_CACHE
    x = np.asarray(L0_attn_x, np.float32)
    w_attn = np.asarray(L0_attn_w_attn, np.float32)
    w_proj = np.asarray(L0_attn_w_proj, np.float32)
    gate_W = np.asarray(L0_ffn_gate_W, np.float32)
    gate_b = np.asarray(L0_ffn_gate_b, np.float32)
    wi_W = np.asarray(L0_ffn_wi_W, np.float32)
    wi_b = np.asarray(L0_ffn_wi_b, np.float32)
    wo_W = np.asarray(L0_ffn_wo_W, np.float32)
    wo_b = np.asarray(L0_ffn_wo_b, np.float32)
    lora_A = np.asarray(L0_ffn_lora_A, np.float32)
    lora_B = np.asarray(L0_ffn_lora_B, np.float32)

    watT = np.ascontiguousarray(w_attn.T)
    wprT = np.ascontiguousarray(w_proj.T)
    gwT = np.ascontiguousarray(gate_W.T)
    gb = np.ascontiguousarray(np.tile(gate_b[None, :], (128, 1)))
    wiT = np.ascontiguousarray(wi_W.T)
    wib = np.ascontiguousarray(wi_b.reshape(32, 128).T)
    woT = np.ascontiguousarray(wo_W.T)
    wob = np.ascontiguousarray(np.tile(wo_b[None, :], (128, 1)))
    AT = np.ascontiguousarray(lora_A.reshape(32, C).T)
    BT = np.ascontiguousarray(lora_B.transpose(0, 2, 1).reshape(32, FF))
    ident = np.eye(128, dtype=ml_dtypes.bfloat16)
    identf = np.eye(128, dtype=np.float32)
    rr, cc_ = np.indices((128, 128))
    tril = np.where(rr > cc_, NEG, 0.0).astype(ml_dtypes.bfloat16)
    iotapb = np.ascontiguousarray(
        np.tile(np.arange(NE, dtype=np.float32)[None, :] + BIGI, (128, 1)))

    if _NC_CACHE is None:
        _NC_CACHE = build_program()
    nc = _NC_CACHE

    shared = {"watT": watT, "wprT": wprT, "gwT": gwT, "gb": gb, "wiT": wiT,
              "wib": wib, "woT": woT, "wob": wob, "AT": AT, "BT": BT,
              "ident": ident, "identf": identf, "tril": tril, "iotapb": iotapb}
    in_maps = []
    for b in range(8):
        m = dict(shared)
        m["x_nat"] = np.ascontiguousarray(x[b])
        in_maps.append(m)

    resobj = run_bass_kernel_spmd(nc, in_maps, core_ids=list(range(8)))
    return np.stack([r["out"] for r in resobj.results], axis=0)


if __name__ == "__main__":
    import sys
    sys.path.insert(0, "/root/problem")
    import reference as R
    d = R.setup_inputs(0)
    inp = {k: np.asarray(v) for k, v in d.items()}
    got = kernel(**inp)
    ref = np.asarray(R.reference(**d))
    num = float(np.mean((got.astype(np.float64) - ref.astype(np.float64)) ** 2))
    den = float(np.mean(ref.astype(np.float64) ** 2))
    print("ratio:", num / den)


# version 31 (final)
# speedup vs baseline: 1.2192x; 1.0112x over previous
import numpy as np
import ml_dtypes
import concourse.bass as bass
import concourse.mybir as mybir
from concourse.bass_utils import run_bass_kernel_spmd

F32 = mybir.dt.float32
BF16 = mybir.dt.bfloat16
I32 = mybir.dt.int32
U8 = mybir.dt.uint8
AF = mybir.ActivationFunctionType
ALU = mybir.AluOpType
AX = mybir.AxisListType

T = 2048
C = 1024
NH = 16
HD = 64
FF = 4096
NE = 8
LR = 4
NT = 16
CCn = 8
NEG = -1.0e30
BIGI = 1000.0

ENGS = ("sync", "tensor", "vector", "scalar", "gpsimd")
DEBUG = False
PREFETCH_QKV = True


class Sched:

    def __init__(self, nc):
        self.nc = nc
        self.sems = {}
        self.cnt = {}
        self.seen = {}
        self.ops = {e: [] for e in ENGS}

    def _sem(self, name):
        if name not in self.sems:
            self.sems[name] = self.nc.alloc_semaphore(name="sem_" + name)
            self.cnt[name] = 0
        return self.sems[name]

    def op(self, eng, build, deps=(), mark=None, inc=1):
        waits = []
        for d in deps:
            if d is None:
                continue
            s, v = d
            if self.seen.get((eng, s), 0) < v:
                waits.append((s, v))
                self.seen[(eng, s)] = v
        tok = None
        if mark is not None:
            self._sem(mark)
            self.cnt[mark] += inc
            tok = (mark, self.cnt[mark])
        self.ops[eng].append((waits, build, mark, inc))
        return tok

    def barrier_wait(self, eng, deps):
        best = {}
        for d in deps:
            if d is None:
                continue
            s, v = d
            best[s] = max(best.get(s, 0), v)
        self.op(eng, None, deps=list(best.items()), mark=None)

    def replay(self, eng_name, handle):
        for waits, build, mark, inc in self.ops[eng_name]:
            for s, v in waits:
                handle.wait_ge(self.sems[s], v)
            if build is not None:
                ins = build(handle)
                if mark is not None:
                    ins.then_inc(self.sems[mark], inc)


def build_program():
    nc = bass.Bass("TRN2", target_bir_lowering=False, debug=False)

    d_x = nc.dram_tensor("x_nat", [T, C], F32, kind="ExternalInput").ap()
    d_watT = nc.dram_tensor("watT", [C, 3 * C], F32, kind="ExternalInput").ap()
    d_wprT = nc.dram_tensor("wprT", [C, C], F32, kind="ExternalInput").ap()
    d_gwT = nc.dram_tensor("gwT", [C, NE], F32, kind="ExternalInput").ap()
    d_gb = nc.dram_tensor("gb", [128, NE], F32, kind="ExternalInput").ap()
    d_wiT = nc.dram_tensor("wiT", [C, FF], F32, kind="ExternalInput").ap()
    d_wib = nc.dram_tensor("wib", [128, 32], F32, kind="ExternalInput").ap()
    d_woT = nc.dram_tensor("woT", [FF, C], F32, kind="ExternalInput").ap()
    d_wob = nc.dram_tensor("wob", [128, C], F32, kind="ExternalInput").ap()
    d_AT = nc.dram_tensor("AT", [C, 32], F32, kind="ExternalInput").ap()
    d_BT = nc.dram_tensor("BT", [32, FF], F32, kind="ExternalInput").ap()
    d_ident = nc.dram_tensor("ident", [128, 128], BF16, kind="ExternalInput").ap()
    d_identf = nc.dram_tensor("identf", [128, 128], F32, kind="ExternalInput").ap()
    d_tril = nc.dram_tensor("tril", [128, 128], BF16, kind="ExternalInput").ap()
    d_iotapb = nc.dram_tensor("iotapb", [128, NE], F32, kind="ExternalInput").ap()
    d_out = nc.dram_tensor("out", [T, C], F32, kind="ExternalOutput").ap()
    if DEBUG:
        d_dbg_xT = nc.dram_tensor("dbg_xT", [128, 16384], BF16, kind="ExternalOutput").ap()
        d_dbg_va = nc.dram_tensor("dbg_va", [128, 16640], BF16, kind="ExternalOutput").ap()
        d_dbg_qk = nc.dram_tensor("dbg_qk", [128, 4096], BF16, kind="ExternalOutput").ap()
        d_dbg_qk7 = nc.dram_tensor("dbg_qk7", [128, 4096], BF16, kind="ExternalOutput").ap()
        d_dbg_yT = nc.dram_tensor("dbg_yT", [128, 16384], BF16, kind="ExternalOutput").ap()
        d_dbg_res = nc.dram_tensor("dbg_res", [128, 16384], F32, kind="ExternalOutput").ap()

    AR = 212160
    arena = nc.alloc_sbuf_tensor("arena", [128, AR], U8).ap()

    def vw(off, nelem, dt):
        nb = {F32: 4, BF16: 2, I32: 4}[dt] * nelem
        return arena[:, off:off + nb].bitcast(dt)

    res = vw(0, 16384, F32)
    slotA = vw(65536, 16384, BF16)
    ident = vw(98304, 128, BF16)
    tril = vw(98560, 128, BF16)
    ones128 = vw(99072, 128, F32)
    iotapb = vw(99584, NE, F32)
    gbuf = vw(99616, NE, F32)
    wib = vw(99648, 32, F32)
    gwT = vw(99776, 64, BF16)
    ATs = vw(99904, 256, BF16)
    ypairs = vw(100416, 512, BF16)
    colv = vw(101440, 64, F32)
    rm_bf = vw(101696, 32, BF16)
    gscr = vw(101760, 24, F32)
    keep8 = vw(101856, NE, F32)
    scr1024 = vw(101888, 1024, BF16)
    wob = vw(104448, 1024, F32)
    V_aug = vw(108544, 16640, BF16)
    qkb = vw(141824, 8192, BF16)
    ptb = vw(158208, 2048, BF16)
    pairW = vw(162304, 4096, BF16)
    wstg = vw(170496, 1024, F32)
    qscr = vw(174592, 1024, F32)
    yT = vw(178688, 16384, BF16)
    wdb0 = vw(108544, 16384, BF16)
    wdb1 = vw(141312, 16384, BF16)
    rmT = vw(174592, 2048, BF16)[0:32, :]
    it_sb = vw(178688, 768, BF16)
    BTs = vw(180224, 4096, BF16)[0:32, :]
    identf = vw(211456, 128, F32)
    rm_bf2 = vw(211968, 32, BF16)
    gscr2 = vw(212032, 24, F32)
    keep8b = vw(212128, NE, F32)
    LNC = vw(104192, 64, F32)

    def col(k):
        return colv[:, k:k + 1]

    ACC_A, TCOL, ACC_P, SCL_A, RCP_A, ALPH, SCL_P, RCP_P, S2C = range(9)
    EPSC, NHLF = 9, 10
    LSUM, LM, LSQ, LT1, LVAR, LRSTD, LMR = 11, 12, 13, 14, 15, 16, 17
    GM, GIDX = 18, 19
    GM2, GIDX2 = 41, 42
    RC0 = 20
    THA = 28
    THP = 34

    p_a = nc.alloc_psum_tensor("p_a", [128, 1024], F32).ap()
    p_b = nc.alloc_psum_tensor("p_b", [128, 1024], F32).ap()
    p_c = nc.alloc_psum_tensor("p_c", [128, 1024], F32).ap()
    p_d = nc.alloc_psum_tensor("p_d", [128, 1024], F32).ap()
    sbanks = [p_a, p_b]
    ybanks = [p_c, p_d]

    S = Sched(nc)

    S.op("sync", lambda e: e.dma_start(ident, d_ident), mark="qC", inc=16)
    S.op("sync", lambda e: e.dma_start(identf, d_identf), mark="qC", inc=16)
    S.op("sync", lambda e: e.dma_start(tril, d_tril), mark="qC", inc=16)
    S.op("sync", lambda e: e.dma_start(iotapb, d_iotapb), mark="qC", inc=16)
    S.op("sync", lambda e: e.dma_start(gbuf, d_gb), mark="qC", inc=16)
    S.op("sync", lambda e: e.dma_start(wib, d_wib), mark="qC", inc=16)
    t_const_all = S.op("sync", lambda e: e.dma_start(wob, d_wob), mark="qC", inc=16)
    t_ident = t_tril = t_iot = t_gb = t_wib = t_wob = t_idf = t_const_all
    S.op("gpsimd", lambda e: e.dma_start(
        gwT.rearrange("p (c n) -> p c n", c=CCn),
        d_gwT.rearrange("(c p) n -> p c n", p=128)), mark="qG2", inc=16)
    t_ATs = S.op("gpsimd", lambda e: e.dma_start(
        ATs.rearrange("p (c n) -> p c n", c=CCn),
        d_AT.rearrange("(c p) n -> p c n", p=128)), mark="qG2", inc=16)
    t_gwT = t_ATs

    t_dbg_xT = None
    if DEBUG:
        t_dbg_xT = S.op("sync", lambda e: e.dma_start(d_dbg_xT, slotA),
                        deps=[t_xT], mark="qD1", inc=16)

    t_ones = S.op("vector", lambda e: e.memset(ones128, 1.0), mark="V")
    t_eps = S.op("vector", lambda e: e.memset(col(EPSC), 1e-5), mark="V")
    t_nh = S.op("vector", lambda e: e.memset(col(NHLF), -0.5), mark="V")
    t_acca = S.op("vector", lambda e: e.memset(col(ACC_A), 0.0), mark="V")
    t_accp = S.op("vector", lambda e: e.memset(col(ACC_P), 0.0), mark="V")
    watT_r = d_watT.rearrange("(c p) o -> p c o", p=128)
    wprT_r = d_wprT.rearrange("(c p) o -> p c o", p=128)

    vstg = vw(108544, 6144, F32)
    p1_tok = [None, None]
    last_ta = [None]

    def pass1(src_r, width, lnc_base, tok):
        for cc in range(CCn):
            h = cc % 2
            stg = vstg[:, h * 3072: h * 3072 + width]
            tdma = S.op("sync", lambda e, stg=stg, s=src_r[:, cc, :]: e.dma_start(stg, s),
                        deps=[p1_tok[h]], mark="qP%d" % h, inc=16)
            ta = S.op("scalar", lambda e, stg=stg, cc=cc, lnc_base=lnc_base: e.activation(
                stg, stg, AF.Abs, accum_out=LNC[:, lnc_base + cc: lnc_base + cc + 1]),
                deps=[tdma], mark="S")
            p1_tok[h] = ta
            tok[0] = ta

    t_sum_a = [t_acca]
    t_sum_p = [t_accp]
    pass1(watT_r, 3 * C, 0, t_sum_a)
    pass1(wprT_r, C, 8, t_sum_p)
    x_chunk_tok = []
    for k in range(4):
        t = S.op("sync", lambda e, k=k: e.dma_start(
            res.rearrange("p (t c) -> p t c", t=NT)[:, 4 * k:4 * (k + 1), :],
            d_x.rearrange("(t p) c -> p t c", p=128)[:, 4 * k:4 * (k + 1), :]),
            mark="qX%d" % k, inc=16)
        x_chunk_tok.append(t)
    t_xnat = x_chunk_tok[3]
    ta_fin = t_sum_a[0]
    tp_fin = t_sum_p[0]
    wstg_free = [None, None, None]
    t_vone = S.op("vector", lambda e: e.memset(
        V_aug.rearrange("p (t h n) -> p (t h) n", t=NT, h=NH)[:, :, 64:65], 1.0),
        deps=[p1_tok[0], p1_tok[1]], mark="V")

    t_mm_a = S.op("tensor", lambda e: e.matmul(p_d[:, 0:8], ones128, LNC[:, 0:8],
                                               start=True, stop=True),
                  deps=[t_ones, ta_fin], mark="T")
    t_mm_p = S.op("tensor", lambda e: e.matmul(p_d[:, 512:520], ones128, LNC[:, 8:16],
                                               start=True, stop=True),
                  deps=[t_ones, tp_fin], mark="T")
    NA = float(3 * C * C)
    NP = float(C * C)
    t_ra = S.op("vector", lambda e: e.reduce_sum(
        col(ACC_A), p_d[:, 0:8], AX.X), deps=[t_mm_a, t_acca], mark="V")
    t_rp = S.op("vector", lambda e: e.reduce_sum(
        col(ACC_P), p_d[:, 512:520], AX.X), deps=[t_mm_p, t_accp], mark="V")
    t_scla = S.op("vector", lambda e: e.tensor_scalar(
        col(SCL_A), col(ACC_A), 1.0 / NA, 1e-8, ALU.mult, ALU.max),
        deps=[t_ra], mark="V")
    t_rcpa = S.op("vector", lambda e: e.reciprocal(col(RCP_A), col(SCL_A)),
                  deps=[t_scla], mark="V")
    t_al1 = S.op("vector", lambda e: e.tensor_mul(col(ALPH), col(SCL_A), col(SCL_A)),
                 deps=[t_scla], mark="V")
    t_alph = S.op("vector", lambda e: e.tensor_scalar_mul(col(ALPH), col(ALPH), 0.125),
                  deps=[t_al1], mark="V")
    t_sclp = S.op("vector", lambda e: e.tensor_scalar(
        col(SCL_P), col(ACC_P), 1.0 / NP, 1e-8, ALU.mult, ALU.max),
        deps=[t_rp], mark="V")
    t_rcpp = S.op("vector", lambda e: e.reciprocal(col(RCP_P), col(SCL_P)),
                  deps=[t_sclp], mark="V")
    t_s2 = S.op("vector", lambda e: e.tensor_mul(col(S2C), col(SCL_A), col(SCL_P)),
                deps=[t_sclp, t_scla], mark="V")
    t_pd_free = S.op("vector", lambda e: e.engine_nop(), deps=[t_scla, t_sclp], mark="V")

    t_tha = t_thp = None
    for i, f in enumerate((0.5, -0.5, 1.5, -1.5)):
        t_tha = S.op("vector", lambda e, i=i, f=f: e.tensor_scalar_mul(
            col(THA + i), col(SCL_A), f), deps=[t_scla], mark="V")
        t_thp = S.op("vector", lambda e, i=i, f=f: e.tensor_scalar_mul(
            col(THP + i), col(SCL_P), f), deps=[t_sclp], mark="V")
    t_ths = S.op("vector", lambda e: e.engine_nop(), deps=[t_tha, t_thp], mark="V")

    q_chain = [None]
    wstg_bufs = [wstg[:, 0:512], wstg[:, 512:1024], qscr[:, 512:1024]]
    wstg_rr = [0]
    qv_steps = []

    def make_quant_steps(src_ap, dst_ap, th, dst_war=()):
        cell = {}
        qa = qscr[:, 0:512]

        def s_dma():
            wh = wstg_rr[0] % 3
            wstg_rr[0] += 1
            stg = wstg_bufs[wh]
            sview = stg if src_ap.ndim == 2 else stg.rearrange(
                "p (c o) -> p c o", c=src_ap.shape[1])
            cell["stg"] = stg
            cell["wh"] = wh
            cell["td"] = S.op("sync", lambda e, sview=sview: e.dma_start(sview, src_ap),
                              deps=[wstg_free[wh]], mark="qW%d" % wh, inc=16)

        def s1():
            cell["t"] = S.op("vector", lambda e: e.tensor_scalar(
                qa, cell["stg"], col(th + 0), 2.0, ALU.is_ge, ALU.subtract),
                deps=[cell["td"], t_ths, q_chain[0]], mark="V")

        def s2():
            cell["t"] = S.op("vector", lambda e: e.scalar_tensor_tensor(
                dst_ap, cell["stg"], col(th + 1), qa, ALU.is_gt, ALU.add),
                deps=[cell["t"]] + list(dst_war), mark="V")

        def s3():
            cell["t"] = S.op("vector", lambda e: e.scalar_tensor_tensor(
                qa, cell["stg"], col(th + 2), dst_ap, ALU.is_ge, ALU.add),
                deps=[cell["t"]], mark="V")

        def s4():
            t4 = S.op("vector", lambda e: e.scalar_tensor_tensor(
                dst_ap, cell["stg"], col(th + 3), qa, ALU.is_gt, ALU.add),
                deps=[cell["t"]], mark="V")
            wstg_free[cell["wh"]] = t4
            q_chain[0] = t4
            cell["t4"] = t4
        return cell, [s_dma, s1, s2, s3, s4]

    def quant(src_ap, dst_ap, th, dst_war=()):
        assert not qv_steps
        cell, steps = make_quant_steps(src_ap, dst_ap, th, dst_war)
        for s in steps:
            s()
        return cell["t4"]

    def push_quant_chunks(specs, sink):
        built = []
        for (src, dst, th, war, key) in specs:
            cell, steps = make_quant_steps(src, dst, th, war)

            def fin(cell=cell, key=key):
                sink[key] = cell["t4"]
            built.append(steps + [fin])
        out = [built[0][0]]
        if len(built) > 1:
            out.append(built[1][0])
        for c in range(len(built)):
            out.extend(built[c][1:])
            if c + 2 < len(built):
                out.append(built[c + 2][0])
        qv_steps.extend(out)

    def pump_quant(n):
        k = 0
        while qv_steps and k < n:
            qv_steps.pop(0)()
            k += 1

    Vv = V_aug.rearrange("p (t h n) -> p t h n", t=NT, h=NH)
    vbank_free = [t_pd_free, t_pd_free, t_pd_free, t_pd_free]
    vbank_ap = [p_a[:, 0:512], p_a[:, 512:1024], p_b[:, 0:512], p_b[:, 512:1024]]
    xTv = slotA.rearrange("p (c t) -> p c t", c=CCn)

    xTile_tok = [None] * NT
    xps = [p_d, p_c]
    xps_free = [[t_ra, t_rp], []]
    for tt in range(NT):
        pd = xps[tt % 2]
        deps0 = [x_chunk_tok[tt // 4], t_idf] + xps_free[tt % 2]
        tr_last = None
        for cc in range(CCn):
            tr_last = S.op("tensor", lambda e, cc=cc, tt=tt, pd=pd: e.matmul(
                pd[:, cc * 128:(cc + 1) * 128],
                res[:, tt * 1024 + cc * 128: tt * 1024 + (cc + 1) * 128], identf,
                start=True, stop=True),
                deps=deps0 if cc == 0 else [], mark="T")
        t_xvu = S.op("scalar", lambda e, tt=tt, pd=pd: e.copy(
            slotA.rearrange("p (c w) -> p c w", c=8)[:, 0:4, tt * 128:(tt + 1) * 128],
            pd.rearrange("p (c w) -> p c w", c=8)[:, 0:4, :]),
            deps=[tr_last], mark="S")
        t_xvl = S.op("scalar", lambda e, tt=tt, pd=pd: e.copy(
            slotA.rearrange("p (c w) -> p c w", c=8)[:, 4:8, tt * 128:(tt + 1) * 128],
            pd.rearrange("p (c w) -> p c w", c=8)[:, 4:8, :]),
            deps=[tr_last], mark="S")
        xTile_tok[tt] = t_xvl
        xps_free[tt % 2] = [t_xvu, t_xvl]
    t_xT = xTile_tok[NT - 1]

    qk_free = [None, None]
    pairW_free = [None, None]
    pc_free = [xps_free[1][0], xps_free[1][1]]
    sb_free = [None, None]
    last_pv_read = {}
    yb_free = [t_pd_free, t_pd_free]
    yT_tok = {}
    pair_attn_done = [None] * 8
    pair_qkv_done = [None] * 8
    qkv_evac_tok = {}
    pair_q_toks = {}
    v_dead = [None]

    def emit_pair_quant(p):
        pb = p % 2
        specs = []
        for half, obase in ((0, p * 128), (1, C + p * 128)):
            for ci in range(2):
                specs.append((watT_r[:, ci * 4:(ci + 1) * 4, obase:obase + 128],
                              pairW[:, pb * 2048 + half * 1024 + ci * 512:
                                    pb * 2048 + half * 1024 + (ci + 1) * 512],
                              THA, [pairW_free[pb]], 2 * half + ci))
        sink = {}
        pair_q_toks[p] = sink
        if p < 2:
            for (sp, dp, th, war, key) in specs:
                sink[key] = quant(sp, dp, th, dst_war=war)
        else:
            push_quant_chunks(specs, sink)

    def emit_pair_qkv(p):
        pb = p % 2
        early = p < 2
        q_toks = [pair_q_toks[p][k] for k in range(4)]
        for half in range(2):
            for st in range(4):
                bk = (half * 4 + st) % 2
                bank = (p_c[:, bk * 512: bk * 512 + 512] if early
                        else sbanks[bk][:, 0:512])
                deps = ([pc_free[bk] if early else sb_free[bk]]
                        + [xTile_tok[4 * st + i] for i in range(4)])
                mmtok = None
                for cc in range(CCn):
                    mmtok = S.op("tensor", lambda e, cc=cc, st=st, half=half, pb=pb, bank=bank: e.matmul(
                        bank,
                        pairW[:, pb * 2048 + half * 1024 + cc * 128:
                              pb * 2048 + half * 1024 + (cc + 1) * 128],
                        xTv[:, cc, st * 512:(st + 1) * 512],
                        start=(cc == 0), stop=(cc == CCn - 1)),
                        deps=(deps if cc == 0 else []) + [q_toks[2 * half + cc // 4]],
                        mark="T" if cc == CCn - 1 else None)
                tev = S.op("vector", lambda e, pb=pb, half=half, st=st, bank=bank: e.tensor_copy(
                    qkb[:, pb * 4096 + half * 2048 + st * 512:
                        pb * 4096 + half * 2048 + (st + 1) * 512],
                    bank),
                    deps=[mmtok, qk_free[pb]], mark="V")
                qkv_evac_tok[(p, half, st)] = tev
                if early:
                    pc_free[bk] = tev
                else:
                    sb_free[bk] = tev
        pair_qkv_done[p] = tev
        pairW_free[pb] = tev

    v_mm_last = None
    for og in range(2):
        vW = yT[:, 8192 + og * 4096: 8192 + (og + 1) * 4096]
        vq_toks = []
        for cc in range(CCn):
            tc = quant(watT_r[:, cc, 2048 + og * 512: 2048 + (og + 1) * 512],
                       vW[:, cc * 512:(cc + 1) * 512], THA)
            vq_toks.append(tc)
        for tt in range(NT):
            bk = (og * NT + tt) % 4
            mmtok = None
            for cc in range(CCn):
                mmtok = S.op("tensor", lambda e, cc=cc, bk=bk, tt=tt, vW=vW: e.matmul(
                    vbank_ap[bk], xTv[:, cc, tt * 128:(tt + 1) * 128],
                    vW[:, cc * 512:(cc + 1) * 512],
                    start=(cc == 0), stop=(cc == CCn - 1)),
                    deps=([vbank_free[bk], xTile_tok[tt]] if cc == 0 else []) + [vq_toks[cc]],
                    mark="T" if cc == CCn - 1 else None)
            tev = S.op("scalar", lambda e, bk=bk, tt=tt, og=og: e.copy(
                Vv[:, tt, og * 8:(og + 1) * 8, 0:64],
                vbank_ap[bk].rearrange("p (h d) -> p h d", h=8)),
                deps=[mmtok, t_vone], mark="S")
            vbank_free[bk] = tev
            v_mm_last = mmtok
    v_dead[0] = v_mm_last
    sb_free[:] = [vbank_free[1], vbank_free[3]]

    emit_pair_quant(0)
    emit_pair_quant(1)
    emit_pair_qkv(0)
    emit_pair_qkv(1)
    yb_free[0] = yb_free[1] = pair_qkv_done[1]

    t_dbg_va = None
    if DEBUG:
        t_dbg_va = S.op("sync", lambda e: e.dma_start(d_dbg_va, V_aug),
                        deps=vbank_free + [v_mm_last, t_vone], mark="qD2", inc=16)

    pend_tr = [None]
    pend_r = [None]
    pend_pv = []

    def drain_tr():
        if pend_tr[0] is not None:
            pend_tr[0]()
            pend_tr[0] = None

    def drain_r():
        if pend_r[0] is not None:
            pend_tr[0] = pend_r[0]()
            pend_r[0] = None

    def flush_pv():
        if not pend_pv:
            return
        kb, ptk, t_exp, jmin, qq, yb, tlpv, p = pend_pv.pop(0)
        t_pv = None
        for h01 in range(2):
            for j in range(jmin, 4):
                qt = 4 * qq + j
                st = (kb == 0 and j == 0)
                sp = (kb == 4 * qq + 3 and j == 3)
                last_for_tile = (kb == qt)
                last_of_flush = (h01 == 1 and j == 3)
                t_pv = S.op("tensor", lambda e, h01=h01, j=j, kb=kb, ptk=ptk, yb=yb, p=p, st=st, sp=sp: e.matmul(
                    yb[:, h01 * 512 + j * 128: h01 * 512 + j * 128 + 65],
                    ptb[:, ptk * 1024 + h01 * 512 + j * 128:
                        ptk * 1024 + h01 * 512 + (j + 1) * 128],
                    Vv[:, kb, 2 * p + h01, :],
                    start=st, stop=sp),
                    deps=[t_exp, t_vone, t_pd_free] if (kb == 0 and j == jmin) else [t_exp],
                    mark="T" if (last_for_tile or last_of_flush) else None)
                if last_for_tile:
                    tlpv[(h01, j)] = t_pv
        last_pv_read[ptk] = t_pv

    def emit_pair_attention(p):
        pb = p % 2
        Qh = qkb[:, pb * 4096: pb * 4096 + 2048]
        Kh = qkb[:, pb * 4096 + 2048: pb * 4096 + 4096]
        qdeps = [qkv_evac_tok[(p, 0, st)] for st in range(4)] + \
                [qkv_evac_tok[(p, 1, st)] for st in range(4)]
        last_yT_evac = []
        t_exp = None

        for qq in range(4):
            if p == 6 and qq == 3:
                prep_kprojT()
            ybk = qq % 2
            yb = ybanks[ybk]
            tile_last_pv = {}

            for kb in range(4 * qq + 4):
                sbk = kb % 2
                j0 = kb - 4 * qq
                off = max(0, j0 * 128)
                diag = j0 >= 0
                sdeps = qdeps + [sb_free[sbk], yb_free[ybk] if kb == 0 else None]
                S.op("tensor", lambda e, sbk=sbk, off=off, kb=kb, qq=qq, diag=diag: e.matmul(
                    sbanks[sbk][:, off:512],
                    Kh[0:64, kb * 128:(kb + 1) * 128],
                    Qh[0:64, qq * 512 + off:(qq + 1) * 512],
                    start=True, stop=not diag, tile_position=(0, 0)),
                    deps=sdeps, mark=None)
                t_sb = S.op("tensor", lambda e, sbk=sbk, off=off, kb=kb, qq=qq, diag=diag: e.matmul(
                    sbanks[sbk][:, 512 + off:1024],
                    Kh[64:128, kb * 128:(kb + 1) * 128],
                    Qh[64:128, qq * 512 + off:(qq + 1) * 512],
                    start=True, stop=not diag, tile_position=(64, 0)),
                    deps=[], mark=None if diag else "T")
                if diag:
                    S.op("tensor", lambda e, sbk=sbk, off=off: e.matmul(
                        sbanks[sbk][:, off:off + 128], ident, tril,
                        start=False, stop=True),
                        deps=[t_ident, t_tril], mark=None)
                    t_sb = S.op("tensor", lambda e, sbk=sbk, off=off: e.matmul(
                        sbanks[sbk][:, 512 + off:512 + off + 128], ident, tril,
                        start=False, stop=True),
                        deps=[], mark="T")
                if kb == 2:
                    drain_r()
                if kb == 3:
                    drain_tr()
                if len(pend_pv) >= 2:
                    flush_pv()
                ptk = kb % 2
                t_exp = S.op("scalar", lambda e, sbk=sbk, off=off, ptk=ptk: e.activation(
                    ptb.rearrange("p (b h w) -> p (b h) w", b=2, h=2)[:, 2 * ptk:2 * ptk + 2, off:512],
                    sbanks[sbk].rearrange("p (h w) -> p h w", h=2)[:, :, off:512],
                    AF.Exp, scale=col(ALPH)),
                    deps=[t_sb, t_alph, last_pv_read.get(ptk)], mark="S")
                sb_free[sbk] = t_exp
                pend_pv.append((kb, ptk, t_exp, max(0, j0), qq, yb, tile_last_pv, p))
                pump_quant(2)

            def stage_r(yb=yb, qq=qq, ybk=ybk, tile_last_pv=tile_last_pv):
                ev_toks = []
                rc_t = {}
                for h01 in range(2):
                    rc_t[h01] = S.op("vector", lambda e, yb=yb, h01=h01: e.reciprocal(
                        colv[:, RC0 + h01 * 4: RC0 + h01 * 4 + 4].unsqueeze(2),
                        yb[:, h01 * 512: h01 * 512 + 512].rearrange(
                            "p (j w) -> p j w", j=4)[:, :, 64:65]),
                        deps=[tile_last_pv[(h01, 3)]], mark="V")
                ypv = ypairs.rearrange("p (j w) -> p j w", j=4)
                t_na = S.op("vector", lambda e, yb=yb, ypv=ypv: e.tensor_tensor(
                    ypv[:, :, 0:64],
                    yb[:, 0:512].rearrange("p (j w) -> p j w", j=4)[:, :, 0:64],
                    colv[:, RC0:RC0 + 4].unsqueeze(2).broadcast_to([128, 4, 64]),
                    ALU.mult),
                    deps=[rc_t[0]], mark="V")
                t_nb = S.op("vector", lambda e, yb=yb, ypv=ypv: e.tensor_tensor(
                    ypv[:, :, 64:128],
                    yb[:, 512:1024].rearrange("p (j w) -> p j w", j=4)[:, :, 0:64],
                    colv[:, RC0 + 4:RC0 + 8].unsqueeze(2).broadcast_to([128, 4, 64]),
                    ALU.mult),
                    deps=[rc_t[1]], mark="V")
                ev_toks.append(t_na)
                ev_toks.append(t_nb)

                def do_tr(yb=yb, qq=qq, ev_toks=ev_toks, ybk=ybk):
                    tr_last = None
                    for j in range(4):
                        tr_last = S.op("tensor", lambda e, yb=yb, j=j: e.matmul(
                            yb[:, j * 128:(j + 1) * 128],
                            ypairs[:, j * 128:(j + 1) * 128], ident,
                            start=True, stop=True),
                            deps=ev_toks + [t_ident], mark="T")
                    t_ye = S.op("vector", lambda e, yb=yb, p=p, qq=qq: e.tensor_copy(
                        yT[:, p * 2048 + qq * 512: p * 2048 + (qq + 1) * 512],
                        yb[:, 0:512]),
                        deps=[tr_last, v_dead[0]], mark="V")
                    for j in range(4):
                        yT_tok[(p, 4 * qq + j)] = t_ye
                    last_yT_evac[:] = [t_ye]
                    yb_free[ybk] = S.op("vector", lambda e: e.engine_nop(),
                                        deps=[t_ye] + ev_toks, mark="V")
                return do_tr

            if qq == 0 and p + 2 < 8:
                emit_pair_quant(p + 2)
            if qq == 3 and p == 7:
                while qv_steps:
                    qv_steps.pop(0)()
                while pend_pv:
                    flush_pv()
                drain_r()
                drain_tr()
                do_tr_f = stage_r()
                do_tr_f()
            else:
                pend_r[0] = stage_r
        done = S.op("vector", lambda e: e.engine_nop(),
                    deps=list(last_yT_evac) + [t_exp], mark="V")
        pair_attn_done[p] = done
        qk_free[pb] = t_exp

    kprojT = slotA
    kp_toks = {}

    def prep_kprojT():
        xT_dead = pair_qkv_done[7]
        specs = []
        for cc in range(CCn):
            for i in range(2):
                specs.append((wprT_r[:, cc, i * 512:(i + 1) * 512],
                              kprojT[:, cc * 1024 + i * 512: cc * 1024 + (i + 1) * 512],
                              THP, [xT_dead], (cc, i)))
        push_quant_chunks(specs, kp_toks)

    for p in range(8):
        if 2 <= p + 1 < 8:
            emit_pair_qkv(p + 1)
        emit_pair_attention(p)
        if DEBUG and p in (0, 7):
            t_dbg_qk = S.op("sync", lambda e, p=p: e.dma_start(
                d_dbg_qk if p == 0 else d_dbg_qk7, qkb[:, (p % 2) * 4096:(p % 2) * 4096 + 4096]),
                deps=[qkv_evac_tok[(p, h, st)] for h in range(2) for st in range(4)],
                mark="qD%d" % (3 + p // 7), inc=16)
            qk_free[p % 2] = S.op("vector", lambda e: e.engine_nop(),
                                  deps=[t_dbg_qk, qk_free[p % 2]], mark="V")

    t_dbg_yT = None
    if DEBUG:
        t_dbg_yT = S.op("sync", lambda e: e.dma_start(d_dbg_yT, yT),
                        deps=[yT_tok[(cc, tt)] for cc in range(CCn)
                              for tt in range(NT)], mark="qD5", inc=16)


    x2T_tok = {}
    ln1_tok = [None] * NT
    ln_stat_tok = []
    mrb_toks = []
    proj_mm_last = None
    pj_free = [sb_free[0], sb_free[1]]
    ln_col_tok = [None, None]
    scr_tok = [None]
    x2t_ps_free = [t_pd_free]

    for tt in range(NT):
        pbk = tt % 2
        pj = sbanks[pbk]
        ydeps = [yT_tok[(cc, tt)] for cc in range(CCn)]
        mmtok = None
        for ch in range(2):
            for cc in range(CCn):
                first = (ch == 0 and cc == 0)
                mmtok = S.op("tensor", lambda e, ch=ch, cc=cc, tt=tt, pj=pj: e.matmul(
                    pj[:, ch * 512:(ch + 1) * 512],
                    yT[:, cc * 2048 + tt * 128: cc * 2048 + (tt + 1) * 128],
                    kprojT[:, cc * 1024 + ch * 512: cc * 1024 + (ch + 1) * 512],
                    start=(cc == 0), stop=(cc == CCn - 1)),
                    deps=(ydeps + [pj_free[pbk]] if first else []) + [kp_toks[(cc, ch)]],
                    mark="T" if cc == CCn - 1 else None)
        proj_mm_last = mmtok
        t_e0 = S.op("vector", lambda e, pj=pj, tt=tt: e.scalar_tensor_tensor(
            res[:, tt * 1024: tt * 1024 + 512], pj[:, 0:512], col(S2C),
            res[:, tt * 1024: tt * 1024 + 512], ALU.mult, ALU.add,
            accum_out=LNC[:, tt:tt + 1]),
            deps=[mmtok, t_s2, t_xnat], mark="V")
        t_e1 = S.op("vector", lambda e, pj=pj, tt=tt: e.scalar_tensor_tensor(
            res[:, tt * 1024 + 512: tt * 1024 + 1024], pj[:, 512:1024], col(S2C),
            res[:, tt * 1024 + 512: tt * 1024 + 1024], ALU.mult, ALU.add,
            accum_out=LNC[:, 16 + tt:17 + tt]),
            deps=[mmtok, t_s2, t_xnat], mark="V")
        pj_free[pbk] = S.op("vector", lambda e: e.engine_nop(), deps=[t_e0, t_e1], mark="V")

        rt = res[:, tt * 1024:(tt + 1) * 1024]
        t_sq = S.op("scalar", lambda e, rt=rt, tt=tt: e.activation(
            scr1024, rt, AF.Square, accum_out=LNC[:, 32 + tt:33 + tt]),
            deps=[t_e0, t_e1, scr_tok[0]], mark="S")
        scr_tok[0] = t_sq
        ln_stat_tok.append(t_sq)
        ln_stat_tok.append(t_e0)
        ln_stat_tok.append(t_e1)
        if tt in (7, 15):
            lo, hi = (0, 8) if tt == 7 else (8, 16)
            t_hs = S.op("vector", lambda e, lo=lo, hi=hi: e.tensor_add(
                LNC[:, lo:hi], LNC[:, lo:hi], LNC[:, 16 + lo:16 + hi]),
                deps=list(ln_stat_tok), mark="V")
            t_bm = S.op("vector", lambda e, lo=lo, hi=hi: e.tensor_scalar_mul(
                LNC[:, lo:hi], LNC[:, lo:hi], 1.0 / C), deps=[t_hs], mark="V")
            t_bs = S.op("vector", lambda e, lo=lo, hi=hi: e.tensor_scalar_mul(
                LNC[:, 16 + lo:16 + hi], LNC[:, 32 + lo:32 + hi], 1.0 / C),
                deps=[t_hs], mark="V")
            t_m2 = S.op("vector", lambda e, lo=lo, hi=hi: e.tensor_mul(
                LNC[:, 32 + lo:32 + hi], LNC[:, lo:hi], LNC[:, lo:hi]),
                deps=[t_bm, t_bs], mark="V")
            t_var = S.op("vector", lambda e, lo=lo, hi=hi: e.tensor_sub(
                LNC[:, 16 + lo:16 + hi], LNC[:, 16 + lo:16 + hi],
                LNC[:, 32 + lo:32 + hi]), deps=[t_bs, t_m2], mark="V")
            t_ln = S.op("scalar", lambda e, lo=lo, hi=hi: e.activation(
                LNC[:, 32 + lo:32 + hi], LNC[:, 16 + lo:16 + hi], AF.Ln,
                bias=col(EPSC)), deps=[t_var, t_eps], mark="S")
            t_rs = S.op("scalar", lambda e, lo=lo, hi=hi: e.activation(
                LNC[:, 32 + lo:32 + hi], LNC[:, 32 + lo:32 + hi], AF.Exp,
                scale=col(NHLF)), deps=[t_ln, t_nh], mark="S")
            t_mrb = S.op("vector", lambda e, lo=lo, hi=hi: e.tensor_mul(
                LNC[:, 48 + lo:48 + hi], LNC[:, lo:hi], LNC[:, 32 + lo:32 + hi]),
                deps=[t_rs, t_bm], mark="V")
            mrb_toks.append(t_mrb)
            ln_stat_tok.clear()
        if tt >= 8:
            t2 = tt - 8
            ln1_tok[t2] = S.op("vector", lambda e, t2=t2: e.tensor_scalar(
                res[:, t2 * 1024:(t2 + 1) * 1024],
                res[:, t2 * 1024:(t2 + 1) * 1024],
                LNC[:, 32 + t2:33 + t2], LNC[:, 48 + t2:49 + t2],
                ALU.mult, ALU.subtract),
                deps=[mrb_toks[0]], mark="V")


    proj_mm_final = proj_mm_last

    def x2T_ap(cc, tt128, wid):
        if cc < 4:
            return slotA[:, 8192 + cc * 2048 + tt128: 8192 + cc * 2048 + tt128 + wid]
        return slotA[:, (cc - 4) * 2048 + tt128: (cc - 4) * 2048 + tt128 + wid]

    x2t_ps = [p_d, p_c]
    x2t_free = [x2t_ps_free[0], x2t_ps_free[0]]
    gmm_stop = {}
    gmm_started = set()

    def emit_gmm(tt):
        bk = tt // 8
        toff = bk * 512 + (tt % 8) * 64
        xdeps = [x2T_tok[(0, tt)], x2T_tok[(4, tt)], t_gwT, t_ATs,
                 pj_free[0], pj_free[1]]
        bank_first = bk not in gmm_started
        gmm_started.add(bk)
        for cc in range(CCn):
            S.op("tensor", lambda e, cc=cc, tt=tt, toff=toff: e.matmul(
                p_a[:, toff: toff + 8], x2T_ap(cc, tt * 128, 128),
                gwT.rearrange("p (c n) -> p c n", c=CCn)[:, cc, :],
                start=(bank_first and cc == 0), stop=False),
                deps=xdeps if cc == 0 else [], mark=None)
        bank_last = (tt % 8) == 7
        for cc in range(CCn):
            tok = S.op("tensor", lambda e, cc=cc, tt=tt, toff=toff: e.matmul(
                p_a[:, toff + 8: toff + 40], x2T_ap(cc, tt * 128, 128),
                ATs.rearrange("p (c n) -> p c n", c=CCn)[:, cc, :],
                start=False, stop=(bank_last and cc == CCn - 1)),
                deps=[], mark="T" if (bank_last and cc == CCn - 1) else None)
        if bank_last:
            gmm_stop[bk] = tok

    for tt in range(NT):
        pd = x2t_ps[tt % 2]
        tr_deps = [ln1_tok[tt], t_idf, x2t_free[tt % 2], t_pd_free, proj_mm_final]
        tr_last = None
        for cc in range(CCn):
            tr_last = S.op("tensor", lambda e, cc=cc, tt=tt, pd=pd: e.matmul(
                pd[:, (cc % 8) * 128:((cc % 8) + 1) * 128],
                res[:, tt * 1024 + cc * 128: tt * 1024 + (cc + 1) * 128], identf,
                start=True, stop=True),
                deps=tr_deps, mark="T")
        t_evu = S.op("scalar", lambda e, tt=tt, pd=pd: e.copy(
            slotA.rearrange("p (c w) -> p c w", c=8)[:, 4:8, tt * 128:(tt + 1) * 128],
            pd.rearrange("p (c w) -> p c w", c=8)[:, 0:4, :]),
            deps=[tr_last], mark="S")
        t_evl = S.op("vector", lambda e, tt=tt, pd=pd: e.tensor_copy(
            slotA.rearrange("p (c w) -> p c w", c=8)[:, 0:4, tt * 128:(tt + 1) * 128],
            pd.rearrange("p (c w) -> p c w", c=8)[:, 4:8, :]),
            deps=[tr_last], mark="V")
        for cc in range(CCn):
            x2T_tok[(cc, tt)] = t_evl if cc < 4 else t_evu
        x2t_free[tt % 2] = S.op("vector", lambda e: e.engine_nop(),
                                deps=[t_evu, t_evl], mark="V")
        if tt < 8:
            t2 = tt + 8
            ln1_tok[t2] = S.op("vector", lambda e, t2=t2: e.tensor_scalar(
                res[:, t2 * 1024:(t2 + 1) * 1024],
                res[:, t2 * 1024:(t2 + 1) * 1024],
                LNC[:, 32 + t2:33 + t2], LNC[:, 48 + t2:49 + t2],
                ALU.mult, ALU.subtract),
                deps=[mrb_toks[1]], mark="V")
        if tt > 0:
            emit_gmm(tt - 1)
    emit_gmm(NT - 1)

    t_dbg_res = None
    if DEBUG:
        t_dbg_res = S.op("sync", lambda e: e.dma_start(d_dbg_res, res),
                         deps=list(ln1_tok), mark="qD6", inc=16)
    t_BT = S.op("gpsimd", lambda e: e.dma_start(BTs, d_BT),
                deps=[proj_mm_last, yT_tok[(7, NT - 1)], t_dbg_yT], mark="qBT", inc=16)

    x2b_tok = {}
    for tt in range(2):
        x2b_tok[tt] = S.op("vector", lambda e, tt=tt: e.tensor_add(
            res[:, tt * 1024:(tt + 1) * 1024], res[:, tt * 1024:(tt + 1) * 1024], wob),
            deps=[ln1_tok[tt], scr_tok[0], t_wob,
                  x2T_tok[(7, tt)], x2T_tok[(0, tt)], t_dbg_res], mark="V")

    q_dead = q_chain[0]
    Lps = p_a[:, 0:1024].rearrange("p (t x) -> p t x", t=16)
    Lsc = qscr[:, 0:128].rearrange("p (t e) -> p t e", t=16)
    Msc = qscr[:, 128:256].rearrange("p (t e) -> p t e", t=16)
    Vsc = qscr[:, 256:384].rearrange("p (t e) -> p t e", t=16)
    Ksc = qscr[:, 384:512].rearrange("p (t e) -> p t e", t=16)
    t_lg = S.op("vector", lambda e: e.tensor_tensor(
        Lsc, Lps[:, :, 0:8],
        gbuf.unsqueeze(1).broadcast_to([128, 16, 8]), ALU.add),
        deps=[gmm_stop[0], gmm_stop[1], t_gb, q_dead], mark="V")
    t_gm = S.op("vector", lambda e: e.reduce_max(
        LNC[:, 0:16].unsqueeze(2), Lsc, AX.X), deps=[t_lg] + mrb_toks, mark="V")
    t_im = S.op("vector", lambda e: e.tensor_tensor(
        Msc, Lsc, LNC[:, 0:16].unsqueeze(2).broadcast_to([128, 16, 8]),
        ALU.is_ge), deps=[t_gm], mark="V")
    t_v8 = S.op("vector", lambda e: e.scalar_tensor_tensor(
        Vsc, Msc, -BIGI, iotapb.unsqueeze(1).broadcast_to([128, 16, 8]),
        ALU.mult, ALU.add), deps=[t_im, t_iot], mark="V")
    t_ix = S.op("vector", lambda e: e.tensor_reduce(
        LNC[:, 16:32].unsqueeze(2), Vsc, AX.X, ALU.min), deps=[t_v8], mark="V")
    t_kp = S.op("vector", lambda e: e.tensor_tensor(
        Ksc, Vsc, LNC[:, 16:32].unsqueeze(2).broadcast_to([128, 16, 8]),
        ALU.is_le), deps=[t_ix], mark="V")
    rmall = scr1024[:, 0:512]
    t_me = S.op("vector", lambda e: e.tensor_tensor(
        rmall.rearrange("p (t e r) -> p t e r", t=16, e=8),
        Lps[:, :, 8:40].rearrange("p t (e r) -> p t e r", e=8),
        Ksc.unsqueeze(3).broadcast_to([128, 16, 8, 4]), ALU.mult),
        deps=[t_kp, scr_tok[0]], mark="V")

    rmT_tok = {}
    rm_state = {"re": None}
    ffn_bank_free = {0: t_me, 1: pj_free[1], 2: None}

    def emit_rm_round(r):
        tr_t = None
        for k in range(8):
            tt = r * 8 + k
            tr_t = S.op("tensor", lambda e, tt=tt, k=k: e.matmul(
                p_d[0:32, k * 128:(k + 1) * 128],
                rmall[:, tt * 32:(tt + 1) * 32], ident,
                start=True, stop=True),
                deps=([t_me, t_ident, x2t_free[0], rm_state["re"]] if k == 0 else []),
                mark="T" if k == 7 else None)
        re_tok = S.op("scalar", lambda e, r=r: e.copy(
            rmT[:, r * 1024:(r + 1) * 1024], p_d[0:32, 0:1024]),
            deps=[tr_t], mark="S")
        for k in range(8):
            rmT_tok[r * 8 + k] = re_tok
        rm_state["re"] = re_tok
        ffn_bank_free[2] = re_tok

    emit_rm_round(0)
    emit_rm_round(1)

    for tt in range(2, NT):
        t0 = S.op("vector", lambda e, tt=tt: e.tensor_add(
            res[:, tt * 1024:(tt + 1) * 1024], res[:, tt * 1024:(tt + 1) * 1024], wob),
            deps=[ln1_tok[tt], scr_tok[0], t_wob,
                  x2T_tok[(7, tt)], x2T_tok[(0, tt)], t_dbg_res], mark="V")
        x2b_tok[tt] = t0

    wiT_r = d_wiT.rearrange("(c p) f -> p c f", p=128)
    woT_r = d_woT.rearrange("(f p) c -> p f c", p=128)
    wdb = [wdb0, wdb1]
    wdb_tok = [None, None]
    wdb_free = [[pair_attn_done[7], t_dbg_va],
                [pair_attn_done[7], q_chain[0]]]

    def load_group(g):
        gb_ = g % 2
        t_wi = S.op("gpsimd", lambda e, g=g, gb_=gb_: e.dma_start(
            wdb[gb_][:, 0:8192].rearrange("p (c f) -> p c f", c=CCn),
            wiT_r[:, :, g * 1024:(g + 1) * 1024]),
            deps=wdb_free[gb_], mark="qF%d" % gb_, inc=16)
        t_wo = S.op("gpsimd", lambda e, g=g, gb_=gb_: e.dma_start(
            wdb[gb_][:, 8192:16384].rearrange("p (f c) -> p f c", f=CCn),
            woT_r[:, g * CCn:(g + 1) * CCn, :]),
            deps=[], mark="qF%d" % gb_, inc=16)
        wdb_tok[gb_] = [t_wo]

    load_group(0)
    load_group(1)

    it_ps_free = [x2t_free[1], x2t_free[1]]
    it_sb_tok = [t_dbg_yT, t_dbg_yT, t_dbg_yT]
    fbanks = [p_a, p_b, p_d]
    add_tok = {tt: None for tt in range(NT)}
    ln2_done = []
    out_dma_toks = []

    pend_ln2 = []

    pend_ln2b = []

    def flush_ln2_stats():
        if not pend_ln2:
            return
        tt, par, atok = pend_ln2.pop(0)
        B0 = LSUM if par == 0 else 43
        GMx = GM if par == 0 else GM2
        rt = res[:, tt * 1024:(tt + 1) * 1024]
        t_hs = S.op("vector", lambda e, B0=B0, GMx=GMx: e.tensor_add(
            col(B0), col(B0), col(GMx)), deps=[atok, ln_col_tok[par]], mark="V")
        t_sq = S.op("vector", lambda e, rt=rt, B0=B0: e.scalar_tensor_tensor(
            scr1024, rt, 1.0, rt, ALU.mult, ALU.mult, accum_out=col(B0 + 1)),
            deps=[atok, scr_tok[0]], mark="V")
        scr_tok[0] = t_sq
        t_m = S.op("vector", lambda e, B0=B0: e.tensor_scalar_mul(
            col(B0), col(B0), 1.0 / C), deps=[t_hs], mark="V")
        t_v1 = S.op("vector", lambda e, B0=B0: e.tensor_scalar_mul(
            col(B0 + 1), col(B0 + 1), 1.0 / C), deps=[t_sq], mark="V")
        t_m2 = S.op("vector", lambda e, B0=B0: e.tensor_mul(
            col(B0 + 2), col(B0), col(B0)), deps=[t_m], mark="V")
        t_var = S.op("vector", lambda e, B0=B0: e.tensor_sub(
            col(B0 + 2), col(B0 + 1), col(B0 + 2)), deps=[t_v1, t_m2], mark="V")
        t_ln = S.op("scalar", lambda e, B0=B0: e.activation(
            col(B0 + 3), col(B0 + 2), AF.Ln, bias=col(EPSC)),
            deps=[t_var], mark="S")
        t_rs = S.op("scalar", lambda e, B0=B0: e.activation(
            col(B0 + 3), col(B0 + 3), AF.Exp, scale=col(NHLF)),
            deps=[t_ln], mark="S")
        t_mr = S.op("vector", lambda e, B0=B0: e.tensor_mul(
            col(B0 + 4), col(B0), col(B0 + 3)), deps=[t_rs, t_m], mark="V")
        pend_ln2b.append((rt, B0, par, t_mr, t_sq, tt))

    def flush_ln2_store():
        while pend_ln2b:
            rt, B0, par, t_mr, t_sq, tt = pend_ln2b.pop(0)
            t_nrm = S.op("vector", lambda e, rt=rt, B0=B0: e.tensor_scalar(
                rt, rt, col(B0 + 3), col(B0 + 4), ALU.mult, ALU.subtract),
                deps=[t_mr, t_sq], mark="V")
            ln_col_tok[par] = t_nrm
            t_st = S.op("sync", lambda e, rt=rt, tt=tt: e.dma_start(
                d_out.rearrange("(t p) c -> p t c", p=128)[:, tt, :], rt),
                deps=[t_nrm], mark="qO", inc=16)
            out_dma_toks.append(t_st)

    for g in range(4):
        gb_ = g % 2
        gdeps = wdb_tok[gb_]
        for tb in range(8):
            wo_last = {}
            pend_wo = []

            def flush_wo(min_keep, tb=tb, gb_=gb_, wo_last=wo_last):
                while len(pend_wo) > min_keep:
                    fc, ib3, t_rl = pend_wo.pop(0)
                    slots = [(2 * tb + tt2) % 3 for tt2 in range(2)]
                    wdeps = [t_rl] + ([ffn_bank_free[slots[0]], ffn_bank_free[slots[1]]]
                                      if fc == 0 else [])
                    for tt2 in range(2):
                        for ch in range(2):
                            wo_last[(tt2, ch)] = S.op(
                                "tensor",
                                lambda e, tt2=tt2, ch=ch, fc=fc, ib3=ib3, gb_=gb_, sl=slots[tt2]: e.matmul(
                                    fbanks[sl][:, ch * 512:(ch + 1) * 512],
                                    it_sb[:, ib3 * 256 + tt2 * 128: ib3 * 256 + (tt2 + 1) * 128],
                                    wdb[gb_][:, 8192 + fc * 1024 + ch * 512:
                                             8192 + fc * 1024 + (ch + 1) * 512],
                                    start=(fc == 0), stop=(fc == CCn - 1)),
                                deps=wdeps, mark="T" if fc == CCn - 1 else None)
                    it_sb_tok[ib3] = wo_last[(1, 1)]

            for fc in range(CCn):
                gf = g * 8 + fc
                ib = fc % 2
                ib3 = fc % 3
                ips = p_c[:, ib * 512: ib * 512 + 256]
                first_deps = gdeps + [it_ps_free[ib],
                                      x2b_tok[2 * tb], x2b_tok[2 * tb + 1]]
                mmtok = None
                for cc in range(CCn):
                    mmtok = S.op("tensor", lambda e, cc=cc, fc=fc, tb=tb, gb_=gb_, ips=ips: e.matmul(
                        ips, wdb[gb_][:, cc * 1024 + fc * 128: cc * 1024 + (fc + 1) * 128],
                        x2T_ap(cc, tb * 256, 256),
                        start=(cc == 0), stop=False),
                        deps=first_deps if cc == 0 else [], mark=None)
                mmtok = S.op("tensor", lambda e, gf=gf, tb=tb, ips=ips: e.matmul(
                    ips, BTs[:, gf * 128:(gf + 1) * 128],
                    rmT[:, tb * 256:(tb + 1) * 256],
                    start=False, stop=True),
                    deps=[t_BT, rmT_tok[2 * tb], rmT_tok[2 * tb + 1]], mark="T")
                flush_wo(1)
                if fc in (3, 5):
                    flush_ln2_stats()
                if fc == 7:
                    flush_ln2_store()
                t_rl = S.op("scalar", lambda e, ib3=ib3, gf=gf, ips=ips: e.activation(
                    it_sb[:, ib3 * 256:(ib3 + 1) * 256], ips, AF.Relu,
                    bias=wib[:, gf:gf + 1]),
                    deps=[mmtok, t_wib, it_sb_tok[ib3]], mark="S")
                it_ps_free[ib] = t_rl
                pend_wo.append((fc, ib3, t_rl))
            flush_wo(0)
            for tt2 in range(2):
                tt = 2 * tb + tt2
                sl = (2 * tb + tt2) % 3
                B0 = LSUM if tt2 == 0 else 43
                GMx = GM if tt2 == 0 else GM2
                t_a0 = S.op("vector", lambda e, tt=tt, sl=sl, B0=B0: e.scalar_tensor_tensor(
                    res[:, tt * 1024: tt * 1024 + 512],
                    res[:, tt * 1024: tt * 1024 + 512], 1.0, fbanks[sl][:, 0:512],
                    ALU.mult, ALU.add, accum_out=col(B0)),
                    deps=[wo_last[(tt2, 0)], x2b_tok[tt], add_tok[tt]], mark="V")
                t_a1 = S.op("vector", lambda e, tt=tt, sl=sl, GMx=GMx: e.scalar_tensor_tensor(
                    res[:, tt * 1024 + 512: tt * 1024 + 1024],
                    res[:, tt * 1024 + 512: tt * 1024 + 1024], 1.0, fbanks[sl][:, 512:1024],
                    ALU.mult, ALU.add, accum_out=col(GMx)),
                    deps=[wo_last[(tt2, 1)], add_tok[tt]], mark="V")
                add_tok[tt] = S.op("vector", lambda e: e.engine_nop(),
                                   deps=[t_a0, t_a1], mark="V")
                ffn_bank_free[sl] = add_tok[tt]
            if g == 3:
                pend_ln2.append((2 * tb, 0, add_tok[2 * tb]))
                pend_ln2.append((2 * tb + 1, 1, add_tok[2 * tb + 1]))
        wdb_free[gb_] = [wo_last[(1, 1)]]
        if g + 2 < 4:
            load_group(g + 2)
    while pend_ln2 or pend_ln2b:
        flush_ln2_stats()
        flush_ln2_stats()
        flush_ln2_store()

    S.barrier_wait("sync", out_dma_toks)

    with nc.Block() as block:
        @block.sync
        def _(eng):
            S.replay("sync", eng)

        @block.tensor
        def _(eng):
            S.replay("tensor", eng)

        @block.vector
        def _(eng):
            S.replay("vector", eng)

        @block.scalar
        def _(eng):
            S.replay("scalar", eng)

        @block.gpsimd
        def _(eng):
            S.replay("gpsimd", eng)

    return nc


_NC_CACHE = None


def kernel(L0_attn_x, L0_attn_w_attn, L0_attn_w_proj, L0_ffn_gate_W, L0_ffn_gate_b,
           L0_ffn_wi_W, L0_ffn_wi_b, L0_ffn_wo_W, L0_ffn_wo_b, L0_ffn_lora_A,
           L0_ffn_lora_B):
    global _NC_CACHE
    x = np.asarray(L0_attn_x, np.float32)
    w_attn = np.asarray(L0_attn_w_attn, np.float32)
    w_proj = np.asarray(L0_attn_w_proj, np.float32)
    gate_W = np.asarray(L0_ffn_gate_W, np.float32)
    gate_b = np.asarray(L0_ffn_gate_b, np.float32)
    wi_W = np.asarray(L0_ffn_wi_W, np.float32)
    wi_b = np.asarray(L0_ffn_wi_b, np.float32)
    wo_W = np.asarray(L0_ffn_wo_W, np.float32)
    wo_b = np.asarray(L0_ffn_wo_b, np.float32)
    lora_A = np.asarray(L0_ffn_lora_A, np.float32)
    lora_B = np.asarray(L0_ffn_lora_B, np.float32)

    watT = np.ascontiguousarray(w_attn.T)
    wprT = np.ascontiguousarray(w_proj.T)
    gwT = np.ascontiguousarray(gate_W.T)
    gb = np.ascontiguousarray(np.tile(gate_b[None, :], (128, 1)))
    wiT = np.ascontiguousarray(wi_W.T)
    wib = np.ascontiguousarray(wi_b.reshape(32, 128).T)
    woT = np.ascontiguousarray(wo_W.T)
    wob = np.ascontiguousarray(np.tile(wo_b[None, :], (128, 1)))
    AT = np.ascontiguousarray(lora_A.reshape(32, C).T)
    BT = np.ascontiguousarray(lora_B.transpose(0, 2, 1).reshape(32, FF))
    ident = np.eye(128, dtype=ml_dtypes.bfloat16)
    identf = np.eye(128, dtype=np.float32)
    rr, cc_ = np.indices((128, 128))
    tril = np.where(rr > cc_, NEG, 0.0).astype(ml_dtypes.bfloat16)
    iotapb = np.ascontiguousarray(
        np.tile(np.arange(NE, dtype=np.float32)[None, :] + BIGI, (128, 1)))

    if _NC_CACHE is None:
        _NC_CACHE = build_program()
    nc = _NC_CACHE

    shared = {"watT": watT, "wprT": wprT, "gwT": gwT, "gb": gb, "wiT": wiT,
              "wib": wib, "woT": woT, "wob": wob, "AT": AT, "BT": BT,
              "ident": ident, "identf": identf, "tril": tril, "iotapb": iotapb}
    in_maps = []
    for b in range(8):
        m = dict(shared)
        m["x_nat"] = np.ascontiguousarray(x[b])
        in_maps.append(m)

    resobj = run_bass_kernel_spmd(nc, in_maps, core_ids=list(range(8)))
    return np.stack([r["out"] for r in resobj.results], axis=0)


if __name__ == "__main__":
    import sys
    sys.path.insert(0, "/root/problem")
    import reference as R
    d = R.setup_inputs(0)
    inp = {k: np.asarray(v) for k, v in d.items()}
    got = kernel(**inp)
    ref = np.asarray(R.reference(**d))
    num = float(np.mean((got.astype(np.float64) - ref.astype(np.float64)) ** 2))
    den = float(np.mean(ref.astype(np.float64) ** 2))
    print("ratio:", num / den)


# version 32
# speedup vs baseline: 1.2289x; 1.0080x over previous
import numpy as np
import ml_dtypes
import concourse.bass as bass
import concourse.mybir as mybir
from concourse.bass_utils import run_bass_kernel_spmd

F32 = mybir.dt.float32
BF16 = mybir.dt.bfloat16
I32 = mybir.dt.int32
U8 = mybir.dt.uint8
AF = mybir.ActivationFunctionType
ALU = mybir.AluOpType
AX = mybir.AxisListType

T = 2048
C = 1024
NH = 16
HD = 64
FF = 4096
NE = 8
LR = 4
NT = 16
CCn = 8
NEG = -1.0e30
BIGI = 1000.0

ENGS = ("sync", "tensor", "vector", "scalar", "gpsimd")
DEBUG = False
PREFETCH_QKV = True


class Sched:

    def __init__(self, nc):
        self.nc = nc
        self.sems = {}
        self.cnt = {}
        self.seen = {}
        self.ops = {e: [] for e in ENGS}

    def _sem(self, name):
        if name not in self.sems:
            self.sems[name] = self.nc.alloc_semaphore(name="sem_" + name)
            self.cnt[name] = 0
        return self.sems[name]

    def op(self, eng, build, deps=(), mark=None, inc=1):
        waits = []
        for d in deps:
            if d is None:
                continue
            s, v = d
            if self.seen.get((eng, s), 0) < v:
                waits.append((s, v))
                self.seen[(eng, s)] = v
        tok = None
        if mark is not None:
            self._sem(mark)
            self.cnt[mark] += inc
            tok = (mark, self.cnt[mark])
        self.ops[eng].append((waits, build, mark, inc))
        return tok

    def barrier_wait(self, eng, deps):
        best = {}
        for d in deps:
            if d is None:
                continue
            s, v = d
            best[s] = max(best.get(s, 0), v)
        self.op(eng, None, deps=list(best.items()), mark=None)

    def replay(self, eng_name, handle):
        for waits, build, mark, inc in self.ops[eng_name]:
            for s, v in waits:
                handle.wait_ge(self.sems[s], v)
            if build is not None:
                ins = build(handle)
                if mark is not None:
                    ins.then_inc(self.sems[mark], inc)


def build_program():
    nc = bass.Bass("TRN2", target_bir_lowering=False, debug=False)

    d_x = nc.dram_tensor("x_nat", [T, C], F32, kind="ExternalInput").ap()
    d_watT = nc.dram_tensor("watT", [C, 3 * C], F32, kind="ExternalInput").ap()
    d_wprT = nc.dram_tensor("wprT", [C, C], F32, kind="ExternalInput").ap()
    d_gwT = nc.dram_tensor("gwT", [C, NE], F32, kind="ExternalInput").ap()
    d_gb = nc.dram_tensor("gb", [128, NE], F32, kind="ExternalInput").ap()
    d_wiT = nc.dram_tensor("wiT", [C, FF], F32, kind="ExternalInput").ap()
    d_wib = nc.dram_tensor("wib", [128, 32], F32, kind="ExternalInput").ap()
    d_woT = nc.dram_tensor("woT", [FF, C], F32, kind="ExternalInput").ap()
    d_wob = nc.dram_tensor("wob", [128, C], F32, kind="ExternalInput").ap()
    d_AT = nc.dram_tensor("AT", [C, 32], F32, kind="ExternalInput").ap()
    d_BT = nc.dram_tensor("BT", [32, FF], F32, kind="ExternalInput").ap()
    d_ident = nc.dram_tensor("ident", [128, 128], BF16, kind="ExternalInput").ap()
    d_identf = nc.dram_tensor("identf", [128, 128], F32, kind="ExternalInput").ap()
    d_tril = nc.dram_tensor("tril", [128, 128], BF16, kind="ExternalInput").ap()
    d_iotapb = nc.dram_tensor("iotapb", [128, NE], F32, kind="ExternalInput").ap()
    d_out = nc.dram_tensor("out", [T, C], F32, kind="ExternalOutput").ap()
    if DEBUG:
        d_dbg_xT = nc.dram_tensor("dbg_xT", [128, 16384], BF16, kind="ExternalOutput").ap()
        d_dbg_va = nc.dram_tensor("dbg_va", [128, 16640], BF16, kind="ExternalOutput").ap()
        d_dbg_qk = nc.dram_tensor("dbg_qk", [128, 4096], BF16, kind="ExternalOutput").ap()
        d_dbg_qk7 = nc.dram_tensor("dbg_qk7", [128, 4096], BF16, kind="ExternalOutput").ap()
        d_dbg_yT = nc.dram_tensor("dbg_yT", [128, 16384], BF16, kind="ExternalOutput").ap()
        d_dbg_res = nc.dram_tensor("dbg_res", [128, 16384], F32, kind="ExternalOutput").ap()

    AR = 212160
    arena = nc.alloc_sbuf_tensor("arena", [128, AR], U8).ap()

    def vw(off, nelem, dt):
        nb = {F32: 4, BF16: 2, I32: 4}[dt] * nelem
        return arena[:, off:off + nb].bitcast(dt)

    res = vw(0, 16384, F32)
    slotA = vw(65536, 16384, BF16)
    ident = vw(98304, 128, BF16)
    tril = vw(98560, 128, BF16)
    ones128 = vw(99072, 128, F32)
    iotapb = vw(99584, NE, F32)
    gbuf = vw(99616, NE, F32)
    wib = vw(99648, 32, F32)
    gwT = vw(99776, 64, BF16)
    ATs = vw(99904, 256, BF16)
    ypairs = vw(100416, 512, BF16)
    colv = vw(101440, 64, F32)
    rm_bf = vw(101696, 32, BF16)
    gscr = vw(101760, 24, F32)
    keep8 = vw(101856, NE, F32)
    scr1024 = vw(101888, 1024, BF16)
    wob = vw(104448, 1024, F32)
    V_aug = vw(108544, 16640, BF16)
    qkb = vw(141824, 8192, BF16)
    ptb = vw(158208, 2048, BF16)
    pairW = vw(162304, 4096, BF16)
    wstg = vw(170496, 1024, F32)
    qscr = vw(174592, 1024, F32)
    yT = vw(178688, 16384, BF16)
    wdb0 = vw(108544, 16384, BF16)
    wdb1 = vw(141312, 16384, BF16)
    rmT = vw(174592, 2048, BF16)[0:32, :]
    it_sb = vw(178688, 768, BF16)
    BTs = vw(180224, 4096, BF16)[0:32, :]
    identf = vw(211456, 128, F32)
    rm_bf2 = vw(211968, 32, BF16)
    gscr2 = vw(212032, 24, F32)
    keep8b = vw(212128, NE, F32)
    LNC = vw(104192, 64, F32)

    def col(k):
        return colv[:, k:k + 1]

    ACC_A, TCOL, ACC_P, SCL_A, RCP_A, ALPH, SCL_P, RCP_P, S2C = range(9)
    EPSC, NHLF = 9, 10
    LSUM, LM, LSQ, LT1, LVAR, LRSTD, LMR = 11, 12, 13, 14, 15, 16, 17
    GM, GIDX = 18, 19
    GM2, GIDX2 = 41, 42
    RC0 = 20
    THA = 28
    THP = 34

    p_a = nc.alloc_psum_tensor("p_a", [128, 1024], F32).ap()
    p_b = nc.alloc_psum_tensor("p_b", [128, 1024], F32).ap()
    p_c = nc.alloc_psum_tensor("p_c", [128, 1024], F32).ap()
    p_d = nc.alloc_psum_tensor("p_d", [128, 1024], F32).ap()
    sbanks = [p_a, p_b]
    ybanks = [p_c, p_d]

    S = Sched(nc)

    S.op("sync", lambda e: e.dma_start(ident, d_ident), mark="qC", inc=16)
    S.op("sync", lambda e: e.dma_start(identf, d_identf), mark="qC", inc=16)
    S.op("sync", lambda e: e.dma_start(tril, d_tril), mark="qC", inc=16)
    S.op("sync", lambda e: e.dma_start(iotapb, d_iotapb), mark="qC", inc=16)
    S.op("sync", lambda e: e.dma_start(gbuf, d_gb), mark="qC", inc=16)
    S.op("sync", lambda e: e.dma_start(wib, d_wib), mark="qC", inc=16)
    t_const_all = S.op("sync", lambda e: e.dma_start(wob, d_wob), mark="qC", inc=16)
    t_ident = t_tril = t_iot = t_gb = t_wib = t_wob = t_idf = t_const_all
    S.op("gpsimd", lambda e: e.dma_start(
        gwT.rearrange("p (c n) -> p c n", c=CCn),
        d_gwT.rearrange("(c p) n -> p c n", p=128)), mark="qG2", inc=16)
    t_ATs = S.op("gpsimd", lambda e: e.dma_start(
        ATs.rearrange("p (c n) -> p c n", c=CCn),
        d_AT.rearrange("(c p) n -> p c n", p=128)), mark="qG2", inc=16)
    t_gwT = t_ATs

    t_dbg_xT = None
    if DEBUG:
        t_dbg_xT = S.op("sync", lambda e: e.dma_start(d_dbg_xT, slotA),
                        deps=[t_xT], mark="qD1", inc=16)

    t_ones = S.op("vector", lambda e: e.memset(ones128, 1.0), mark="V")
    t_eps = S.op("vector", lambda e: e.memset(col(EPSC), 1e-5), mark="V")
    t_nh = S.op("vector", lambda e: e.memset(col(NHLF), -0.5), mark="V")
    t_acca = S.op("vector", lambda e: e.memset(col(ACC_A), 0.0), mark="V")
    t_accp = S.op("vector", lambda e: e.memset(col(ACC_P), 0.0), mark="V")
    watT_r = d_watT.rearrange("(c p) o -> p c o", p=128)
    wprT_r = d_wprT.rearrange("(c p) o -> p c o", p=128)

    vstg = vw(108544, 6144, F32)
    p1_tok = [None, None]
    last_ta = [None]

    def pass1(src_r, width, lnc_base, tok):
        for cc in range(CCn):
            h = cc % 2
            stg = vstg[:, h * 3072: h * 3072 + width]
            tdma = S.op("sync", lambda e, stg=stg, s=src_r[:, cc, :]: e.dma_start(stg, s),
                        deps=[p1_tok[h]], mark="qP%d" % h, inc=16)
            ta = S.op("scalar", lambda e, stg=stg, cc=cc, lnc_base=lnc_base: e.activation(
                stg, stg, AF.Abs, accum_out=LNC[:, lnc_base + cc: lnc_base + cc + 1]),
                deps=[tdma], mark="S")
            p1_tok[h] = ta
            tok[0] = ta

    t_sum_a = [t_acca]
    t_sum_p = [t_accp]
    pass1(watT_r, 3 * C, 0, t_sum_a)
    pass1(wprT_r, C, 8, t_sum_p)
    x_chunk_tok = []
    for k in range(4):
        t = S.op("sync", lambda e, k=k: e.dma_start(
            res.rearrange("p (t c) -> p t c", t=NT)[:, 4 * k:4 * (k + 1), :],
            d_x.rearrange("(t p) c -> p t c", p=128)[:, 4 * k:4 * (k + 1), :]),
            mark="qX%d" % k, inc=16)
        x_chunk_tok.append(t)
    t_xnat = x_chunk_tok[3]
    ta_fin = t_sum_a[0]
    tp_fin = t_sum_p[0]
    wstg_free = [None, None, None]
    t_vone = S.op("vector", lambda e: e.memset(
        V_aug.rearrange("p (t h n) -> p (t h) n", t=NT, h=NH)[:, :, 64:65], 1.0),
        deps=[p1_tok[0], p1_tok[1]], mark="V")

    t_mm_a = S.op("tensor", lambda e: e.matmul(p_d[:, 0:8], ones128, LNC[:, 0:8],
                                               start=True, stop=True),
                  deps=[t_ones, ta_fin], mark="T")
    t_mm_p = S.op("tensor", lambda e: e.matmul(p_d[:, 512:520], ones128, LNC[:, 8:16],
                                               start=True, stop=True),
                  deps=[t_ones, tp_fin], mark="T")
    NA = float(3 * C * C)
    NP = float(C * C)
    t_ra = S.op("vector", lambda e: e.reduce_sum(
        col(ACC_A), p_d[:, 0:8], AX.X), deps=[t_mm_a, t_acca], mark="V")
    t_rp = S.op("vector", lambda e: e.reduce_sum(
        col(ACC_P), p_d[:, 512:520], AX.X), deps=[t_mm_p, t_accp], mark="V")
    t_scla = S.op("vector", lambda e: e.tensor_scalar(
        col(SCL_A), col(ACC_A), 1.0 / NA, 1e-8, ALU.mult, ALU.max),
        deps=[t_ra], mark="V")
    t_rcpa = S.op("vector", lambda e: e.reciprocal(col(RCP_A), col(SCL_A)),
                  deps=[t_scla], mark="V")
    t_al1 = S.op("vector", lambda e: e.tensor_mul(col(ALPH), col(SCL_A), col(SCL_A)),
                 deps=[t_scla], mark="V")
    t_alph = S.op("vector", lambda e: e.tensor_scalar_mul(col(ALPH), col(ALPH), 0.125),
                  deps=[t_al1], mark="V")
    t_sclp = S.op("vector", lambda e: e.tensor_scalar(
        col(SCL_P), col(ACC_P), 1.0 / NP, 1e-8, ALU.mult, ALU.max),
        deps=[t_rp], mark="V")
    t_rcpp = S.op("vector", lambda e: e.reciprocal(col(RCP_P), col(SCL_P)),
                  deps=[t_sclp], mark="V")
    t_s2 = S.op("vector", lambda e: e.tensor_mul(col(S2C), col(SCL_A), col(SCL_P)),
                deps=[t_sclp, t_scla], mark="V")
    t_pd_free = S.op("vector", lambda e: e.engine_nop(), deps=[t_scla, t_sclp], mark="V")

    t_tha = t_thp = None
    for i, f in enumerate((0.5, -0.5, 1.5, -1.5)):
        t_tha = S.op("vector", lambda e, i=i, f=f: e.tensor_scalar_mul(
            col(THA + i), col(SCL_A), f), deps=[t_scla], mark="V")
        t_thp = S.op("vector", lambda e, i=i, f=f: e.tensor_scalar_mul(
            col(THP + i), col(SCL_P), f), deps=[t_sclp], mark="V")
    t_ths = S.op("vector", lambda e: e.engine_nop(), deps=[t_tha, t_thp], mark="V")

    q_chain = [None]
    wstg_bufs = [wstg[:, 0:512], wstg[:, 512:1024], qscr[:, 512:1024]]
    wstg_rr = [0]
    qv_steps = []

    def make_quant_steps(src_ap, dst_ap, th, dst_war=()):
        cell = {}
        qa = qscr[:, 0:512]

        def s_dma():
            wh = wstg_rr[0] % 3
            wstg_rr[0] += 1
            stg = wstg_bufs[wh]
            sview = stg if src_ap.ndim == 2 else stg.rearrange(
                "p (c o) -> p c o", c=src_ap.shape[1])
            cell["stg"] = stg
            cell["wh"] = wh
            cell["td"] = S.op("sync", lambda e, sview=sview: e.dma_start(sview, src_ap),
                              deps=[wstg_free[wh]], mark="qW%d" % wh, inc=16)

        def s1():
            cell["t"] = S.op("vector", lambda e: e.tensor_scalar(
                qa, cell["stg"], col(th + 0), 2.0, ALU.is_ge, ALU.subtract),
                deps=[cell["td"], t_ths, q_chain[0]], mark="V")

        def s2():
            cell["t"] = S.op("vector", lambda e: e.scalar_tensor_tensor(
                dst_ap, cell["stg"], col(th + 1), qa, ALU.is_gt, ALU.add),
                deps=[cell["t"]] + list(dst_war), mark="V")

        def s3():
            cell["t"] = S.op("vector", lambda e: e.scalar_tensor_tensor(
                qa, cell["stg"], col(th + 2), dst_ap, ALU.is_ge, ALU.add),
                deps=[cell["t"]], mark="V")

        def s4():
            t4 = S.op("vector", lambda e: e.scalar_tensor_tensor(
                dst_ap, cell["stg"], col(th + 3), qa, ALU.is_gt, ALU.add),
                deps=[cell["t"]], mark="V")
            wstg_free[cell["wh"]] = t4
            q_chain[0] = t4
            cell["t4"] = t4
        return cell, [s_dma, s1, s2, s3, s4]

    def quant(src_ap, dst_ap, th, dst_war=()):
        assert not qv_steps
        cell, steps = make_quant_steps(src_ap, dst_ap, th, dst_war)
        for s in steps:
            s()
        return cell["t4"]

    def push_quant_chunks(specs, sink):
        built = []
        for (src, dst, th, war, key) in specs:
            cell, steps = make_quant_steps(src, dst, th, war)

            def fin(cell=cell, key=key):
                sink[key] = cell["t4"]
            built.append(steps + [fin])
        out = [built[0][0]]
        if len(built) > 1:
            out.append(built[1][0])
        for c in range(len(built)):
            out.extend(built[c][1:])
            if c + 2 < len(built):
                out.append(built[c + 2][0])
        qv_steps.extend(out)

    def pump_quant(n):
        k = 0
        while qv_steps and k < n:
            qv_steps.pop(0)()
            k += 1

    Vv = V_aug.rearrange("p (t h n) -> p t h n", t=NT, h=NH)
    vbank_free = [t_pd_free, t_pd_free, t_pd_free, t_pd_free]
    vbank_ap = [p_a[:, 0:512], p_a[:, 512:1024], p_b[:, 0:512], p_b[:, 512:1024]]
    xTv = slotA.rearrange("p (c t) -> p c t", c=CCn)

    xTile_tok = [None] * NT
    xps = [p_d, p_c]
    xps_free = [[t_ra, t_rp], []]
    for tt in range(NT):
        pd = xps[tt % 2]
        deps0 = [x_chunk_tok[tt // 4], t_idf] + xps_free[tt % 2]
        tr_last = None
        for cc in range(CCn):
            tr_last = S.op("tensor", lambda e, cc=cc, tt=tt, pd=pd: e.matmul(
                pd[:, cc * 128:(cc + 1) * 128],
                res[:, tt * 1024 + cc * 128: tt * 1024 + (cc + 1) * 128], identf,
                start=True, stop=True),
                deps=deps0 if cc == 0 else [], mark="T")
        t_xvu = S.op("scalar", lambda e, tt=tt, pd=pd: e.copy(
            slotA.rearrange("p (c w) -> p c w", c=8)[:, 0:4, tt * 128:(tt + 1) * 128],
            pd.rearrange("p (c w) -> p c w", c=8)[:, 0:4, :]),
            deps=[tr_last], mark="S")
        t_xvl = S.op("scalar", lambda e, tt=tt, pd=pd: e.copy(
            slotA.rearrange("p (c w) -> p c w", c=8)[:, 4:8, tt * 128:(tt + 1) * 128],
            pd.rearrange("p (c w) -> p c w", c=8)[:, 4:8, :]),
            deps=[tr_last], mark="S")
        xTile_tok[tt] = t_xvl
        xps_free[tt % 2] = [t_xvu, t_xvl]
    t_xT = xTile_tok[NT - 1]

    qk_free = [None, None]
    pairW_free = [None, None]
    pc_free = [xps_free[1][0], xps_free[1][1]]
    sb_free = [None, None]
    last_pv_read = {}
    yb_free = [t_pd_free, t_pd_free]
    yT_tok = {}
    pair_attn_done = [None] * 8
    pair_qkv_done = [None] * 8
    qkv_evac_tok = {}
    pair_q_toks = {}
    v_dead = [None]

    def emit_pair_quant(p):
        pb = p % 2
        specs = []
        for half, obase in ((0, p * 128), (1, C + p * 128)):
            for ci in range(2):
                specs.append((watT_r[:, ci * 4:(ci + 1) * 4, obase:obase + 128],
                              pairW[:, pb * 2048 + half * 1024 + ci * 512:
                                    pb * 2048 + half * 1024 + (ci + 1) * 512],
                              THA, [pairW_free[pb]], 2 * half + ci))
        sink = {}
        pair_q_toks[p] = sink
        if p < 2:
            for (sp, dp, th, war, key) in specs:
                sink[key] = quant(sp, dp, th, dst_war=war)
        else:
            push_quant_chunks(specs, sink)

    def emit_pair_qkv(p):
        pb = p % 2
        early = p < 2
        q_toks = [pair_q_toks[p][k] for k in range(4)]
        for half in range(2):
            for st in range(4):
                bk = (half * 4 + st) % 2
                bank = (p_c[:, bk * 512: bk * 512 + 512] if early
                        else sbanks[bk][:, 0:512])
                deps = ([pc_free[bk] if early else sb_free[bk]]
                        + [xTile_tok[4 * st + i] for i in range(4)])
                mmtok = None
                for cc in range(CCn):
                    mmtok = S.op("tensor", lambda e, cc=cc, st=st, half=half, pb=pb, bank=bank: e.matmul(
                        bank,
                        pairW[:, pb * 2048 + half * 1024 + cc * 128:
                              pb * 2048 + half * 1024 + (cc + 1) * 128],
                        xTv[:, cc, st * 512:(st + 1) * 512],
                        start=(cc == 0), stop=(cc == CCn - 1)),
                        deps=(deps if cc == 0 else []) + [q_toks[2 * half + cc // 4]],
                        mark="T" if cc == CCn - 1 else None)
                tev = S.op("vector", lambda e, pb=pb, half=half, st=st, bank=bank: e.tensor_copy(
                    qkb[:, pb * 4096 + half * 2048 + st * 512:
                        pb * 4096 + half * 2048 + (st + 1) * 512],
                    bank),
                    deps=[mmtok, qk_free[pb]], mark="V")
                qkv_evac_tok[(p, half, st)] = tev
                if early:
                    pc_free[bk] = tev
                else:
                    sb_free[bk] = tev
        pair_qkv_done[p] = tev
        pairW_free[pb] = tev

    v_mm_last = None
    for og in range(2):
        vW = yT[:, 8192 + og * 4096: 8192 + (og + 1) * 4096]
        vq_toks = []
        for cc in range(CCn):
            tc = quant(watT_r[:, cc, 2048 + og * 512: 2048 + (og + 1) * 512],
                       vW[:, cc * 512:(cc + 1) * 512], THA)
            vq_toks.append(tc)
        for tt in range(NT):
            bk = (og * NT + tt) % 4
            mmtok = None
            for cc in range(CCn):
                mmtok = S.op("tensor", lambda e, cc=cc, bk=bk, tt=tt, vW=vW: e.matmul(
                    vbank_ap[bk], xTv[:, cc, tt * 128:(tt + 1) * 128],
                    vW[:, cc * 512:(cc + 1) * 512],
                    start=(cc == 0), stop=(cc == CCn - 1)),
                    deps=([vbank_free[bk], xTile_tok[tt]] if cc == 0 else []) + [vq_toks[cc]],
                    mark="T" if cc == CCn - 1 else None)
            tev = S.op("scalar", lambda e, bk=bk, tt=tt, og=og: e.copy(
                Vv[:, tt, og * 8:(og + 1) * 8, 0:64],
                vbank_ap[bk].rearrange("p (h d) -> p h d", h=8)),
                deps=[mmtok, t_vone], mark="S")
            vbank_free[bk] = tev
            v_mm_last = mmtok
    v_dead[0] = v_mm_last
    sb_free[:] = [vbank_free[1], vbank_free[3]]

    emit_pair_quant(0)
    emit_pair_quant(1)
    emit_pair_qkv(0)
    emit_pair_qkv(1)
    yb_free[0] = yb_free[1] = pair_qkv_done[1]

    t_dbg_va = None
    if DEBUG:
        t_dbg_va = S.op("sync", lambda e: e.dma_start(d_dbg_va, V_aug),
                        deps=vbank_free + [v_mm_last, t_vone], mark="qD2", inc=16)

    pend_tr = [None]
    pend_r = [None]
    pend_pv = []

    def drain_tr():
        if pend_tr[0] is not None:
            pend_tr[0]()
            pend_tr[0] = None

    def drain_r():
        if pend_r[0] is not None:
            pend_tr[0] = pend_r[0]()
            pend_r[0] = None

    def flush_pv():
        if not pend_pv:
            return
        kb, ptk, t_exp, jmin, qq, yb, tlpv, p = pend_pv.pop(0)
        t_pv = None
        for h01 in range(2):
            for j in range(jmin, 4):
                qt = 4 * qq + j
                st = (kb == 0 and j == 0)
                sp = (kb == 4 * qq + 3 and j == 3)
                last_for_tile = (kb == qt)
                last_of_flush = (h01 == 1 and j == 3)
                t_pv = S.op("tensor", lambda e, h01=h01, j=j, kb=kb, ptk=ptk, yb=yb, p=p, st=st, sp=sp: e.matmul(
                    yb[:, h01 * 512 + j * 128: h01 * 512 + j * 128 + 65],
                    ptb[:, ptk * 1024 + h01 * 512 + j * 128:
                        ptk * 1024 + h01 * 512 + (j + 1) * 128],
                    Vv[:, kb, 2 * p + h01, :],
                    start=st, stop=sp),
                    deps=[t_exp, t_vone, t_pd_free] if (kb == 0 and j == jmin) else [t_exp],
                    mark="T" if (last_for_tile or last_of_flush) else None)
                if last_for_tile:
                    tlpv[(h01, j)] = t_pv
        last_pv_read[ptk] = t_pv

    def emit_pair_attention(p):
        pb = p % 2
        Qh = qkb[:, pb * 4096: pb * 4096 + 2048]
        Kh = qkb[:, pb * 4096 + 2048: pb * 4096 + 4096]
        qdeps = [qkv_evac_tok[(p, 0, st)] for st in range(4)] + \
                [qkv_evac_tok[(p, 1, st)] for st in range(4)]
        last_yT_evac = []
        t_exp = None

        for qq in range(4):
            if p == 6 and qq == 3:
                prep_kprojT()
            ybk = qq % 2
            yb = ybanks[ybk]
            tile_last_pv = {}

            for kb in range(4 * qq + 4):
                sbk = kb % 2
                j0 = kb - 4 * qq
                off = max(0, j0 * 128)
                diag = j0 >= 0
                sdeps = qdeps + [sb_free[sbk], yb_free[ybk] if kb == 0 else None]
                S.op("tensor", lambda e, sbk=sbk, off=off, kb=kb, qq=qq, diag=diag: e.matmul(
                    sbanks[sbk][:, off:512],
                    Kh[0:64, kb * 128:(kb + 1) * 128],
                    Qh[0:64, qq * 512 + off:(qq + 1) * 512],
                    start=True, stop=not diag, tile_position=(0, 0)),
                    deps=sdeps, mark=None)
                t_sb = S.op("tensor", lambda e, sbk=sbk, off=off, kb=kb, qq=qq, diag=diag: e.matmul(
                    sbanks[sbk][:, 512 + off:1024],
                    Kh[64:128, kb * 128:(kb + 1) * 128],
                    Qh[64:128, qq * 512 + off:(qq + 1) * 512],
                    start=True, stop=not diag, tile_position=(64, 0)),
                    deps=[], mark=None if diag else "T")
                if diag:
                    S.op("tensor", lambda e, sbk=sbk, off=off: e.matmul(
                        sbanks[sbk][:, off:off + 128], ident, tril,
                        start=False, stop=True),
                        deps=[t_ident, t_tril], mark=None)
                    t_sb = S.op("tensor", lambda e, sbk=sbk, off=off: e.matmul(
                        sbanks[sbk][:, 512 + off:512 + off + 128], ident, tril,
                        start=False, stop=True),
                        deps=[], mark="T")
                if kb == 2:
                    drain_r()
                if kb == 3:
                    drain_tr()
                if len(pend_pv) >= 2:
                    flush_pv()
                ptk = kb % 2
                t_exp = S.op("scalar", lambda e, sbk=sbk, off=off, ptk=ptk: e.activation(
                    ptb.rearrange("p (b h w) -> p (b h) w", b=2, h=2)[:, 2 * ptk:2 * ptk + 2, off:512],
                    sbanks[sbk].rearrange("p (h w) -> p h w", h=2)[:, :, off:512],
                    AF.Exp, scale=col(ALPH)),
                    deps=[t_sb, t_alph, last_pv_read.get(ptk)], mark="S")
                sb_free[sbk] = t_exp
                pend_pv.append((kb, ptk, t_exp, max(0, j0), qq, yb, tile_last_pv, p))
                pump_quant(2)

            def stage_r(yb=yb, qq=qq, ybk=ybk, tile_last_pv=tile_last_pv):
                ev_toks = []
                rc_t = {}
                for h01 in range(2):
                    rc_t[h01] = S.op("vector", lambda e, yb=yb, h01=h01: e.reciprocal(
                        colv[:, RC0 + h01 * 4: RC0 + h01 * 4 + 4].unsqueeze(2),
                        yb[:, h01 * 512: h01 * 512 + 512].rearrange(
                            "p (j w) -> p j w", j=4)[:, :, 64:65]),
                        deps=[tile_last_pv[(h01, 3)]], mark="V")
                ypv = ypairs.rearrange("p (j w) -> p j w", j=4)
                t_na = S.op("vector", lambda e, yb=yb, ypv=ypv: e.tensor_tensor(
                    ypv[:, :, 0:64],
                    yb[:, 0:512].rearrange("p (j w) -> p j w", j=4)[:, :, 0:64],
                    colv[:, RC0:RC0 + 4].unsqueeze(2).broadcast_to([128, 4, 64]),
                    ALU.mult),
                    deps=[rc_t[0]], mark="V")
                t_nb = S.op("vector", lambda e, yb=yb, ypv=ypv: e.tensor_tensor(
                    ypv[:, :, 64:128],
                    yb[:, 512:1024].rearrange("p (j w) -> p j w", j=4)[:, :, 0:64],
                    colv[:, RC0 + 4:RC0 + 8].unsqueeze(2).broadcast_to([128, 4, 64]),
                    ALU.mult),
                    deps=[rc_t[1]], mark="V")
                ev_toks.append(t_na)
                ev_toks.append(t_nb)

                def do_tr(yb=yb, qq=qq, ev_toks=ev_toks, ybk=ybk):
                    tr_last = None
                    for j in range(4):
                        tr_last = S.op("tensor", lambda e, yb=yb, j=j: e.matmul(
                            yb[:, j * 128:(j + 1) * 128],
                            ypairs[:, j * 128:(j + 1) * 128], ident,
                            start=True, stop=True),
                            deps=ev_toks + [t_ident], mark="T")
                    t_ye = S.op("vector", lambda e, yb=yb, p=p, qq=qq: e.tensor_copy(
                        yT[:, p * 2048 + qq * 512: p * 2048 + (qq + 1) * 512],
                        yb[:, 0:512]),
                        deps=[tr_last, v_dead[0]], mark="V")
                    for j in range(4):
                        yT_tok[(p, 4 * qq + j)] = t_ye
                    last_yT_evac[:] = [t_ye]
                    yb_free[ybk] = S.op("vector", lambda e: e.engine_nop(),
                                        deps=[t_ye] + ev_toks, mark="V")
                return do_tr

            if qq == 0 and p + 2 < 8:
                emit_pair_quant(p + 2)
            if qq == 3 and p == 7:
                while qv_steps:
                    qv_steps.pop(0)()
                while pend_pv:
                    flush_pv()
                drain_r()
                drain_tr()
                do_tr_f = stage_r()
                do_tr_f()
            else:
                pend_r[0] = stage_r
        done = S.op("vector", lambda e: e.engine_nop(),
                    deps=list(last_yT_evac) + [t_exp], mark="V")
        pair_attn_done[p] = done
        qk_free[pb] = t_exp

    kprojT = slotA
    kp_toks = {}

    def prep_kprojT():
        xT_dead = pair_qkv_done[7]
        specs = []
        for cc in range(CCn):
            for i in range(2):
                specs.append((wprT_r[:, cc, i * 512:(i + 1) * 512],
                              kprojT[:, cc * 1024 + i * 512: cc * 1024 + (i + 1) * 512],
                              THP, [xT_dead], (cc, i)))
        push_quant_chunks(specs, kp_toks)

    for p in range(8):
        if 2 <= p + 1 < 8:
            emit_pair_qkv(p + 1)
        emit_pair_attention(p)
        if DEBUG and p in (0, 7):
            t_dbg_qk = S.op("sync", lambda e, p=p: e.dma_start(
                d_dbg_qk if p == 0 else d_dbg_qk7, qkb[:, (p % 2) * 4096:(p % 2) * 4096 + 4096]),
                deps=[qkv_evac_tok[(p, h, st)] for h in range(2) for st in range(4)],
                mark="qD%d" % (3 + p // 7), inc=16)
            qk_free[p % 2] = S.op("vector", lambda e: e.engine_nop(),
                                  deps=[t_dbg_qk, qk_free[p % 2]], mark="V")

    t_dbg_yT = None
    if DEBUG:
        t_dbg_yT = S.op("sync", lambda e: e.dma_start(d_dbg_yT, yT),
                        deps=[yT_tok[(cc, tt)] for cc in range(CCn)
                              for tt in range(NT)], mark="qD5", inc=16)


    x2T_tok = {}
    ln1_tok = [None] * NT
    ln_stat_tok = []
    mrb_toks = []
    proj_mm_last = None
    pj_free = [sb_free[0], sb_free[1]]
    ln_col_tok = [None, None]
    scr_tok = [None]
    x2t_ps_free = [t_pd_free]

    for tt in range(NT):
        pbk = tt % 2
        pj = sbanks[pbk]
        ydeps = [yT_tok[(cc, tt)] for cc in range(CCn)]
        mmtok = None
        for ch in range(2):
            for cc in range(CCn):
                first = (ch == 0 and cc == 0)
                mmtok = S.op("tensor", lambda e, ch=ch, cc=cc, tt=tt, pj=pj: e.matmul(
                    pj[:, ch * 512:(ch + 1) * 512],
                    yT[:, cc * 2048 + tt * 128: cc * 2048 + (tt + 1) * 128],
                    kprojT[:, cc * 1024 + ch * 512: cc * 1024 + (ch + 1) * 512],
                    start=(cc == 0), stop=(cc == CCn - 1)),
                    deps=(ydeps + [pj_free[pbk]] if first else []) + [kp_toks[(cc, ch)]],
                    mark="T" if cc == CCn - 1 else None)
        proj_mm_last = mmtok
        t_e0 = S.op("vector", lambda e, pj=pj, tt=tt: e.scalar_tensor_tensor(
            res[:, tt * 1024: tt * 1024 + 512], pj[:, 0:512], col(S2C),
            res[:, tt * 1024: tt * 1024 + 512], ALU.mult, ALU.add,
            accum_out=LNC[:, tt:tt + 1]),
            deps=[mmtok, t_s2, t_xnat], mark="V")
        t_e1 = S.op("vector", lambda e, pj=pj, tt=tt: e.scalar_tensor_tensor(
            res[:, tt * 1024 + 512: tt * 1024 + 1024], pj[:, 512:1024], col(S2C),
            res[:, tt * 1024 + 512: tt * 1024 + 1024], ALU.mult, ALU.add,
            accum_out=LNC[:, 16 + tt:17 + tt]),
            deps=[mmtok, t_s2, t_xnat], mark="V")
        pj_free[pbk] = S.op("vector", lambda e: e.engine_nop(), deps=[t_e0, t_e1], mark="V")

        rt = res[:, tt * 1024:(tt + 1) * 1024]
        t_sq = S.op("scalar", lambda e, rt=rt, tt=tt: e.activation(
            scr1024, rt, AF.Square, accum_out=LNC[:, 32 + tt:33 + tt]),
            deps=[t_e0, t_e1, scr_tok[0]], mark="S")
        scr_tok[0] = t_sq
        ln_stat_tok.append(t_sq)
        ln_stat_tok.append(t_e0)
        ln_stat_tok.append(t_e1)
        if tt in (7, 15):
            lo, hi = (0, 8) if tt == 7 else (8, 16)
            t_hs = S.op("vector", lambda e, lo=lo, hi=hi: e.tensor_add(
                LNC[:, lo:hi], LNC[:, lo:hi], LNC[:, 16 + lo:16 + hi]),
                deps=list(ln_stat_tok), mark="V")
            t_bm = S.op("vector", lambda e, lo=lo, hi=hi: e.tensor_scalar_mul(
                LNC[:, lo:hi], LNC[:, lo:hi], 1.0 / C), deps=[t_hs], mark="V")
            t_bs = S.op("vector", lambda e, lo=lo, hi=hi: e.tensor_scalar_mul(
                LNC[:, 16 + lo:16 + hi], LNC[:, 32 + lo:32 + hi], 1.0 / C),
                deps=[t_hs], mark="V")
            t_m2 = S.op("vector", lambda e, lo=lo, hi=hi: e.tensor_mul(
                LNC[:, 32 + lo:32 + hi], LNC[:, lo:hi], LNC[:, lo:hi]),
                deps=[t_bm, t_bs], mark="V")
            t_var = S.op("vector", lambda e, lo=lo, hi=hi: e.tensor_sub(
                LNC[:, 16 + lo:16 + hi], LNC[:, 16 + lo:16 + hi],
                LNC[:, 32 + lo:32 + hi]), deps=[t_bs, t_m2], mark="V")
            t_ln = S.op("scalar", lambda e, lo=lo, hi=hi: e.activation(
                LNC[:, 32 + lo:32 + hi], LNC[:, 16 + lo:16 + hi], AF.Ln,
                bias=col(EPSC)), deps=[t_var, t_eps], mark="S")
            t_rs = S.op("scalar", lambda e, lo=lo, hi=hi: e.activation(
                LNC[:, 32 + lo:32 + hi], LNC[:, 32 + lo:32 + hi], AF.Exp,
                scale=col(NHLF)), deps=[t_ln, t_nh], mark="S")
            t_mrb = S.op("vector", lambda e, lo=lo, hi=hi: e.tensor_mul(
                LNC[:, 48 + lo:48 + hi], LNC[:, lo:hi], LNC[:, 32 + lo:32 + hi]),
                deps=[t_rs, t_bm], mark="V")
            mrb_toks.append(t_mrb)
            ln_stat_tok.clear()
        if tt >= 8:
            t2 = tt - 8
            ln1_tok[t2] = S.op("vector", lambda e, t2=t2: e.tensor_scalar(
                res[:, t2 * 1024:(t2 + 1) * 1024],
                res[:, t2 * 1024:(t2 + 1) * 1024],
                LNC[:, 32 + t2:33 + t2], LNC[:, 48 + t2:49 + t2],
                ALU.mult, ALU.subtract),
                deps=[mrb_toks[0]], mark="V")


    proj_mm_final = proj_mm_last

    def x2T_ap(cc, tt128, wid):
        if cc < 4:
            return slotA[:, 8192 + cc * 2048 + tt128: 8192 + cc * 2048 + tt128 + wid]
        return slotA[:, (cc - 4) * 2048 + tt128: (cc - 4) * 2048 + tt128 + wid]

    x2t_ps = [p_d, p_c]
    x2t_free = [x2t_ps_free[0], x2t_ps_free[0]]
    mask_half_tok = [None]
    gmm_stop = {}

    rmall = scr1024[:, 0:512]

    def emit_mask_half(hb):
        q_dead = q_chain[0]
        Lps = p_a[:, hb * 512:(hb + 1) * 512].rearrange("p (t x) -> p t x", t=8)
        sc0 = 512 * hb
        Lsc = qscr[:, sc0:sc0 + 64].rearrange("p (t e) -> p t e", t=8)
        Msc = qscr[:, sc0 + 64:sc0 + 128].rearrange("p (t e) -> p t e", t=8)
        Vsc = qscr[:, sc0 + 128:sc0 + 192].rearrange("p (t e) -> p t e", t=8)
        Ksc = qscr[:, sc0 + 192:sc0 + 256].rearrange("p (t e) -> p t e", t=8)
        c0 = 8 * hb
        t_lg = S.op("vector", lambda e: e.tensor_tensor(
            Lsc, Lps[:, :, 0:8],
            gbuf.unsqueeze(1).broadcast_to([128, 8, 8]), ALU.add),
            deps=[gmm_stop[hb], t_gb, q_dead], mark="V")
        t_gm = S.op("vector", lambda e: e.reduce_max(
            LNC[:, c0:c0 + 8].unsqueeze(2), Lsc, AX.X),
            deps=[t_lg] + mrb_toks, mark="V")
        t_im = S.op("vector", lambda e: e.tensor_tensor(
            Msc, Lsc, LNC[:, c0:c0 + 8].unsqueeze(2).broadcast_to([128, 8, 8]),
            ALU.is_ge), deps=[t_gm], mark="V")
        t_v8 = S.op("vector", lambda e: e.scalar_tensor_tensor(
            Vsc, Msc, -BIGI, iotapb.unsqueeze(1).broadcast_to([128, 8, 8]),
            ALU.mult, ALU.add), deps=[t_im, t_iot], mark="V")
        t_ix = S.op("vector", lambda e: e.tensor_reduce(
            LNC[:, 16 + c0:16 + c0 + 8].unsqueeze(2), Vsc, AX.X, ALU.min),
            deps=[t_v8], mark="V")
        t_kp = S.op("vector", lambda e: e.tensor_tensor(
            Ksc, Vsc, LNC[:, 16 + c0:16 + c0 + 8].unsqueeze(2).broadcast_to([128, 8, 8]),
            ALU.is_le), deps=[t_ix], mark="V")
        return S.op("vector", lambda e: e.tensor_tensor(
            rmall[:, 256 * hb:256 * (hb + 1)].rearrange("p (t e r) -> p t e r", t=8, e=8),
            Lps[:, :, 8:40].rearrange("p t (e r) -> p t e r", e=8),
            Ksc.unsqueeze(3).broadcast_to([128, 8, 8, 4]), ALU.mult),
            deps=[t_kp, scr_tok[0]], mark="V")

    gmm_started = set()

    def emit_gmm(tt):
        bk = tt // 8
        toff = bk * 512 + (tt % 8) * 64
        xdeps = [x2T_tok[(0, tt)], x2T_tok[(4, tt)], t_gwT, t_ATs,
                 pj_free[0], pj_free[1]]
        bank_first = bk not in gmm_started
        gmm_started.add(bk)
        for cc in range(CCn):
            S.op("tensor", lambda e, cc=cc, tt=tt, toff=toff: e.matmul(
                p_a[:, toff: toff + 8], x2T_ap(cc, tt * 128, 128),
                gwT.rearrange("p (c n) -> p c n", c=CCn)[:, cc, :],
                start=(bank_first and cc == 0), stop=False),
                deps=xdeps if cc == 0 else [], mark=None)
        bank_last = (tt % 8) == 7
        for cc in range(CCn):
            tok = S.op("tensor", lambda e, cc=cc, tt=tt, toff=toff: e.matmul(
                p_a[:, toff + 8: toff + 40], x2T_ap(cc, tt * 128, 128),
                ATs.rearrange("p (c n) -> p c n", c=CCn)[:, cc, :],
                start=False, stop=(bank_last and cc == CCn - 1)),
                deps=[], mark="T" if (bank_last and cc == CCn - 1) else None)
        if bank_last:
            gmm_stop[bk] = tok

    for tt in range(NT):
        pd = x2t_ps[tt % 2]
        tr_deps = [ln1_tok[tt], t_idf, x2t_free[tt % 2], t_pd_free, proj_mm_final]
        tr_last = None
        for cc in range(CCn):
            tr_last = S.op("tensor", lambda e, cc=cc, tt=tt, pd=pd: e.matmul(
                pd[:, (cc % 8) * 128:((cc % 8) + 1) * 128],
                res[:, tt * 1024 + cc * 128: tt * 1024 + (cc + 1) * 128], identf,
                start=True, stop=True),
                deps=tr_deps, mark="T")
        t_evu = S.op("scalar", lambda e, tt=tt, pd=pd: e.copy(
            slotA.rearrange("p (c w) -> p c w", c=8)[:, 4:8, tt * 128:(tt + 1) * 128],
            pd.rearrange("p (c w) -> p c w", c=8)[:, 0:4, :]),
            deps=[tr_last], mark="S")
        t_evl = S.op("vector", lambda e, tt=tt, pd=pd: e.tensor_copy(
            slotA.rearrange("p (c w) -> p c w", c=8)[:, 0:4, tt * 128:(tt + 1) * 128],
            pd.rearrange("p (c w) -> p c w", c=8)[:, 4:8, :]),
            deps=[tr_last], mark="V")
        for cc in range(CCn):
            x2T_tok[(cc, tt)] = t_evl if cc < 4 else t_evu
        x2t_free[tt % 2] = S.op("vector", lambda e: e.engine_nop(),
                                deps=[t_evu, t_evl], mark="V")
        if tt < 8:
            t2 = tt + 8
            ln1_tok[t2] = S.op("vector", lambda e, t2=t2: e.tensor_scalar(
                res[:, t2 * 1024:(t2 + 1) * 1024],
                res[:, t2 * 1024:(t2 + 1) * 1024],
                LNC[:, 32 + t2:33 + t2], LNC[:, 48 + t2:49 + t2],
                ALU.mult, ALU.subtract),
                deps=[mrb_toks[1]], mark="V")
        if tt > 0:
            emit_gmm(tt - 1)
        if tt == 9:
            mask_half_tok[0] = emit_mask_half(0)
    emit_gmm(NT - 1)

    t_dbg_res = None
    if DEBUG:
        t_dbg_res = S.op("sync", lambda e: e.dma_start(d_dbg_res, res),
                         deps=list(ln1_tok), mark="qD6", inc=16)
    t_BT = S.op("gpsimd", lambda e: e.dma_start(BTs, d_BT),
                deps=[proj_mm_last, yT_tok[(7, NT - 1)], t_dbg_yT], mark="qBT", inc=16)

    x2b_tok = {}
    for tt in range(2):
        x2b_tok[tt] = S.op("vector", lambda e, tt=tt: e.tensor_add(
            res[:, tt * 1024:(tt + 1) * 1024], res[:, tt * 1024:(tt + 1) * 1024], wob),
            deps=[ln1_tok[tt], scr_tok[0], t_wob,
                  x2T_tok[(7, tt)], x2T_tok[(0, tt)], t_dbg_res], mark="V")

    t_me_a = mask_half_tok[0] if mask_half_tok[0] is not None else emit_mask_half(0)
    t_me_b = emit_mask_half(1)
    t_me = t_me_b

    rmT_tok = {}
    rm_state = {"re": None}
    ffn_bank_free = {0: t_me, 1: pj_free[1], 2: None}

    def emit_rm_round(r):
        tr_t = None
        for k in range(8):
            tt = r * 8 + k
            tr_t = S.op("tensor", lambda e, tt=tt, k=k: e.matmul(
                p_d[0:32, k * 128:(k + 1) * 128],
                rmall[:, tt * 32:(tt + 1) * 32], ident,
                start=True, stop=True),
                deps=([t_me_a if r == 0 else t_me_b, t_ident, x2t_free[0],
                       rm_state["re"]] if k == 0 else []),
                mark="T" if k == 7 else None)
        re_tok = S.op("scalar", lambda e, r=r: e.copy(
            rmT[:, r * 1024:(r + 1) * 1024], p_d[0:32, 0:1024]),
            deps=[tr_t], mark="S")
        for k in range(8):
            rmT_tok[r * 8 + k] = re_tok
        rm_state["re"] = re_tok
        ffn_bank_free[2] = re_tok

    emit_rm_round(0)
    emit_rm_round(1)

    for tt in range(2, NT):
        t0 = S.op("vector", lambda e, tt=tt: e.tensor_add(
            res[:, tt * 1024:(tt + 1) * 1024], res[:, tt * 1024:(tt + 1) * 1024], wob),
            deps=[ln1_tok[tt], scr_tok[0], t_wob,
                  x2T_tok[(7, tt)], x2T_tok[(0, tt)], t_dbg_res], mark="V")
        x2b_tok[tt] = t0

    wiT_r = d_wiT.rearrange("(c p) f -> p c f", p=128)
    woT_r = d_woT.rearrange("(f p) c -> p f c", p=128)
    wdb = [wdb0, wdb1]
    wdb_tok = [None, None]
    wdb_free = [[pair_attn_done[7], t_dbg_va],
                [pair_attn_done[7], q_chain[0]]]

    def load_group(g):
        gb_ = g % 2
        t_wi = S.op("gpsimd", lambda e, g=g, gb_=gb_: e.dma_start(
            wdb[gb_][:, 0:8192].rearrange("p (c f) -> p c f", c=CCn),
            wiT_r[:, :, g * 1024:(g + 1) * 1024]),
            deps=wdb_free[gb_], mark="qF%d" % gb_, inc=16)
        t_wo = S.op("gpsimd", lambda e, g=g, gb_=gb_: e.dma_start(
            wdb[gb_][:, 8192:16384].rearrange("p (f c) -> p f c", f=CCn),
            woT_r[:, g * CCn:(g + 1) * CCn, :]),
            deps=[], mark="qF%d" % gb_, inc=16)
        wdb_tok[gb_] = [t_wo]

    load_group(0)
    load_group(1)

    it_ps_free = [x2t_free[1], x2t_free[1]]
    it_sb_tok = [t_dbg_yT, t_dbg_yT, t_dbg_yT]
    fbanks = [p_a, p_b, p_d]
    add_tok = {tt: None for tt in range(NT)}
    ln2_done = []
    out_dma_toks = []

    pend_ln2 = []

    pend_ln2b = []

    def flush_ln2_stats():
        if not pend_ln2:
            return
        tt, par, atok = pend_ln2.pop(0)
        B0 = LSUM if par == 0 else 43
        GMx = GM if par == 0 else GM2
        rt = res[:, tt * 1024:(tt + 1) * 1024]
        t_hs = S.op("vector", lambda e, B0=B0, GMx=GMx: e.tensor_add(
            col(B0), col(B0), col(GMx)), deps=[atok, ln_col_tok[par]], mark="V")
        t_sq = S.op("vector", lambda e, rt=rt, B0=B0: e.scalar_tensor_tensor(
            scr1024, rt, 1.0, rt, ALU.mult, ALU.mult, accum_out=col(B0 + 1)),
            deps=[atok, scr_tok[0]], mark="V")
        scr_tok[0] = t_sq
        t_m = S.op("vector", lambda e, B0=B0: e.tensor_scalar_mul(
            col(B0), col(B0), 1.0 / C), deps=[t_hs], mark="V")
        t_v1 = S.op("vector", lambda e, B0=B0: e.tensor_scalar_mul(
            col(B0 + 1), col(B0 + 1), 1.0 / C), deps=[t_sq], mark="V")
        t_m2 = S.op("vector", lambda e, B0=B0: e.tensor_mul(
            col(B0 + 2), col(B0), col(B0)), deps=[t_m], mark="V")
        t_var = S.op("vector", lambda e, B0=B0: e.tensor_sub(
            col(B0 + 2), col(B0 + 1), col(B0 + 2)), deps=[t_v1, t_m2], mark="V")
        t_ln = S.op("scalar", lambda e, B0=B0: e.activation(
            col(B0 + 3), col(B0 + 2), AF.Ln, bias=col(EPSC)),
            deps=[t_var], mark="S")
        t_rs = S.op("scalar", lambda e, B0=B0: e.activation(
            col(B0 + 3), col(B0 + 3), AF.Exp, scale=col(NHLF)),
            deps=[t_ln], mark="S")
        t_mr = S.op("vector", lambda e, B0=B0: e.tensor_mul(
            col(B0 + 4), col(B0), col(B0 + 3)), deps=[t_rs, t_m], mark="V")
        pend_ln2b.append((rt, B0, par, t_mr, t_sq, tt))

    def flush_ln2_store():
        while pend_ln2b:
            rt, B0, par, t_mr, t_sq, tt = pend_ln2b.pop(0)
            t_nrm = S.op("vector", lambda e, rt=rt, B0=B0: e.tensor_scalar(
                rt, rt, col(B0 + 3), col(B0 + 4), ALU.mult, ALU.subtract),
                deps=[t_mr, t_sq], mark="V")
            ln_col_tok[par] = t_nrm
            t_st = S.op("sync", lambda e, rt=rt, tt=tt: e.dma_start(
                d_out.rearrange("(t p) c -> p t c", p=128)[:, tt, :], rt),
                deps=[t_nrm], mark="qO", inc=16)
            out_dma_toks.append(t_st)

    for g in range(4):
        gb_ = g % 2
        gdeps = wdb_tok[gb_]
        for tb in range(8):
            wo_last = {}
            pend_wo = []

            def flush_wo(min_keep, tb=tb, gb_=gb_, wo_last=wo_last):
                while len(pend_wo) > min_keep:
                    fc, ib3, t_rl = pend_wo.pop(0)
                    slots = [(2 * tb + tt2) % 3 for tt2 in range(2)]
                    wdeps = [t_rl] + ([ffn_bank_free[slots[0]], ffn_bank_free[slots[1]]]
                                      if fc == 0 else [])
                    for tt2 in range(2):
                        for ch in range(2):
                            wo_last[(tt2, ch)] = S.op(
                                "tensor",
                                lambda e, tt2=tt2, ch=ch, fc=fc, ib3=ib3, gb_=gb_, sl=slots[tt2]: e.matmul(
                                    fbanks[sl][:, ch * 512:(ch + 1) * 512],
                                    it_sb[:, ib3 * 256 + tt2 * 128: ib3 * 256 + (tt2 + 1) * 128],
                                    wdb[gb_][:, 8192 + fc * 1024 + ch * 512:
                                             8192 + fc * 1024 + (ch + 1) * 512],
                                    start=(fc == 0), stop=(fc == CCn - 1)),
                                deps=wdeps, mark="T" if fc == CCn - 1 else None)
                    it_sb_tok[ib3] = wo_last[(1, 1)]

            for fc in range(CCn):
                gf = g * 8 + fc
                ib = fc % 2
                ib3 = fc % 3
                ips = p_c[:, ib * 512: ib * 512 + 256]
                first_deps = gdeps + [it_ps_free[ib],
                                      x2b_tok[2 * tb], x2b_tok[2 * tb + 1]]
                mmtok = None
                for cc in range(CCn):
                    mmtok = S.op("tensor", lambda e, cc=cc, fc=fc, tb=tb, gb_=gb_, ips=ips: e.matmul(
                        ips, wdb[gb_][:, cc * 1024 + fc * 128: cc * 1024 + (fc + 1) * 128],
                        x2T_ap(cc, tb * 256, 256),
                        start=(cc == 0), stop=False),
                        deps=first_deps if cc == 0 else [], mark=None)
                mmtok = S.op("tensor", lambda e, gf=gf, tb=tb, ips=ips: e.matmul(
                    ips, BTs[:, gf * 128:(gf + 1) * 128],
                    rmT[:, tb * 256:(tb + 1) * 256],
                    start=False, stop=True),
                    deps=[t_BT, rmT_tok[2 * tb], rmT_tok[2 * tb + 1]], mark="T")
                flush_wo(1)
                if fc in (3, 5):
                    flush_ln2_stats()
                if fc == 7:
                    flush_ln2_store()
                t_rl = S.op("scalar", lambda e, ib3=ib3, gf=gf, ips=ips: e.activation(
                    it_sb[:, ib3 * 256:(ib3 + 1) * 256], ips, AF.Relu,
                    bias=wib[:, gf:gf + 1]),
                    deps=[mmtok, t_wib, it_sb_tok[ib3]], mark="S")
                it_ps_free[ib] = t_rl
                pend_wo.append((fc, ib3, t_rl))
            flush_wo(0)
            for tt2 in range(2):
                tt = 2 * tb + tt2
                sl = (2 * tb + tt2) % 3
                B0 = LSUM if tt2 == 0 else 43
                GMx = GM if tt2 == 0 else GM2
                t_a0 = S.op("vector", lambda e, tt=tt, sl=sl, B0=B0: e.scalar_tensor_tensor(
                    res[:, tt * 1024: tt * 1024 + 512],
                    res[:, tt * 1024: tt * 1024 + 512], 1.0, fbanks[sl][:, 0:512],
                    ALU.mult, ALU.add, accum_out=col(B0)),
                    deps=[wo_last[(tt2, 0)], x2b_tok[tt], add_tok[tt]], mark="V")
                t_a1 = S.op("vector", lambda e, tt=tt, sl=sl, GMx=GMx: e.scalar_tensor_tensor(
                    res[:, tt * 1024 + 512: tt * 1024 + 1024],
                    res[:, tt * 1024 + 512: tt * 1024 + 1024], 1.0, fbanks[sl][:, 512:1024],
                    ALU.mult, ALU.add, accum_out=col(GMx)),
                    deps=[wo_last[(tt2, 1)], add_tok[tt]], mark="V")
                add_tok[tt] = S.op("vector", lambda e: e.engine_nop(),
                                   deps=[t_a0, t_a1], mark="V")
                ffn_bank_free[sl] = add_tok[tt]
            if g == 3:
                pend_ln2.append((2 * tb, 0, add_tok[2 * tb]))
                pend_ln2.append((2 * tb + 1, 1, add_tok[2 * tb + 1]))
        wdb_free[gb_] = [wo_last[(1, 1)]]
        if g + 2 < 4:
            load_group(g + 2)
    while pend_ln2 or pend_ln2b:
        flush_ln2_stats()
        flush_ln2_stats()
        flush_ln2_store()

    S.barrier_wait("sync", out_dma_toks)

    with nc.Block() as block:
        @block.sync
        def _(eng):
            S.replay("sync", eng)

        @block.tensor
        def _(eng):
            S.replay("tensor", eng)

        @block.vector
        def _(eng):
            S.replay("vector", eng)

        @block.scalar
        def _(eng):
            S.replay("scalar", eng)

        @block.gpsimd
        def _(eng):
            S.replay("gpsimd", eng)

    return nc


_NC_CACHE = None


def kernel(L0_attn_x, L0_attn_w_attn, L0_attn_w_proj, L0_ffn_gate_W, L0_ffn_gate_b,
           L0_ffn_wi_W, L0_ffn_wi_b, L0_ffn_wo_W, L0_ffn_wo_b, L0_ffn_lora_A,
           L0_ffn_lora_B):
    global _NC_CACHE
    x = np.asarray(L0_attn_x, np.float32)
    w_attn = np.asarray(L0_attn_w_attn, np.float32)
    w_proj = np.asarray(L0_attn_w_proj, np.float32)
    gate_W = np.asarray(L0_ffn_gate_W, np.float32)
    gate_b = np.asarray(L0_ffn_gate_b, np.float32)
    wi_W = np.asarray(L0_ffn_wi_W, np.float32)
    wi_b = np.asarray(L0_ffn_wi_b, np.float32)
    wo_W = np.asarray(L0_ffn_wo_W, np.float32)
    wo_b = np.asarray(L0_ffn_wo_b, np.float32)
    lora_A = np.asarray(L0_ffn_lora_A, np.float32)
    lora_B = np.asarray(L0_ffn_lora_B, np.float32)

    watT = np.ascontiguousarray(w_attn.T)
    wprT = np.ascontiguousarray(w_proj.T)
    gwT = np.ascontiguousarray(gate_W.T)
    gb = np.ascontiguousarray(np.tile(gate_b[None, :], (128, 1)))
    wiT = np.ascontiguousarray(wi_W.T)
    wib = np.ascontiguousarray(wi_b.reshape(32, 128).T)
    woT = np.ascontiguousarray(wo_W.T)
    wob = np.ascontiguousarray(np.tile(wo_b[None, :], (128, 1)))
    AT = np.ascontiguousarray(lora_A.reshape(32, C).T)
    BT = np.ascontiguousarray(lora_B.transpose(0, 2, 1).reshape(32, FF))
    ident = np.eye(128, dtype=ml_dtypes.bfloat16)
    identf = np.eye(128, dtype=np.float32)
    rr, cc_ = np.indices((128, 128))
    tril = np.where(rr > cc_, NEG, 0.0).astype(ml_dtypes.bfloat16)
    iotapb = np.ascontiguousarray(
        np.tile(np.arange(NE, dtype=np.float32)[None, :] + BIGI, (128, 1)))

    if _NC_CACHE is None:
        _NC_CACHE = build_program()
    nc = _NC_CACHE

    shared = {"watT": watT, "wprT": wprT, "gwT": gwT, "gb": gb, "wiT": wiT,
              "wib": wib, "woT": woT, "wob": wob, "AT": AT, "BT": BT,
              "ident": ident, "identf": identf, "tril": tril, "iotapb": iotapb}
    in_maps = []
    for b in range(8):
        m = dict(shared)
        m["x_nat"] = np.ascontiguousarray(x[b])
        in_maps.append(m)

    resobj = run_bass_kernel_spmd(nc, in_maps, core_ids=list(range(8)))
    return np.stack([r["out"] for r in resobj.results], axis=0)


if __name__ == "__main__":
    import sys
    sys.path.insert(0, "/root/problem")
    import reference as R
    d = R.setup_inputs(0)
    inp = {k: np.asarray(v) for k, v in d.items()}
    got = kernel(**inp)
    ref = np.asarray(R.reference(**d))
    num = float(np.mean((got.astype(np.float64) - ref.astype(np.float64)) ** 2))
    den = float(np.mean(ref.astype(np.float64) ** 2))
    print("ratio:", num / den)
